# version 04
# baseline (speedup 1.0000x reference)
import math

import numpy as np
import ml_dtypes

import concourse.bass as bass
import concourse.mybir as mybir
from concourse.bass_utils import run_bass_kernel_spmd

F32 = mybir.dt.float32
BF16 = mybir.dt.bfloat16
AF = mybir.ActivationFunctionType
ALU = mybir.AluOpType

B, T, M, D, G, NH = 4, 2048, 2048, 128, 4, 16
HF = 8192
NMC = 16
NC_H = 64
NSLAB = 4
KB = 1024

QKSCALE = float(128.0 ** -0.5)
S8192 = float(1.0 / math.sqrt(8192.0))
EXP_SHIFT = -20.0
EPS = 1e-5

ENGS = ("tensor", "scalar", "vector", "sync", "gpsimd")

_TABLE_IDS = None


def _table_ids(nc):
    global _TABLE_IDS
    if _TABLE_IDS is None:
        from concourse.hw_specs import get_activation_tables
        names = list(get_activation_tables(nc.m.arch).keys())
        _TABLE_IDS = {n: i for i, n in enumerate(names)}
    return _TABLE_IDS


class Plan:
    def __init__(self, nc):
        self.nc = nc
        self.items = {e: [] for e in ENGS}
        self.sem_handles = {}
        self.counts = {}
        self.watermark = {}

    def add_sem(self, name, handle):
        self.sem_handles[name] = handle
        self.counts[name] = 0

    def wait(self, eng, sem, val=None):
        if val is None:
            val = self.counts[sem]
        assert val <= self.counts[sem], (eng, sem, val, self.counts[sem])
        if val <= 0:
            return
        key = (eng, sem)
        if self.watermark.get(key, -1) >= val:
            return
        self.watermark[key] = val
        h = self.sem_handles[sem]
        self.items[eng].append(lambda e, h=h, v=val: e.wait_ge(h, v))

    def op(self, eng, fn, inc=None, inc_n=1):
        if inc is not None:
            self.counts[inc] += inc_n
            h = self.sem_handles[inc]

            def f(e, fn=fn, h=h, n=inc_n):
                fn(e).then_inc(h, n)
        else:
            def f(e, fn=fn):
                fn(e)
        self.items[eng].append(f)
        return self.counts[inc] if inc is not None else None

    def dma(self, eng, out, in_, sem):
        return self.op(eng, lambda e, o=out, i=in_: e.dma_start(out=o, in_=i),
                       inc=sem, inc_n=16)

    def run(self, block):
        block.tensor(lambda e: [f(e) for f in self.items["tensor"]])
        block.scalar(lambda e: [f(e) for f in self.items["scalar"]])
        block.vector(lambda e: [f(e) for f in self.items["vector"]])
        block.sync(lambda e: [f(e) for f in self.items["sync"]])
        block.gpsimd(lambda e: [f(e) for f in self.items["gpsimd"]])


def build_nc(dumps=()):
    nc = bass.Bass("TRN2", target_bir_lowering=False)

    xs = nc.dram_tensor("xs", [NSLAB, 128, NMC, 512], F32, kind="ExternalInput").ap()
    waq = nc.dram_tensor("waq", [NH, 128, 2048], F32, kind="ExternalInput").ap()
    wak = nc.dram_tensor("wak", [128, 8192], F32, kind="ExternalInput").ap()
    wav = nc.dram_tensor("wav", [128, 8192], F32, kind="ExternalInput").ap()
    wao = nc.dram_tensor("wao", [4, 128, 8192], F32, kind="ExternalInput").ap()
    w13s = nc.dram_tensor("w13s", [NC_H, 128, 4096], F32, kind="ExternalInput").ap()
    w2t = nc.dram_tensor("w2t", [32, 128, 4096], F32, kind="ExternalInput").ap()
    cosk = nc.dram_tensor("cosk", [128, 2048], F32, kind="ExternalInput").ap()
    sink = nc.dram_tensor("sink", [128, 2048], F32, kind="ExternalInput").ap()
    cosq = nc.dram_tensor("cosq", [128, 1024], F32, kind="ExternalInput").ap()
    sinq = nc.dram_tensor("sinq", [128, 1024], F32, kind="ExternalInput").ap()
    tri_i = nc.dram_tensor("tri", [128, 128], BF16, kind="ExternalInput").ap()
    sig1_i = nc.dram_tensor("sig1", [128, 1], F32, kind="ExternalInput").ap()
    h128_i = nc.dram_tensor("h128", [128, 128], BF16, kind="ExternalInput").ap()
    h64d_i = nc.dram_tensor("h64d", [128, 128], BF16, kind="ExternalInput").ap()

    out_t = nc.dram_tensor("out", [NMC, 128, 1024], F32, kind="ExternalOutput").ap()
    dump_aps = {}
    for nm in dumps:
        shapes = {"xn": [128, 32768], "kt": [128, 8192], "vsb": [128, 8192],
                  "ot": [128, 16384], "hid": [128, 65536], "pt": [128, 8192],
                  "qt0": [128, 1024], "rt": [NMC, 128, 1024], "xnq": [128, 16384]}
        dt = BF16
        dump_aps[nm] = nc.dram_tensor("dump_" + nm, shapes[nm], dt, kind="ExternalOutput").ap()

    scrA = nc.dram_tensor("scrA", [64, 262144], BF16, kind="Internal").ap()
    scrB = nc.dram_tensor("scrB", [4, 64, 128, 512], BF16, kind="Internal").ap()
    scrR = nc.dram_tensor("scrR", [NMC, 128, 1024], BF16, kind="Internal").ap()
    scrA_v = scrA.rearrange("c (p m) -> c p m", p=128)

    A = {}

    SB_BASE = 16512

    def sb(name, shape, dtype, off_kb):
        A[name] = nc.alloc_sbuf_tensor_at(name, shape, dtype,
                                          offset=SB_BASE + int(off_kb * KB))
        return A[name]

    sb("ones_sq", [128, 128], BF16, 0.0)
    sb("bias20", [128, 1], F32, 0.25)
    sb("epsb", [128, 1], F32, 0.28125)
    sb("sig1", [128, 1], F32, 0.3125)
    sb("tri", [128, 128], BF16, 0.5)
    sb("h128", [128, 128], BF16, 0.75)
    sb("h64d", [128, 128], BF16, 1.0)
    sb("cosq", [128, 1024], F32, 1.25)
    sb("sinq", [128, 1024], F32, 5.25)
    sb("xnq", [128, NMC * 1024], BF16, 9.25)
    XN = 41.5
    sb("xn", [128, NMC * 2048], BF16, XN)
    sb("hidden", [128, NC_H * 1024], BF16, XN)
    sb("qraw", [128, 1024], F32, XN)
    sb("qshift", [128, 1024], F32, XN + 4)
    sb("qtmp", [128, 1024], F32, XN + 8)
    sb("waq0", [128, 2048], BF16, XN + 12)
    sb("waq1", [128, 2048], BF16, XN + 16)
    sb("reca", [128, 512], F32, XN + 20)
    sb("recb", [128, 512], F32, XN + 22)
    sb("wao0", [128, 8192], BF16, XN)
    sb("wao1", [128, 8192], BF16, XN + 16)
    sb("rts0", [128, 1024], BF16, XN + 32)
    sb("rts1", [128, 1024], BF16, XN + 34)
    sb("xslab", [128, NMC * 512], F32, 105.5)
    sb("xb", [128, NMC * 512], BF16, 137.5)
    sb("xsq", [128, NMC * 512], BF16, 153.5)
    sb("ln_mean", [128, 512], F32, 169.5)
    sb("ln_t1", [128, 512], F32, 171.5)
    sb("ln_t2", [128, 512], F32, 173.5)
    sb("kraw", [128, 1024], F32, 105.5)
    sb("kshift", [128, 1024], F32, 109.5)
    sb("ktmp", [128, 1024], F32, 113.5)
    sb("cosk", [128, 2048], F32, 117.5)
    sb("sink", [128, 2048], F32, 125.5)
    sb("wakb", [128, 8192], BF16, 133.5)
    sb("wavb", [128, 8192], BF16, 149.5)
    sb("pt", [128, 16 * 512], BF16, 105.5)
    sb("ot", [128, NH * 1024], BF16, 121.5)
    sb("qt0", [128, 1024], BF16, 153.5)
    sb("qt1", [128, 1024], BF16, 155.5)
    sb("kt", [128, G * 2048], BF16, 157.5)
    sb("vsb", [128, 16 * 512], BF16, 173.5)
    sb("w2c0", [128, 4096], BF16, 189.5)
    sb("w2c1", [128, 4096], BF16, 197.5)
    sb("aout0", [128, 1024], BF16, 205.5)
    sb("bin0", [128, 2048], BF16, 189.5)
    sb("bin1", [128, 2048], BF16, 193.5)
    sb("bout0", [128, 2048], BF16, 197.5)
    sb("bout1", [128, 2048], BF16, 201.5)
    sb("w13b0", [128, 4096], BF16, 169.5)
    sb("w13b1", [128, 4096], BF16, 177.5)
    sb("gsil", [128, 1024], F32, 185.5)
    sb("w2q0", [128, 512], BF16, 169.5)
    sb("w2q1", [128, 512], BF16, 170.5)
    sb("w2q2", [128, 512], BF16, 171.5)
    sb("w2q3", [128, 512], BF16, 172.5)
    sb("gf", [128, 1024], F32, 173.5)
    sb("xc0", [128, 1024], F32, 177.5)
    sb("xc1", [128, 1024], F32, 181.5)
    sb("rt0", [128, 1024], F32, 185.5)
    sb("rt1", [128, 1024], F32, 189.5)
    sb("tf", [128, 1024], F32, 193.5)
    sb("ou0", [128, 1024], F32, 197.5)
    sb("ou1", [128, 1024], F32, 201.5)

    P01 = nc.alloc_psum_tensor("P01", [128, 1024], F32)
    P23 = nc.alloc_psum_tensor("P23", [128, 1024], F32)
    P45 = nc.alloc_psum_tensor("P45", [128, 1024], F32)
    P67 = nc.alloc_psum_tensor("P67", [128, 1024], F32)
    PS = [P01, P23, P45, P67]

    with (
        nc.semaphore("s_pe") as s_pe,
        nc.semaphore("s_ac") as s_ac,
        nc.semaphore("s_dv") as s_dv,
        nc.semaphore("s_sy") as s_sy,
        nc.semaphore("s_gq") as s_gq,
        nc.Block() as block,
    ):
        P = Plan(nc)
        for nm, h in (("pe", s_pe), ("ac", s_ac), ("dv", s_dv), ("sy", s_sy), ("gq", s_gq)):
            P.add_sem(nm, h)

        tids = _table_ids(nc)

        def load_act_table(tid):
            P.op("scalar", lambda e, tid=tid: e.add_instruction(mybir.InstLoadActFuncSet(
                name=nc.get_next_instruction_name(), act_func_set_id=tid, ins=[], outs=[])))

        P.op("vector", lambda e: e.memset(A["ones_sq"][:], 1.0))
        P.op("vector", lambda e: e.memset(A["bias20"][:], EXP_SHIFT))
        consts_dv = P.op("vector", lambda e: e.memset(A["epsb"][:], EPS), inc="dv")
        P.dma("sync", A["tri"][:], tri_i, "sy")
        P.dma("sync", A["sig1"][:], sig1_i, "sy")
        P.dma("sync", A["h128"][:], h128_i, "sy")
        P.dma("sync", A["h64d"][:], h64d_i, "sy")
        P.dma("sync", A["cosq"][:], cosq, "sy")
        P.dma("sync", A["sinq"][:], sinq, "sy")
        consts_sy = P.counts["sy"]
        load_act_table(tids["natural_log_exp_and_others"])

        a_st = {"unit": 0, "ld": {}, "mmcg": {}, "stores": {}, "st": 0}
        p67_war = {"eng": None, "val": 0}

        def a_load(cg):
            if cg > 31 or cg in a_st["ld"]:
                return
            if cg >= 2:
                P.wait("gpsimd", "pe", a_st["mmcg"].get(cg - 2, 0))
            a_st["ld"][cg] = P.dma("gpsimd", A["w2c%d" % (cg % 2)][:], w2t[cg], "gq")

        def a_unit():
            unit = a_st["unit"]
            if unit >= 128:
                return
            c, u = unit >> 1, unit & 1
            cg, cl = c >> 1, c & 1
            a_load(cg)
            a_load(cg + 1)
            wb = A["w2c%d" % (cg % 2)]
            ob = A["aout0"]
            P.wait("tensor", "sy", consts_sy)
            P.wait("tensor", "gq", a_st["ld"][cg])
            if p67_war["eng"] is not None:
                P.wait("tensor", p67_war["eng"], p67_war["val"])
            for j in range(2):
                mm = P.op("tensor", lambda e, j=j, u=u, cl=cl, wb=wb: e.matmul(
                    P67[:, j * 512:(j + 1) * 512], A["h128"][:],
                    wb[:, cl * 2048 + u * 1024 + j * 512: cl * 2048 + u * 1024 + (j + 1) * 512],
                    start=True, stop=True), inc="pe")
            a_st["mmcg"][cg] = mm
            buf_war = a_st["stores"].get(unit - 1, 0)
            if unit % 2 == 0:
                P.wait("scalar", "pe", mm)
                P.wait("scalar", "sy", buf_war)
                ev = P.op("scalar", lambda e, ob=ob: e.activation(
                    ob[:], P67[:], AF.Copy, bias=0.0, scale=S8192), inc="ac")
                p67_war.update(eng="ac", val=ev)
                ev_eng = "ac"
            else:
                P.wait("vector", "pe", mm)
                P.wait("vector", "sy", buf_war)
                ev = P.op("vector", lambda e, ob=ob: e.tensor_scalar_mul(
                    ob[:], P67[:], S8192), inc="dv")
                p67_war.update(eng="dv", val=ev)
                ev_eng = "dv"
            P.wait("sync", ev_eng, ev)
            a_st["st"] = P.dma("sync", scrA_v[c][:, u * 1024:(u + 1) * 1024], ob[:], "sy")
            a_st["stores"][unit] = a_st["st"]
            a_st["unit"] = unit + 1

        b_st = {"pp": 0, "mm": [0] * 64, "first": False, "st": 0, "stores": {}}

        def b_unit():
            pp = b_st["pp"]
            if pp >= 64:
                return
            bi = A["bin%d" % (pp % 2)]
            bo = A["bout%d" % (pp % 2)]
            if not b_st["first"]:
                assert a_st["unit"] >= 128, "A must finish before B"
                P.wait("sync", "sy", a_st["st"])
                b_st["first"] = True
            if pp >= 2:
                P.wait("sync", "pe", b_st["mm"][pp - 2])
            P.dma("sync", bi[0:64, :], scrA[:, (2 * pp) * 2048:(2 * pp + 1) * 2048], "sy")
            ld = P.dma("sync", bi[64:128, :], scrA[:, (2 * pp + 1) * 2048:(2 * pp + 2) * 2048], "sy")
            evs = []
            buf_war = b_st["stores"].get(pp - 2, 0) if pp >= 2 else a_st["st"]
            for u in range(2):
                P.wait("tensor", "sy", ld)
                if p67_war["eng"] is not None:
                    P.wait("tensor", p67_war["eng"], p67_war["val"])
                for j in range(2):
                    mm = P.op("tensor", lambda e, u=u, j=j, bi=bi: e.matmul(
                        P67[:, j * 512:(j + 1) * 512], A["h64d"][:],
                        bi[:, u * 1024 + j * 512: u * 1024 + (j + 1) * 512],
                        start=True, stop=True), inc="pe")
                b_st["mm"][pp] = mm
                if (pp + u) % 2 == 0:
                    P.wait("scalar", "pe", mm)
                    P.wait("scalar", "sy", buf_war)
                    ev = P.op("scalar", lambda e, bo=bo, u=u: e.activation(
                        bo[:, u * 1024:(u + 1) * 1024], P67[:], AF.Copy,
                        bias=0.0, scale=1.0), inc="ac")
                    p67_war.update(eng="ac", val=ev)
                    evs.append(("ac", ev))
                else:
                    P.wait("vector", "pe", mm)
                    P.wait("vector", "sy", buf_war)
                    ev = P.op("vector", lambda e, bo=bo, u=u: e.tensor_copy(
                        bo[:, u * 1024:(u + 1) * 1024], P67[:]), inc="dv")
                    p67_war.update(eng="dv", val=ev)
                    evs.append(("dv", ev))
            for eng, ev in evs:
                P.wait("sync", eng, ev)
            for j in range(2):
                dst = scrB[:, :, 2 * pp + j, :].rearrange("q c m -> c q m")
                b_st["st"] = P.dma("sync", dst, bo[j * 64:(j + 1) * 64, :], "sy")
            b_st["stores"][pp] = b_st["st"]
            b_st["pp"] = pp + 1

        xslab_v = A["xslab"][:].rearrange("p (mc t) -> p mc t", mc=NMC)
        xld = [0] * NSLAB
        apply_done = [0] * NSLAB
        stats_done = [0] * NSLAB
        rows_done = [0] * NSLAB

        xbld = [0] * NSLAB
        for s in range(NSLAB):
            if s > 0:
                P.wait("sync", "dv", apply_done[s - 1])
                P.wait("gpsimd", "pe", stats_done[s - 1])
            P.dma("sync", xslab_v[:, 0:8, :], xs[s][:, 0:8, :], "sy")
            xld[s] = P.dma("sync", xslab_v[:, 8:16, :], xs[s][:, 8:16, :], "sy")
            xbld[s] = P.dma("gpsimd", A["xb"][:], xs[s], "gq")
            P.wait("scalar", "sy", xld[s] - 16)
            if s > 0:
                P.wait("scalar", "pe", stats_done[s - 1])
            P.op("scalar", lambda e: e.activation(
                A["xsq"][:, 0:4096], A["xslab"][:, 0:4096], AF.Square, bias=0.0, scale=1.0), inc="ac")
            P.wait("scalar", "sy", xld[s])
            sq_ac = P.op("scalar", lambda e: e.activation(
                A["xsq"][:, 4096:8192], A["xslab"][:, 4096:8192], AF.Square, bias=0.0, scale=1.0), inc="ac")
            P.wait("tensor", "gq", xbld[s])
            P.wait("tensor", "ac", sq_ac)
            P.wait("tensor", "dv", consts_dv)
            if s > 0:
                P.wait("tensor", "dv", rows_done[s - 1])
            for mc in range(NMC):
                P.op("tensor", lambda e, mc=mc: e.matmul(
                    P01[:, 0:512], A["ones_sq"][:], A["xb"][:, mc * 512:(mc + 1) * 512],
                    start=(mc == 0), stop=(mc == NMC - 1)), inc="pe")
            for mc in range(NMC):
                stats_done[s] = P.op("tensor", lambda e, mc=mc: e.matmul(
                    P01[:, 512:1024], A["ones_sq"][:], A["xsq"][:, mc * 512:(mc + 1) * 512],
                    start=(mc == 0), stop=(mc == NMC - 1)), inc="pe")
            P.wait("vector", "pe", stats_done[s])
            P.op("vector", lambda e: e.tensor_scalar_mul(A["ln_mean"][:], P01[:, 0:512], 1.0 / 2048.0))
            P.op("vector", lambda e: e.tensor_scalar_mul(A["ln_t1"][:], P01[:, 512:1024], 1.0 / 2048.0))
            P.op("vector", lambda e: e.tensor_mul(A["ln_t2"][:], A["ln_mean"][:], A["ln_mean"][:]))
            rows_done[s] = P.op("vector", lambda e: e.tensor_sub(
                A["ln_t1"][:], A["ln_t1"][:], A["ln_t2"][:]), inc="dv")
            P.wait("scalar", "dv", rows_done[s])
            P.op("scalar", lambda e: e.activation(
                A["ln_t2"][:], A["ln_t1"][:], AF.Ln, bias=A["epsb"][:], scale=1.0))
            ac_rstd = P.op("scalar", lambda e: e.activation(
                A["ln_t1"][:], A["ln_t2"][:], AF.Exp, scale=-0.5), inc="ac")
            P.wait("vector", "ac", ac_rstd)
            P.op("vector", lambda e: e.tensor_mul(A["ln_t2"][:], A["ln_mean"][:], A["ln_t1"][:]))
            for mc in range(NMC):
                P.op("vector", lambda e, mc=mc: e.tensor_mul(
                    A["xslab"][:, mc * 512:(mc + 1) * 512],
                    A["xslab"][:, mc * 512:(mc + 1) * 512], A["ln_t1"][:]))
                apply_done[s] = P.op("vector", lambda e, mc=mc, s=s: e.tensor_sub(
                    A["xn"][:, mc * 2048 + s * 512: mc * 2048 + (s + 1) * 512],
                    A["xslab"][:, mc * 512:(mc + 1) * 512], A["ln_t2"][:]), inc="dv")
            for _ in range(8):
                a_unit()
        xn_done = apply_done[NSLAB - 1]

        xnq_done = 0
        for mc in range(NMC):
            src = A["xn"][:, mc * 2048:(mc + 1) * 2048].rearrange(
                "p (s t) -> p s t", s=NSLAB)[:, :, 0:256]
            xnq_done = P.op("vector", lambda e, mc=mc, src=src: e.tensor_copy(
                A["xnq"][:, mc * 1024:(mc + 1) * 1024], src), inc="dv")
        if "xn" in dump_aps:
            P.wait("sync", "dv", xn_done)
            P.dma("sync", dump_aps["xn"], A["xn"][:], "sy")
        if "xnq" in dump_aps:
            P.wait("sync", "dv", xnq_done)
            P.dma("sync", dump_aps["xnq"], A["xnq"][:], "sy")

        P.wait("gpsimd", "pe", stats_done[NSLAB - 1])
        P.wait("gpsimd", "dv", xn_done)
        ld_wak = P.dma("gpsimd", A["wakb"][:], wak, "gq")
        ld_wav = P.dma("gpsimd", A["wavb"][:], wav, "gq")
        P.wait("sync", "dv", xn_done)
        P.dma("sync", A["cosk"][:], cosk, "sy")
        ld_sk = P.dma("sync", A["sink"][:], sink, "sy")

        v_ev = [0] * 16
        P.wait("tensor", "gq", ld_wav)
        P.wait("tensor", "dv", xn_done)
        P.wait("tensor", "dv", rows_done[NSLAB - 1])
        vmm = 0
        for tc in range(16):
            half = tc % 2
            if tc >= 2:
                P.wait("tensor", "dv", v_ev[tc - 2])
            for mc in range(NMC):
                vmm = P.op("tensor", lambda e, tc=tc, mc=mc, half=half: e.matmul(
                    P01[:, half * 512:(half + 1) * 512],
                    A["xn"][:, mc * 2048 + tc * 128: mc * 2048 + (tc + 1) * 128],
                    A["wavb"][:, mc * 512:(mc + 1) * 512],
                    start=(mc == 0), stop=(mc == NMC - 1)), inc="pe")
            P.wait("vector", "pe", vmm)
            v_ev[tc] = P.op("vector", lambda e, tc=tc, half=half: e.tensor_copy(
                A["vsb"][:, tc * 512:(tc + 1) * 512],
                P01[:, half * 512:(half + 1) * 512]), inc="dv")
        v_done_dv = v_ev[15]

        P.wait("tensor", "gq", ld_wak)
        kv_done_dv = 0
        kt_war = [0, 0]
        for g in range(G):
            for th in range(2):
                ev = 0
                for tq in range(2):
                    q = th * 2 + tq
                    if kt_war[tq]:
                        P.wait("tensor", "dv", kt_war[tq])
                    for mc in range(NMC):
                        kmm = P.op("tensor", lambda e, g=g, q=q, mc=mc, tq=tq: e.matmul(
                            P23[:, tq * 512:(tq + 1) * 512],
                            A["wakb"][:, (g * 16 + mc) * 128:(g * 16 + mc + 1) * 128],
                            A["xn"][:, mc * 2048 + q * 512: mc * 2048 + (q + 1) * 512],
                            start=(mc == 0), stop=(mc == NMC - 1)), inc="pe")
                    P.wait("vector", "pe", kmm)
                    ev = P.op("vector", lambda e, tq=tq: e.tensor_copy(
                        A["kraw"][:, tq * 512:(tq + 1) * 512],
                        P23[:, tq * 512:(tq + 1) * 512]), inc="dv")
                P.wait("sync", "dv", ev)
                P.dma("sync", A["kshift"][0:64, :], A["kraw"][64:128, :], "sy")
                shf = P.dma("sync", A["kshift"][64:128, :], A["kraw"][0:64, :], "sy")
                P.wait("vector", "sy", shf)
                P.wait("vector", "sy", ld_sk)
                P.op("vector", lambda e, th=th: e.tensor_mul(
                    A["ktmp"][:], A["kraw"][:], A["cosk"][:, th * 1024:(th + 1) * 1024]))
                P.op("vector", lambda e, th=th: e.tensor_mul(
                    A["kshift"][:], A["kshift"][:], A["sink"][:, th * 1024:(th + 1) * 1024]))
                kv_done_dv = P.op("vector", lambda e, g=g, th=th: e.tensor_add(
                    A["kt"][:, g * 2048 + th * 1024: g * 2048 + (th + 1) * 1024],
                    A["ktmp"][:], A["kshift"][:]), inc="dv")
                kt_war[0] = kv_done_dv
                kt_war[1] = kv_done_dv
                a_unit()
                a_unit()
        if "kt" in dump_aps:
            P.wait("sync", "dv", kv_done_dv)
            P.dma("sync", dump_aps["kt"], A["kt"][:], "sy")
        if "vsb" in dump_aps:
            P.wait("sync", "dv", v_done_dv)
            P.dma("sync", dump_aps["vsb"], A["vsb"][:], "sy")

        def slot_geom(jg, kap):
            Pp, sg = kap >> 1, kap & 1
            rp = Pp - 4 * jg
            moff = max(0, rp) * 128
            lch = (Pp >> 1) * 4 + sg * 2 + (Pp & 1)
            return Pp, sg, rp, moff, lch

        waq_ld = [0] * (NH + 1)
        qproj_done = [0] * NH
        qe_done = [0] * NH
        qrope_done = [0] * NH
        slot_ac = {}
        exp_all = 0
        pt_war = 0
        ot_war = [kv_done_dv, kv_done_dv]
        ot_dv = 0
        pv_last = [0, 0]
        scores_prev = [0, 0]

        def emit_qproj_rope(h):
            qb = A["qt%d" % (h % 2)]
            wb = A["waq%d" % (h % 2)]
            P.wait("tensor", "gq", waq_ld[h])
            P.wait("tensor", "dv", xnq_done)
            P.wait("tensor", "ac", slot_ac.get(("P45", 0), 0))
            P.wait("tensor", "ac", slot_ac.get(("P45", 1), 0))
            if h >= 1:
                P.wait("tensor", "dv", qe_done[h - 1])
            for th in range(2):
                for mc in range(NMC):
                    qproj_done[h] = P.op("tensor", lambda e, th=th, mc=mc, wb=wb: e.matmul(
                        P45[:, th * 512:(th + 1) * 512],
                        wb[:, mc * 128:(mc + 1) * 128],
                        A["xnq"][:, mc * 1024 + th * 512: mc * 1024 + (th + 1) * 512],
                        start=(mc == 0), stop=(mc == NMC - 1)), inc="pe")
            P.wait("vector", "pe", qproj_done[h])
            if h >= 2:
                P.wait("vector", "pe", scores_prev[h % 2])
            qe_done[h] = P.op("vector", lambda e: e.tensor_copy(A["qraw"][:], P45[:]), inc="dv")
            P.wait("sync", "dv", qe_done[h])
            P.dma("sync", A["qshift"][0:64, :], A["qraw"][64:128, :], "sy")
            qsh = P.dma("sync", A["qshift"][64:128, :], A["qraw"][0:64, :], "sy")
            P.wait("vector", "sy", qsh)
            P.wait("vector", "sy", consts_sy)
            P.op("vector", lambda e: e.tensor_mul(A["qtmp"][:], A["qraw"][:], A["cosq"][:]))
            P.op("vector", lambda e: e.tensor_mul(A["qshift"][:], A["qshift"][:], A["sinq"][:]))
            qrope_done[h] = P.op("vector", lambda e, qb=qb: e.tensor_add(
                qb[:], A["qtmp"][:], A["qshift"][:]), inc="dv")

        kv_pe_done = P.counts["pe"]
        P.wait("gpsimd", "pe", kv_pe_done)
        P.wait("gpsimd", "dv", kv_done_dv)
        P.wait("gpsimd", "sy", P.counts["sy"])
        waq_ld[0] = P.dma("gpsimd", A["waq0"][:], waq[0], "gq")
        P.wait("tensor", "dv", v_done_dv)
        P.wait("tensor", "dv", rows_done[NSLAB - 1])
        emit_qproj_rope(0)
        waq_ld[1] = P.dma("gpsimd", A["waq1"][:], waq[1], "gq")

        for h in range(NH):
            g = h // 4
            qb = A["qt%d" % (h % 2)]
            for jg in range(2):
                nk = 8 * (jg + 1)
                slots = [(P01, "P01", 0), (P01, "P01", 1),
                         (P45, "P45", 0), (P45, "P45", 1)] if jg == 0 else \
                        [(P01, "P01", 0), (P01, "P01", 1),
                         (P23, "P23", 0), (P23, "P23", 1)]
                P.wait("tensor", "dv", qrope_done[h])
                P.wait("tensor", "dv", kv_done_dv)
                mask_dv = 0
                smm_last = 0
                for kap in range(0, nk, 2):
                    Pp, _, rp, moff, _ = slot_geom(jg, kap)
                    base = 2 * (Pp & 1)
                    ps, psname, _ = slots[base]
                    smm = 0
                    for sg in range(2):
                        _, _, _, _, lch = slot_geom(jg, kap + sg)
                        hf = slots[base + sg][2]
                        P.wait("tensor", "ac", slot_ac.get((psname, hf), 0))
                        if psname == "P23":
                            P.wait("tensor", "dv", ot_war[hf])
                        smm = P.op("tensor", lambda e, g=g, lch=lch, ps=ps, hf=hf, moff=moff, jg=jg, qb=qb: e.matmul(
                            ps[:, hf * 512 + moff: (hf + 1) * 512],
                            A["kt"][:, g * 2048 + lch * 128: g * 2048 + (lch + 1) * 128],
                            qb[:, jg * 512 + moff: (jg + 1) * 512],
                            start=True, stop=True), inc="pe")
                    smm_last = smm
                    P.wait("scalar", "pe", smm)
                    P.wait("scalar", "pe", pt_war)
                    src = ps[:].rearrange("p (two w) -> p two w", two=2)[:, :, moff:512]
                    dst = A["pt"][:, kap * 512:(kap + 2) * 512].rearrange(
                        "p (two w) -> p two w", two=2)[:, :, moff:512]
                    eac = P.op("scalar", lambda e, src=src, dst=dst: e.activation(
                        dst, src, AF.Exp, bias=A["bias20"][:], scale=QKSCALE), inc="ac")
                    slot_ac[(psname, 0)] = slot_ac[(psname, 1)] = eac
                    exp_all = eac
                    if 0 <= rp <= 3:
                        P.wait("vector", "ac", eac)
                        P.wait("vector", "sy", consts_sy)
                        for sg in range(2):
                            kp = kap + sg
                            if sg == 0:
                                mask_dv = P.op("vector", lambda e, kp=kp, rp=rp: e.tensor_mul(
                                    A["pt"][:, kp * 512 + rp * 128: kp * 512 + (rp + 1) * 128],
                                    A["pt"][:, kp * 512 + rp * 128: kp * 512 + (rp + 1) * 128],
                                    A["tri"][:]), inc="dv")
                            else:
                                mask_dv = P.op("vector", lambda e, kp=kp, rp=rp: e.tensor_scalar_mul(
                                    A["pt"][:, kp * 512 + rp * 128: kp * 512 + (rp + 1) * 128],
                                    A["pt"][:, kp * 512 + rp * 128: kp * 512 + (rp + 1) * 128],
                                    A["sig1"][:]), inc="dv")
                scores_prev[h % 2] = smm_last
                P.wait("tensor", "ac", exp_all)
                if mask_dv:
                    P.wait("tensor", "dv", mask_dv)
                lh = jg
                P.wait("tensor", "ac", slot_ac.get(("P01", lh), 0))
                lmm = 0
                for kap in range(nk):
                    _, _, _, moff, _ = slot_geom(jg, kap)
                    lmm = P.op("tensor", lambda e, kap=kap, moff=moff, lh=lh: e.matmul(
                        P01[:, lh * 512 + moff: (lh + 1) * 512], A["ones_sq"][:],
                        A["pt"][:, kap * 512 + moff: (kap + 1) * 512],
                        start=(kap == 0), stop=(kap == nk - 1)), inc="pe")
                oh = jg
                P.wait("tensor", "dv", ot_war[oh])
                pvm = 0
                for kap in range(nk):
                    _, _, _, moff, lch = slot_geom(jg, kap)
                    pvm = P.op("tensor", lambda e, kap=kap, moff=moff, lch=lch, g=g, oh=oh: e.matmul(
                        P23[:, oh * 512 + moff: (oh + 1) * 512],
                        A["vsb"][:, lch * 512 + g * 128: lch * 512 + (g + 1) * 128],
                        A["pt"][:, kap * 512 + moff: (kap + 1) * 512],
                        start=(kap == 0), stop=(kap == nk - 1)), inc="pe")
                pt_war = pvm
                pv_last[jg] = pvm
                P.wait("scalar", "pe", lmm)
                P.wait("scalar", "dv", ot_dv)
                P.op("scalar", lambda e, lh=lh: e.activation(
                    A["reca"][:], P01[:, lh * 512:(lh + 1) * 512], AF.Ln, bias=0.0, scale=1.0))
                rc = P.op("scalar", lambda e: e.activation(
                    A["recb"][:], A["reca"][:], AF.Exp, scale=-1.0), inc="ac")
                slot_ac[("P01", lh)] = rc
                P.wait("vector", "pe", pvm)
                P.wait("vector", "ac", rc)
                ot_dv = P.op("vector", lambda e, h=h, jg=jg, oh=oh: e.tensor_tensor(
                    A["ot"][:, h * 1024 + jg * 512: h * 1024 + (jg + 1) * 512],
                    P23[:, oh * 512:(oh + 1) * 512], A["recb"][:], ALU.mult), inc="dv")
                ot_war[oh] = ot_dv
                if jg == 0 and h + 1 < NH:
                    emit_qproj_rope(h + 1)
                    if h + 2 < NH:
                        P.wait("gpsimd", "pe", qproj_done[h])
                        waq_ld[h + 2] = P.dma("gpsimd", A["waq%d" % ((h + 2) % 2)][:],
                                              waq[h + 2], "gq")
            for _ in range(6):
                a_unit()

        attn_ot_dv = ot_dv
        attn_pe_done = pv_last[1]
        attn_ac_done = P.counts["ac"]
        if "qt0" in dump_aps:
            P.wait("sync", "dv", qrope_done[0])
            P.dma("sync", dump_aps["qt0"], A["qt0"][:], "sy")
        if "pt" in dump_aps:
            P.wait("sync", "pe", pt_war)
            P.dma("sync", dump_aps["pt"], A["pt"][:], "sy")
        if "ot" in dump_aps:
            P.wait("sync", "dv", attn_ot_dv)
            P.dma("sync", dump_aps["ot"], A["ot"][:], "sy")

        P.wait("gpsimd", "dv", qrope_done[NH - 1])
        P.wait("gpsimd", "dv", attn_ot_dv)
        P.wait("gpsimd", "pe", attn_pe_done)
        wao_mm = {}
        wao_ld = [0] * 4
        rt_ev = {}
        rt_store = [0] * NMC
        for mg in range(4):
            buf = A["wao%d" % (mg % 2)]
            if mg >= 2:
                P.wait("gpsimd", "pe", wao_mm[(mg - 2, NH - 1)])
            wao_ld[mg] = P.dma("gpsimd", buf[:], wao[mg], "gq")
            for h in range(NH):
                P.wait("tensor", "gq", wao_ld[mg])
                if h == 0:
                    P.wait("tensor", "ac", attn_ac_done)
                    P.wait("tensor", "dv", attn_ot_dv)
                    P.wait("tensor", "dv", qe_done[NH - 1])
                    if p67_war["eng"] is not None:
                        P.wait("tensor", p67_war["eng"], p67_war["val"])
                    if mg > 0:
                        for mc4 in range(4):
                            P.wait("tensor", "dv", rt_ev[(mg - 1, mc4)])
                    P.wait("tensor", "dv", attn_ot_dv)
                for mc4 in range(4):
                    for th in range(2):
                        wao_mm[(mg, h)] = P.op("tensor", lambda e, mc4=mc4, th=th, h=h, buf=buf: e.matmul(
                            PS[mc4][:, th * 512:(th + 1) * 512],
                            buf[:, h * 512 + mc4 * 128: h * 512 + (mc4 + 1) * 128],
                            A["ot"][:, h * 1024 + th * 512: h * 1024 + (th + 1) * 512],
                            start=(h == 0), stop=(h == NH - 1)), inc="pe")
            for mc4 in range(4):
                mc = mg * 4 + mc4
                rb = A["rts%d" % (mc4 % 2)]
                P.wait("vector", "pe", wao_mm[(mg, NH - 1)])
                if mc4 >= 2:
                    P.wait("vector", "sy", rt_store[mc - 2])
                rt_ev[(mg, mc4)] = P.op("vector", lambda e, mc4=mc4, rb=rb: e.tensor_copy(
                    rb[:], PS[mc4][:]), inc="dv")
                P.wait("sync", "dv", rt_ev[(mg, mc4)])
                rt_store[mc] = P.dma("sync", scrR[mc], rb[:], "sy")
            p67_war.update(eng="dv", val=rt_ev[(mg, 3)])
            b_unit()
            b_unit()
        rt_stores_done = rt_store[NMC - 1]
        p67_war.update(eng="dv", val=rt_ev[(3, 3)])

        load_act_table(tids["silu_and_others"])
        P.wait("gpsimd", "pe", attn_pe_done)
        P.wait("vector", "sy", rt_stores_done)
        f1_ld = [0] * NC_H
        f1_mm_g = {}
        f1_mm_l = {}
        f1_sil = {}
        f1_hid = {}
        for c in range(NC_H):
            w13b = A["w13b%d" % (c % 2)]
            w1b = w13b[:, 0:2048]
            w3b = w13b[:, 2048:4096]
            if c >= 2:
                P.wait("gpsimd", "pe", f1_mm_l[c - 2])
            f1_ld[c] = P.dma("gpsimd", w13b[:], w13s[c], "gq")
            P.wait("tensor", "gq", f1_ld[c])
            if c == 0:
                P.wait("tensor", "ac", attn_ac_done)
                P.wait("tensor", "dv", rt_ev[(3, 0)])
                P.wait("tensor", "dv", rt_ev[(3, 1)])
            for th in range(2):
                if (c, th) != (0, 0):
                    pass
                prev_sil = f1_sil.get((c - 1, th), 0)
                P.wait("tensor", "ac", prev_sil)
                for mc in range(NMC):
                    f1_mm_g[(c, th)] = P.op("tensor", lambda e, th=th, mc=mc, w1b=w1b: e.matmul(
                        P01[:, th * 512:(th + 1) * 512],
                        w1b[:, mc * 128:(mc + 1) * 128],
                        A["xnq"][:, mc * 1024 + th * 512: mc * 1024 + (th + 1) * 512],
                        start=(mc == 0), stop=(mc == NMC - 1)), inc="pe")
            for th in range(2):
                prev_hid = f1_hid.get((c - 1, th), 0)
                P.wait("tensor", "dv", prev_hid)
                for mc in range(NMC):
                    f1_mm_l[c] = P.op("tensor", lambda e, th=th, mc=mc, w3b=w3b: e.matmul(
                        P23[:, th * 512:(th + 1) * 512],
                        w3b[:, mc * 128:(mc + 1) * 128],
                        A["xnq"][:, mc * 1024 + th * 512: mc * 1024 + (th + 1) * 512],
                        start=(mc == 0), stop=(mc == NMC - 1)), inc="pe")
                f1_mm_l.setdefault((c, th), f1_mm_l[c])
                f1_mm_l[(c, th)] = f1_mm_l[c]
            for th in range(2):
                P.wait("scalar", "pe", f1_mm_g[(c, th)])
                f1_sil[(c, th)] = P.op("scalar", lambda e, th=th: e.activation(
                    A["gsil"][:, th * 512:(th + 1) * 512], P01[:, th * 512:(th + 1) * 512],
                    AF.Silu, bias=0.0, scale=1.0), inc="ac")
                P.wait("vector", "ac", f1_sil[(c, th)])
                P.wait("vector", "pe", f1_mm_l[(c, th)])
                f1_hid[(c, th)] = P.op("vector", lambda e, c=c, th=th: e.tensor_tensor(
                    A["hidden"][:, c * 1024 + th * 512: c * 1024 + (th + 1) * 512],
                    A["gsil"][:, th * 512:(th + 1) * 512],
                    P23[:, th * 512:(th + 1) * 512], ALU.mult), inc="dv")
            if b_st["pp"] < 64:
                b_unit()
        hid_done = f1_hid[(NC_H - 1, 1)]
        f1_pe_done = f1_mm_l[NC_H - 1]
        if "hid" in dump_aps:
            P.wait("sync", "dv", hid_done)
            P.dma("sync", dump_aps["hid"], A["hidden"][:], "sy")

        assert b_st["pp"] >= 64, "B-step incomplete: %d" % b_st["pp"]
        P.wait("gpsimd", "pe", f1_pe_done)
        P.wait("gpsimd", "dv", hid_done)
        P.wait("gpsimd", "sy", b_st["st"])
        P.wait("sync", "pe", f1_pe_done)
        P.wait("sync", "sy", b_st["st"])
        f2_ld = {}
        f2_mm = {}
        f2_sil = {}
        cmb_dv = {}
        out_st = [0] * NMC
        xc_ld = {}
        rt_ld = {}
        for mq in range(4):
            for c in range(NC_H):
                wqb = A["w2q%d" % (c % 4)]
                prev = f2_mm.get((mq, c - 4), f2_mm.get((mq - 1, NC_H - 4 + (c % 4)), 0))
                P.wait("sync", "pe", prev)
                f2_ld[(mq, c)] = P.dma("sync", wqb[:], scrB[mq, c], "sy")
                P.wait("tensor", "sy", f2_ld[(mq, c)])
                if c == 0:
                    P.wait("tensor", "dv", hid_done)
                    if mq == 0:
                        P.wait("tensor", "ac", f1_sil[(NC_H - 1, 0)])
                        P.wait("tensor", "ac", f1_sil[(NC_H - 1, 1)])
                        P.wait("tensor", "dv", f1_hid[(NC_H - 1, 0)])
                        P.wait("tensor", "dv", f1_hid[(NC_H - 1, 1)])
                        if p67_war["eng"] is not None:
                            P.wait("tensor", p67_war["eng"], p67_war["val"])
                    else:
                        for mc4 in range(4):
                            P.wait("tensor", "ac", f2_sil[(mq - 1, mc4)])
                for mc4 in range(4):
                    for th in range(2):
                        f2_mm[(mq, c)] = P.op("tensor", lambda e, mc4=mc4, th=th, c=c, wqb=wqb: e.matmul(
                            PS[mc4][:, th * 512:(th + 1) * 512],
                            wqb[:, mc4 * 128:(mc4 + 1) * 128],
                            A["hidden"][:, c * 1024 + th * 512: c * 1024 + (th + 1) * 512],
                            start=(c == 0), stop=(c == NC_H - 1)), inc="pe")
            for mc4 in range(4):
                mc = mq * 4 + mc4
                P.wait("sync", "dv", cmb_dv.get(mc - 2, 0))
                for s4 in range(NSLAB):
                    xc_ld[mc] = P.dma("sync", A["xc%d" % (mc % 2)][:, s4 * 256:(s4 + 1) * 256],
                                      xs[s4, :, mc, 0:256], "sy")
                P.wait("gpsimd", "sy", rt_stores_done)
                P.wait("gpsimd", "dv", cmb_dv.get(mc - 2, 0))
                rt_ld[mc] = P.dma("gpsimd", A["rt%d" % (mc % 2)][:], scrR[mc], "gq")
                P.wait("scalar", "pe", f2_mm[(mq, NC_H - 1)])
                P.wait("scalar", "dv", cmb_dv.get(mc - 1, 0))
                f2_sil[(mq, mc4)] = P.op("scalar", lambda e, mc4=mc4: e.activation(
                    A["gf"][:], PS[mc4][:], AF.Silu, bias=0.0, scale=1.0), inc="ac")
                P.wait("vector", "ac", f2_sil[(mq, mc4)])
                P.wait("vector", "gq", rt_ld[mc])
                P.wait("vector", "sy", xc_ld[mc])
                P.wait("vector", "sy", b_st["st"])
                if mc >= 2:
                    P.wait("vector", "sy", out_st[mc - 2])
                P.op("vector", lambda e, mc=mc: e.tensor_mul(
                    A["tf"][:], A["gf"][:], A["rt%d" % (mc % 2)][:]))
                cmb_dv[mc] = P.op("vector", lambda e, mc=mc: e.tensor_add(
                    A["ou%d" % (mc % 2)][:], A["tf"][:], A["xc%d" % (mc % 2)][:]), inc="dv")
                P.wait("sync", "dv", cmb_dv[mc])
                out_st[mc] = P.dma("sync", out_t[mc], A["ou%d" % (mc % 2)][:], "sy")
        if "rt" in dump_aps:
            P.wait("sync", "sy", rt_stores_done)
            for mc in range(NMC):
                P.dma("sync", dump_aps["rt"][mc], scrR[mc], "sy")

        P.wait("sync", "sy", P.counts["sy"])
        P.run(block)
    return nc



def _rope_tables():
    pos = np.arange(T, dtype=np.float32)[:, None]
    freqs = np.power(np.float32(10000.0),
                     -(np.arange(64, dtype=np.float32) / np.float32(64.0)))[None, :].astype(np.float32)
    ang = (pos * freqs).astype(np.float32)
    cos = np.cos(ang.astype(np.float64)).astype(np.float32)
    sin = np.sin(ang.astype(np.float64)).astype(np.float32)
    ck = np.empty((128, T), np.float32)
    sk = np.empty((128, T), np.float32)
    ck[0:64] = cos.T
    ck[64:128] = cos.T
    sk[0:64] = -sin.T
    sk[64:128] = sin.T
    return ck, sk


def _local_perm(par):
    idx = []
    for s in range(NSLAB):
        us = [par, 2 + par, 1 - par, 3 - par]
        for u in us:
            c0 = (s * 4 + u) * 128
            idx.extend(range(c0, c0 + 128))
    return np.array(idx, np.int64)


def prepare_in_maps(L0_attn_x, L0_attn_w_aq, L0_attn_w_ak, L0_attn_w_av, L0_attn_w_ao,
                    L0_ffn_W1, L0_ffn_W2, L0_ffn_W3):
    x = np.ascontiguousarray(np.asarray(L0_attn_x, dtype=np.float32))
    w_aq = np.asarray(L0_attn_w_aq, np.float32)
    w_ak = np.asarray(L0_attn_w_ak, np.float32)
    w_av = np.asarray(L0_attn_w_av, np.float32)
    w_ao = np.asarray(L0_attn_w_ao, np.float32)
    W1 = np.asarray(L0_ffn_W1, np.float32)
    W2 = np.asarray(L0_ffn_W2, np.float32)
    W3 = np.asarray(L0_ffn_W3, np.float32)

    waq_h = np.ascontiguousarray(
        w_aq.reshape(NH, 16, 128, 128).transpose(0, 2, 1, 3).reshape(NH, 128, 2048))
    wak_h = np.ascontiguousarray(
        w_ak.reshape(G, 16, 128, 128).transpose(2, 0, 1, 3).reshape(128, 8192))
    wav_h = np.ascontiguousarray(
        w_av.reshape(G, 16, 128, 128).transpose(2, 1, 0, 3).reshape(128, 8192))
    wao_h = np.ascontiguousarray(
        w_ao.reshape(NH, 128, 4, 512).transpose(2, 1, 0, 3).reshape(4, 128, 8192))
    w1_h = W1.reshape(NC_H, 128, NMC, 128).transpose(0, 3, 2, 1).reshape(NC_H, 128, 2048)
    w3_h = W3.reshape(NC_H, 128, NMC, 128).transpose(0, 3, 2, 1).reshape(NC_H, 128, 2048)
    w13_h = np.ascontiguousarray(
        np.stack([w1_h, w3_h], axis=2).reshape(NC_H, 128, 4096))
    w2t_h = np.ascontiguousarray(
        W2.T.reshape(32, 2, 128, 2048).transpose(0, 2, 1, 3).reshape(32, 128, 4096))

    ck, sk = _rope_tables()

    def sylvester(n):
        m = np.array([[1]], np.int64)
        while m.shape[0] < n:
            m = np.block([[m, m], [m, -m]])
        return m

    h128_h = sylvester(128).astype(ml_dtypes.bfloat16)
    h64 = sylvester(64)
    h64d_h = np.zeros((128, 128), np.int64)
    h64d_h[0:64, 0:64] = h64
    h64d_h[64:128, 64:128] = h64
    h64d_h = h64d_h.astype(ml_dtypes.bfloat16)
    tri_h = np.triu(np.ones((128, 128), np.float32)).astype(ml_dtypes.bfloat16)

    in_maps = []
    perms = {}
    for core in range(8):
        b, par = core // 2, core % 2
        perm = _local_perm(par)
        perms[core] = perm
        xl = x[b][perm]
        xs_h = np.ascontiguousarray(
            xl.reshape(NSLAB, 512, NMC, 128).transpose(0, 3, 2, 1))
        ckl = np.ascontiguousarray(ck[:, perm])
        skl = np.ascontiguousarray(sk[:, perm])
        qsel = perm.reshape(NSLAB, 4, 128)[:, 0:2].reshape(-1)
        cql = np.ascontiguousarray(ck[:, qsel])
        sql = np.ascontiguousarray(sk[:, qsel])
        sig1_h = np.full((128, 1), 1.0 if par == 1 else 0.0, np.float32)
        in_maps.append({
            "xs": xs_h, "waq": waq_h, "wak": wak_h, "wav": wav_h, "wao": wao_h,
            "w13s": w13_h, "w2t": w2t_h,
            "cosk": ckl, "sink": skl, "cosq": cql, "sinq": sql,
            "tri": tri_h, "sig1": sig1_h, "h128": h128_h, "h64d": h64d_h,
        })
    return in_maps, perms


def kernel(L0_attn_x, L0_attn_w_aq, L0_attn_w_ak, L0_attn_w_av, L0_attn_w_ao,
           L0_ffn_W1, L0_ffn_W2, L0_ffn_W3):
    in_maps, perms = prepare_in_maps(
        L0_attn_x, L0_attn_w_aq, L0_attn_w_ak, L0_attn_w_av, L0_attn_w_ao,
        L0_ffn_W1, L0_ffn_W2, L0_ffn_W3)
    nc = build_nc()
    res = run_bass_kernel_spmd(nc, in_maps, core_ids=list(range(8)))

    out = np.empty((B, T, M), np.float32)
    for core in range(8):
        b = core // 2
        oc = res.results[core]["out"]
        ov = oc.reshape(NMC, 128, 8, 128).transpose(2, 3, 0, 1).reshape(1024, M)
        qsel = perms[core].reshape(NSLAB, 4, 128)[:, 0:2].reshape(-1)
        out[b][qsel] = ov
    return out


if __name__ == "__main__":
    pass


# version 09
# speedup vs baseline: 1.1180x; 1.1180x over previous
import math

import numpy as np
import ml_dtypes

import concourse.bass as bass
import concourse.mybir as mybir
from concourse.bass_utils import run_bass_kernel_spmd

F32 = mybir.dt.float32
BF16 = mybir.dt.bfloat16
AF = mybir.ActivationFunctionType
ALU = mybir.AluOpType

B, T, M, D, G, NH = 4, 2048, 2048, 128, 4, 16
HF = 8192
NMC = 16
NC_H = 64
NSLAB = 4
KB = 1024

QKSCALE = float(128.0 ** -0.5)
S8192 = float(1.0 / math.sqrt(8192.0))
EXP_SHIFT = -20.0
EPS = 1e-5

ENGS = ("tensor", "scalar", "vector", "sync", "gpsimd")

_TABLE_IDS = None


def _table_ids(nc):
    global _TABLE_IDS
    if _TABLE_IDS is None:
        from concourse.hw_specs import get_activation_tables
        names = list(get_activation_tables(nc.m.arch).keys())
        _TABLE_IDS = {n: i for i, n in enumerate(names)}
    return _TABLE_IDS


class Plan:
    def __init__(self, nc):
        self.nc = nc
        self.items = {e: [] for e in ENGS}
        self.sem_handles = {}
        self.counts = {}
        self.watermark = {}

    def add_sem(self, name, handle):
        self.sem_handles[name] = handle
        self.counts[name] = 0

    def wait(self, eng, sem, val=None):
        if val is None:
            val = self.counts[sem]
        assert val <= self.counts[sem], (eng, sem, val, self.counts[sem])
        if val <= 0:
            return
        key = (eng, sem)
        if self.watermark.get(key, -1) >= val:
            return
        self.watermark[key] = val
        h = self.sem_handles[sem]
        self.items[eng].append(lambda e, h=h, v=val: e.wait_ge(h, v))

    def op(self, eng, fn, inc=None, inc_n=1):
        if inc is not None:
            self.counts[inc] += inc_n
            h = self.sem_handles[inc]

            def f(e, fn=fn, h=h, n=inc_n):
                fn(e).then_inc(h, n)
        else:
            def f(e, fn=fn):
                fn(e)
        self.items[eng].append(f)
        return self.counts[inc] if inc is not None else None

    def dma(self, eng, out, in_, sem):
        return self.op(eng, lambda e, o=out, i=in_: e.dma_start(out=o, in_=i),
                       inc=sem, inc_n=16)

    def run(self, block):
        block.tensor(lambda e: [f(e) for f in self.items["tensor"]])
        block.scalar(lambda e: [f(e) for f in self.items["scalar"]])
        block.vector(lambda e: [f(e) for f in self.items["vector"]])
        block.sync(lambda e: [f(e) for f in self.items["sync"]])
        block.gpsimd(lambda e: [f(e) for f in self.items["gpsimd"]])


def build_nc(dumps=()):
    nc = bass.Bass("TRN2", target_bir_lowering=False)

    xs = nc.dram_tensor("xs", [NSLAB, 128, NMC, 512], F32, kind="ExternalInput").ap()
    xq = nc.dram_tensor("xq", [NMC, 128, 1024], F32, kind="ExternalInput").ap()
    waq = nc.dram_tensor("waq", [NH, 128, 2048], F32, kind="ExternalInput").ap()
    wak = nc.dram_tensor("wak", [128, 8192], F32, kind="ExternalInput").ap()
    wav = nc.dram_tensor("wav", [128, 8192], F32, kind="ExternalInput").ap()
    wao = nc.dram_tensor("wao", [4, 128, 8192], F32, kind="ExternalInput").ap()
    w13s = nc.dram_tensor("w13s", [NC_H, 128, 4096], F32, kind="ExternalInput").ap()
    w2t = nc.dram_tensor("w2t", [HF, 2048], F32, kind="ExternalInput").ap()
    cosk = nc.dram_tensor("cosk", [128, 2048], F32, kind="ExternalInput").ap()
    sink = nc.dram_tensor("sink", [128, 2048], F32, kind="ExternalInput").ap()
    cosq = nc.dram_tensor("cosq", [128, 1024], F32, kind="ExternalInput").ap()
    sinq = nc.dram_tensor("sinq", [128, 1024], F32, kind="ExternalInput").ap()
    tri_i = nc.dram_tensor("tri", [128, 128], BF16, kind="ExternalInput").ap()
    sig1_i = nc.dram_tensor("sig1", [128, 1], F32, kind="ExternalInput").ap()
    h128_i = nc.dram_tensor("h128", [128, 128], BF16, kind="ExternalInput").ap()
    h64d_i = nc.dram_tensor("h64d", [128, 128], BF16, kind="ExternalInput").ap()

    out_t = nc.dram_tensor("out", [NMC, 128, 1024], F32, kind="ExternalOutput").ap()
    dump_aps = {}
    for nm in dumps:
        shapes = {"xn": [128, 32768], "kt": [128, 8192], "vsb": [128, 8192],
                  "ot": [128, 16384], "hid": [128, 65536], "pt": [128, 8192],
                  "qt0": [128, 1024], "rt": [NMC, 128, 1024], "xnq": [128, 16384]}
        dt = BF16
        dump_aps[nm] = nc.dram_tensor("dump_" + nm, shapes[nm], dt, kind="ExternalOutput").ap()

    scrB = nc.dram_tensor("scrB", [4, 64, 128, 512], BF16, kind="Internal").ap()
    scrR = nc.dram_tensor("scrR", [NMC, 128, 1024], BF16, kind="Internal").ap()

    A = {}

    SB_BASE = 16512

    def sb(name, shape, dtype, off_kb):
        A[name] = nc.alloc_sbuf_tensor_at(name, shape, dtype,
                                          offset=SB_BASE + int(off_kb * KB))
        return A[name]

    sb("ones_sq", [128, 128], BF16, 0.0)
    sb("bias20", [128, 1], F32, 0.25)
    sb("epsb", [128, 1], F32, 0.28125)
    sb("sig1", [128, 1], F32, 0.3125)
    sb("tri", [128, 128], BF16, 0.5)
    sb("h128", [128, 128], BF16, 0.75)
    sb("h64d", [128, 128], BF16, 1.0)
    sb("cosq", [128, 1024], F32, 1.25)
    sb("sinq", [128, 1024], F32, 5.25)
    sb("xnq", [128, NMC * 1024], BF16, 9.25)
    XN = 41.5
    sb("xn", [128, NMC * 2048], BF16, XN)
    sb("hidden", [128, NC_H * 1024], BF16, XN)
    sb("qraw", [128, 1024], F32, XN)
    sb("qshift", [128, 1024], F32, XN + 4)
    sb("qtmp", [128, 1024], F32, XN + 8)
    sb("waq0", [128, 2048], BF16, XN + 12)
    sb("waq1", [128, 2048], BF16, XN + 16)
    sb("reca", [128, 512], F32, XN + 20)
    sb("recb", [128, 512], F32, XN + 22)
    sb("wao0", [128, 8192], BF16, XN)
    sb("wao1", [128, 8192], BF16, XN + 16)
    sb("rts0", [128, 1024], BF16, XN + 32)
    sb("rts1", [128, 1024], BF16, XN + 34)
    sb("xslab", [128, NMC * 512], F32, 105.5)
    sb("xb", [128, NMC * 512], BF16, 9.25)
    sb("xsq", [128, NMC * 512], BF16, 25.25)
    sb("ln_mean", [128, 512], F32, 137.5)
    sb("ln_t1", [128, 512], F32, 139.5)
    sb("ln_t2", [128, 512], F32, 141.5)
    sb("kraw", [128, 1024], F32, 105.5)
    sb("kshift", [128, 1024], F32, 109.5)
    sb("ktmp", [128, 1024], F32, 113.5)
    sb("cosk", [128, 2048], F32, 117.5)
    sb("sink", [128, 2048], F32, 125.5)
    sb("wakb", [128, 8192], BF16, 133.5)
    sb("wavb", [128, 8192], BF16, 149.5)
    sb("pt", [128, 16 * 512], BF16, 105.5)
    sb("ot", [128, NH * 1024], BF16, 121.5)
    sb("qt0", [128, 1024], BF16, 153.5)
    sb("qt1", [128, 1024], BF16, 155.5)
    sb("kt", [128, G * 2048], BF16, 157.5)
    sb("vsb", [128, 16 * 512], BF16, 173.5)
    sb("bin0", [128, 2048], BF16, 189.5)
    sb("bin1", [128, 2048], BF16, 193.5)
    sb("bout0", [128, 2048], BF16, 197.5)
    sb("bout1", [128, 2048], BF16, 201.5)
    sb("w13b0", [128, 4096], BF16, 169.5)
    sb("w13b1", [128, 4096], BF16, 177.5)
    sb("w13b2", [128, 4096], BF16, 189.5)
    sb("gsil", [128, 1024], F32, 185.5)
    for _i in range(8):
        sb("w2q%d" % _i, [128, 512], BF16, 169.5 + _i)
    sb("gf", [128, 1024], F32, 177.5)
    sb("xc0", [128, 1024], F32, 181.5)
    sb("xc1", [128, 1024], F32, 185.5)
    sb("rt0", [128, 1024], F32, 189.5)
    sb("rt1", [128, 1024], F32, 193.5)
    sb("ou0", [128, 1024], F32, 197.5)
    sb("ou1", [128, 1024], F32, 201.5)

    P01 = nc.alloc_psum_tensor("P01", [128, 1024], F32)
    P23 = nc.alloc_psum_tensor("P23", [128, 1024], F32)
    P45 = nc.alloc_psum_tensor("P45", [128, 1024], F32)
    P67 = nc.alloc_psum_tensor("P67", [128, 1024], F32)
    PS = [P01, P23, P45, P67]

    with (
        nc.semaphore("s_pe") as s_pe,
        nc.semaphore("s_ac") as s_ac,
        nc.semaphore("s_dv") as s_dv,
        nc.semaphore("s_sy") as s_sy,
        nc.semaphore("s_gq") as s_gq,
        nc.semaphore("s_sh") as s_sh,
        nc.semaphore("s_bl") as s_bl,
        nc.Block() as block,
    ):
        P = Plan(nc)
        for nm, h in (("pe", s_pe), ("ac", s_ac), ("dv", s_dv), ("sy", s_sy),
                      ("gq", s_gq), ("sh", s_sh), ("bl", s_bl)):
            P.add_sem(nm, h)

        tids = _table_ids(nc)

        def load_act_table(tid):
            P.op("scalar", lambda e, tid=tid: e.add_instruction(mybir.InstLoadActFuncSet(
                name=nc.get_next_instruction_name(), act_func_set_id=tid, ins=[], outs=[])))

        P.op("vector", lambda e: e.memset(A["ones_sq"][:], 1.0))
        P.op("vector", lambda e: e.memset(A["bias20"][:], EXP_SHIFT))
        consts_dv = P.op("vector", lambda e: e.memset(A["epsb"][:], EPS), inc="dv")
        P.dma("sync", A["tri"][:], tri_i, "sy")
        P.dma("sync", A["sig1"][:], sig1_i, "sy")
        P.dma("sync", A["h128"][:], h128_i, "sy")
        P.dma("sync", A["h64d"][:], h64d_i, "sy")
        P.dma("sync", A["cosq"][:], cosq, "sy")
        P.dma("sync", A["sinq"][:], sinq, "sy")
        consts_sy = P.counts["sy"]
        load_act_table(tids["natural_log_exp_and_others"])

        p67_war = {"eng": None, "val": 0}
        b_st = {"k": 0, "ld": {}, "mm": {}, "stores": {}, "st": 0}

        def b_load(pp):
            if pp > 63 or pp in b_st["ld"]:
                return
            if pp >= 2:
                P.wait("gpsimd", "pe", b_st["mm"].get(2 * (pp - 2) + 1, 0))
            bi = A["bin%d" % (pp % 2)]
            P.dma("gpsimd", bi[0:64, :],
                  w2t.rearrange("(c p) m -> c p m", p=128)[:, 2 * pp, :], "bl")
            b_st["ld"][pp] = P.dma("gpsimd", bi[64:128, :],
                  w2t.rearrange("(c p) m -> c p m", p=128)[:, 2 * pp + 1, :], "bl")

        def b_half():
            k = b_st["k"]
            if k >= 128:
                return
            pp, u = k >> 1, k & 1
            bi = A["bin%d" % (pp % 2)]
            bo = A["bout%d" % (pp % 2)]
            b_load(pp)
            b_load(pp + 1)
            P.wait("tensor", "sy", consts_sy)
            P.wait("tensor", "bl", b_st["ld"][pp])
            if p67_war["eng"] is not None:
                P.wait("tensor", p67_war["eng"], p67_war["val"])
            for j in range(2):
                mm = P.op("tensor", lambda e, u=u, j=j, bi=bi: e.matmul(
                    P67[:, j * 512:(j + 1) * 512], A["h64d"][:],
                    bi[:, u * 1024 + j * 512: u * 1024 + (j + 1) * 512],
                    start=True, stop=True), inc="pe")
            b_st["mm"][k] = mm
            buf_war = b_st["stores"].get(k - 4, 0)
            if k % 2 == 0:
                P.wait("scalar", "pe", mm)
                P.wait("scalar", "sy", buf_war)
                ev = P.op("scalar", lambda e, bo=bo, u=u: e.activation(
                    bo[:, u * 1024:(u + 1) * 1024], P67[:], AF.Copy,
                    bias=0.0, scale=S8192), inc="ac")
                p67_war.update(eng="ac", val=ev)
                ev_eng = "ac"
            else:
                P.wait("vector", "pe", mm)
                P.wait("vector", "sy", buf_war)
                ev = P.op("vector", lambda e, bo=bo, u=u: e.tensor_scalar_mul(
                    bo[:, u * 1024:(u + 1) * 1024], P67[:], S8192), inc="dv")
                p67_war.update(eng="dv", val=ev)
                ev_eng = "dv"
            P.wait("sync", ev_eng, ev)
            for j in range(2):
                dst = scrB[2 * u:2 * u + 2, :, 2 * pp + j, :].rearrange("q c m -> c q m")
                b_st["st"] = P.dma("sync", dst,
                                   bo[j * 64:(j + 1) * 64, u * 1024:(u + 1) * 1024], "sy")
            b_st["stores"][k] = b_st["st"]
            b_st["k"] = k + 1

        xslab_v = A["xslab"][:].rearrange("p (mc t) -> p mc t", mc=NMC)
        xld = [0] * NSLAB
        apply_done = [0] * NSLAB
        stats_done = [0] * NSLAB
        rows_done = [0] * NSLAB

        xbld = [0] * NSLAB
        v_ev = [0] * 16
        ld_wav = 0
        for s in range(NSLAB):
            if s > 0:
                P.wait("sync", "dv", apply_done[s - 1])
                P.wait("gpsimd", "pe", stats_done[s - 1])
            P.dma("sync", xslab_v[:, 0:8, :], xs[s][:, 0:8, :], "sy")
            xld[s] = P.dma("sync", xslab_v[:, 8:16, :], xs[s][:, 8:16, :], "sy")
            if s == 0:
                ld_wav = P.dma("gpsimd", A["wavb"][:], wav, "gq")
            P.wait("scalar", "sy", xld[s] - 16)
            if s > 0:
                P.wait("scalar", "pe", stats_done[s - 1])
            P.op("scalar", lambda e: e.activation(
                A["xb"][:, 0:4096], A["xslab"][:, 0:4096], AF.Copy, bias=0.0, scale=1.0), inc="ac")
            P.op("scalar", lambda e: e.activation(
                A["xsq"][:, 0:4096], A["xslab"][:, 0:4096], AF.Square, bias=0.0, scale=1.0), inc="ac")
            P.wait("scalar", "sy", xld[s])
            P.op("scalar", lambda e: e.activation(
                A["xb"][:, 4096:8192], A["xslab"][:, 4096:8192], AF.Copy, bias=0.0, scale=1.0), inc="ac")
            sq_ac = P.op("scalar", lambda e: e.activation(
                A["xsq"][:, 4096:8192], A["xslab"][:, 4096:8192], AF.Square, bias=0.0, scale=1.0), inc="ac")
            P.wait("tensor", "ac", sq_ac)
            P.wait("tensor", "dv", consts_dv)
            if s > 0:
                P.wait("tensor", "dv", rows_done[s - 1])
            for mc in range(NMC):
                P.op("tensor", lambda e, mc=mc: e.matmul(
                    P01[:, 0:512], A["ones_sq"][:], A["xb"][:, mc * 512:(mc + 1) * 512],
                    start=(mc == 0), stop=(mc == NMC - 1)), inc="pe")
            for mc in range(NMC):
                stats_done[s] = P.op("tensor", lambda e, mc=mc: e.matmul(
                    P01[:, 512:1024], A["ones_sq"][:], A["xsq"][:, mc * 512:(mc + 1) * 512],
                    start=(mc == 0), stop=(mc == NMC - 1)), inc="pe")
            P.wait("vector", "pe", stats_done[s])
            P.op("vector", lambda e: e.tensor_scalar_mul(A["ln_mean"][:], P01[:, 0:512], 1.0 / 2048.0))
            P.op("vector", lambda e: e.tensor_scalar_mul(A["ln_t1"][:], P01[:, 512:1024], 1.0 / 2048.0))
            P.op("vector", lambda e: e.tensor_mul(A["ln_t2"][:], A["ln_mean"][:], A["ln_mean"][:]))
            rows_done[s] = P.op("vector", lambda e: e.tensor_sub(
                A["ln_t1"][:], A["ln_t1"][:], A["ln_t2"][:]), inc="dv")
            P.wait("scalar", "dv", rows_done[s])
            P.op("scalar", lambda e: e.activation(
                A["ln_t2"][:], A["ln_t1"][:], AF.Ln, bias=A["epsb"][:], scale=1.0))
            ac_rstd = P.op("scalar", lambda e: e.activation(
                A["ln_t1"][:], A["ln_t2"][:], AF.Exp, scale=-0.5), inc="ac")
            P.wait("vector", "ac", ac_rstd)
            P.op("vector", lambda e: e.tensor_mul(A["ln_t2"][:], A["ln_mean"][:], A["ln_t1"][:]))
            for mc in range(NMC):
                P.op("vector", lambda e, mc=mc: e.tensor_mul(
                    A["xslab"][:, mc * 512:(mc + 1) * 512],
                    A["xslab"][:, mc * 512:(mc + 1) * 512], A["ln_t1"][:]))
                apply_done[s] = P.op("vector", lambda e, mc=mc, s=s: e.tensor_sub(
                    A["xn"][:, mc * 2048 + s * 512: mc * 2048 + (s + 1) * 512],
                    A["xslab"][:, mc * 512:(mc + 1) * 512], A["ln_t2"][:]), inc="dv")
            P.wait("tensor", "gq", ld_wav)
            P.wait("tensor", "dv", apply_done[s])
            for tc in range(4 * s, 4 * s + 4):
                half = tc % 2
                if tc >= 2:
                    P.wait("tensor", "dv", v_ev[tc - 2])
                for mc in range(NMC):
                    vmm = P.op("tensor", lambda e, tc=tc, mc=mc, half=half: e.matmul(
                        P23[:, half * 512:(half + 1) * 512],
                        A["xn"][:, mc * 2048 + tc * 128: mc * 2048 + (tc + 1) * 128],
                        A["wavb"][:, mc * 512:(mc + 1) * 512],
                        start=(mc == 0), stop=(mc == NMC - 1)), inc="pe")
                P.wait("vector", "pe", vmm)
                v_ev[tc] = P.op("vector", lambda e, tc=tc, half=half: e.tensor_copy(
                    A["vsb"][:, tc * 512:(tc + 1) * 512],
                    P23[:, half * 512:(half + 1) * 512]), inc="dv")
            for _ in range(4):
                b_half()
        xn_done = apply_done[NSLAB - 1]
        v_done_dv = v_ev[15]

        P.wait("vector", "pe", stats_done[NSLAB - 1])
        xnq_done = 0
        for mc in range(NMC):
            src = A["xn"][:, mc * 2048:(mc + 1) * 2048].rearrange(
                "p (s t) -> p s t", s=NSLAB)[:, :, 0:256]
            xnq_done = P.op("vector", lambda e, mc=mc, src=src: e.tensor_copy(
                A["xnq"][:, mc * 1024:(mc + 1) * 1024], src), inc="dv")
        if "xn" in dump_aps:
            P.wait("sync", "dv", xn_done)
            P.dma("sync", dump_aps["xn"], A["xn"][:], "sy")
        if "xnq" in dump_aps:
            P.wait("sync", "dv", xnq_done)
            P.dma("sync", dump_aps["xnq"], A["xnq"][:], "sy")

        P.wait("gpsimd", "pe", stats_done[NSLAB - 1])
        P.wait("gpsimd", "dv", xn_done)
        ld_wak = P.dma("gpsimd", A["wakb"][:], wak, "gq")
        P.wait("sync", "dv", xn_done)
        P.dma("sync", A["cosk"][:], cosk, "sy")
        ld_sk = P.dma("sync", A["sink"][:], sink, "sy")

        P.wait("tensor", "gq", ld_wak)
        kv_done_dv = 0
        kt_war = [v_ev[14], v_ev[15]]
        for g in range(G):
            for th in range(2):
                ev = 0
                for tq in range(2):
                    q = th * 2 + tq
                    if kt_war[tq]:
                        P.wait("tensor", "dv", kt_war[tq])
                    for mc in range(NMC):
                        kmm = P.op("tensor", lambda e, g=g, q=q, mc=mc, tq=tq: e.matmul(
                            P23[:, tq * 512:(tq + 1) * 512],
                            A["wakb"][:, (g * 16 + mc) * 128:(g * 16 + mc + 1) * 128],
                            A["xn"][:, mc * 2048 + q * 512: mc * 2048 + (q + 1) * 512],
                            start=(mc == 0), stop=(mc == NMC - 1)), inc="pe")
                    P.wait("vector", "pe", kmm)
                    ev = P.op("vector", lambda e, tq=tq: e.tensor_copy(
                        A["kraw"][:, tq * 512:(tq + 1) * 512],
                        P23[:, tq * 512:(tq + 1) * 512]), inc="dv")
                P.wait("sync", "dv", ev)
                P.dma("sync", A["kshift"][0:64, :], A["kraw"][64:128, :], "sh")
                shf = P.dma("sync", A["kshift"][64:128, :], A["kraw"][0:64, :], "sh")
                P.wait("vector", "sh", shf)
                P.wait("vector", "sy", ld_sk)
                P.op("vector", lambda e, th=th: e.tensor_mul(
                    A["ktmp"][:], A["kraw"][:], A["cosk"][:, th * 1024:(th + 1) * 1024]))
                P.op("vector", lambda e, th=th: e.tensor_mul(
                    A["kshift"][:], A["kshift"][:], A["sink"][:, th * 1024:(th + 1) * 1024]))
                kv_done_dv = P.op("vector", lambda e, g=g, th=th: e.tensor_add(
                    A["kt"][:, g * 2048 + th * 1024: g * 2048 + (th + 1) * 1024],
                    A["ktmp"][:], A["kshift"][:]), inc="dv")
                kt_war[0] = kv_done_dv
                kt_war[1] = kv_done_dv
                b_half()
                b_half()
        if "kt" in dump_aps:
            P.wait("sync", "dv", kv_done_dv)
            P.dma("sync", dump_aps["kt"], A["kt"][:], "sy")
        if "vsb" in dump_aps:
            P.wait("sync", "dv", v_done_dv)
            P.dma("sync", dump_aps["vsb"], A["vsb"][:], "sy")

        def slot_geom(jg, kap):
            Pp, sg = kap >> 1, kap & 1
            rp = Pp - 4 * jg
            moff = max(0, rp) * 128
            lch = (Pp >> 1) * 4 + sg * 2 + (Pp & 1)
            return Pp, sg, rp, moff, lch

        waq_ld = [0] * (NH + 1)
        qproj_done = [0] * NH
        qe_done = [0] * NH
        qrope_done = [0] * NH
        slot_ac = {}
        exp_all = 0
        pt_war = 0
        ot_war = [kv_done_dv, kv_done_dv]
        ot_dv = 0
        pv_last = [0, 0]
        scores_prev = [0, 0]

        def emit_qproj_rope(h):
            qb = A["qt%d" % (h % 2)]
            wb = A["waq%d" % (h % 2)]
            P.wait("tensor", "gq", waq_ld[h])
            P.wait("tensor", "dv", xnq_done)
            P.wait("tensor", "ac", slot_ac.get(("P45", 0), 0))
            P.wait("tensor", "ac", slot_ac.get(("P45", 1), 0))
            if h >= 1:
                P.wait("tensor", "dv", qe_done[h - 1])
            for th in range(2):
                for mc in range(NMC):
                    qproj_done[h] = P.op("tensor", lambda e, th=th, mc=mc, wb=wb: e.matmul(
                        P45[:, th * 512:(th + 1) * 512],
                        wb[:, mc * 128:(mc + 1) * 128],
                        A["xnq"][:, mc * 1024 + th * 512: mc * 1024 + (th + 1) * 512],
                        start=(mc == 0), stop=(mc == NMC - 1)), inc="pe")
            P.wait("vector", "pe", qproj_done[h])
            if h >= 2:
                P.wait("vector", "pe", scores_prev[h % 2])
            qe_done[h] = P.op("vector", lambda e: e.tensor_copy(A["qraw"][:], P45[:]), inc="dv")
            P.wait("sync", "dv", qe_done[h])
            P.dma("sync", A["qshift"][0:64, :], A["qraw"][64:128, :], "sh")
            qsh = P.dma("sync", A["qshift"][64:128, :], A["qraw"][0:64, :], "sh")
            P.wait("vector", "sh", qsh)
            P.wait("vector", "sy", consts_sy)
            P.op("vector", lambda e: e.tensor_mul(A["qtmp"][:], A["qraw"][:], A["cosq"][:]))
            P.op("vector", lambda e: e.tensor_mul(A["qshift"][:], A["qshift"][:], A["sinq"][:]))
            qrope_done[h] = P.op("vector", lambda e, qb=qb: e.tensor_add(
                qb[:], A["qtmp"][:], A["qshift"][:]), inc="dv")

        kv_pe_done = P.counts["pe"]
        P.wait("gpsimd", "pe", kv_pe_done)
        P.wait("gpsimd", "dv", kv_done_dv)
        P.wait("gpsimd", "sy", P.counts["sy"])
        waq_ld[0] = P.dma("gpsimd", A["waq0"][:], waq[0], "gq")
        P.wait("tensor", "dv", v_done_dv)
        P.wait("tensor", "dv", rows_done[NSLAB - 1])
        emit_qproj_rope(0)
        waq_ld[1] = P.dma("gpsimd", A["waq1"][:], waq[1], "gq")

        for h in range(NH):
            g = h // 4
            qb = A["qt%d" % (h % 2)]
            for jg in range(2):
                nk = 8 * (jg + 1)
                slots = [(P01, "P01", 0), (P01, "P01", 1),
                         (P45, "P45", 0), (P45, "P45", 1)] if jg == 0 else \
                        [(P01, "P01", 0), (P01, "P01", 1),
                         (P23, "P23", 0), (P23, "P23", 1)]
                P.wait("tensor", "dv", qrope_done[h])
                P.wait("tensor", "dv", kv_done_dv)
                mask_dv = 0
                smm_last = 0
                for kap in range(0, nk, 2):
                    Pp, _, rp, moff, _ = slot_geom(jg, kap)
                    base = 2 * (Pp & 1)
                    ps, psname, _ = slots[base]
                    smm = 0
                    for sg in range(2):
                        _, _, _, _, lch = slot_geom(jg, kap + sg)
                        hf = slots[base + sg][2]
                        P.wait("tensor", "ac", slot_ac.get((psname, hf), 0))
                        if psname == "P23":
                            P.wait("tensor", "dv", ot_war[hf])
                        smm = P.op("tensor", lambda e, g=g, lch=lch, ps=ps, hf=hf, moff=moff, jg=jg, qb=qb: e.matmul(
                            ps[:, hf * 512 + moff: (hf + 1) * 512],
                            A["kt"][:, g * 2048 + lch * 128: g * 2048 + (lch + 1) * 128],
                            qb[:, jg * 512 + moff: (jg + 1) * 512],
                            start=True, stop=True), inc="pe")
                    smm_last = smm
                    P.wait("scalar", "pe", smm)
                    P.wait("scalar", "pe", pt_war)
                    src = ps[:].rearrange("p (two w) -> p two w", two=2)[:, :, moff:512]
                    dst = A["pt"][:, kap * 512:(kap + 2) * 512].rearrange(
                        "p (two w) -> p two w", two=2)[:, :, moff:512]
                    eac = P.op("scalar", lambda e, src=src, dst=dst: e.activation(
                        dst, src, AF.Exp, bias=A["bias20"][:], scale=QKSCALE), inc="ac")
                    slot_ac[(psname, 0)] = slot_ac[(psname, 1)] = eac
                    exp_all = eac
                    if 0 <= rp <= 3:
                        P.wait("vector", "ac", eac)
                        P.wait("vector", "sy", consts_sy)
                        for sg in range(2):
                            kp = kap + sg
                            if sg == 0:
                                mask_dv = P.op("vector", lambda e, kp=kp, rp=rp: e.tensor_mul(
                                    A["pt"][:, kp * 512 + rp * 128: kp * 512 + (rp + 1) * 128],
                                    A["pt"][:, kp * 512 + rp * 128: kp * 512 + (rp + 1) * 128],
                                    A["tri"][:]), inc="dv")
                            else:
                                mask_dv = P.op("vector", lambda e, kp=kp, rp=rp: e.tensor_scalar_mul(
                                    A["pt"][:, kp * 512 + rp * 128: kp * 512 + (rp + 1) * 128],
                                    A["pt"][:, kp * 512 + rp * 128: kp * 512 + (rp + 1) * 128],
                                    A["sig1"][:]), inc="dv")

                scores_prev[h % 2] = smm_last
                P.wait("tensor", "ac", exp_all)
                if mask_dv:
                    P.wait("tensor", "dv", mask_dv)
                lh = jg
                P.wait("tensor", "ac", slot_ac.get(("P01", lh), 0))
                lmm = 0
                for kap in range(nk):
                    _, _, _, moff, _ = slot_geom(jg, kap)
                    lmm = P.op("tensor", lambda e, kap=kap, moff=moff, lh=lh: e.matmul(
                        P01[:, lh * 512 + moff: (lh + 1) * 512], A["ones_sq"][:],
                        A["pt"][:, kap * 512 + moff: (kap + 1) * 512],
                        start=(kap == 0), stop=(kap == nk - 1)), inc="pe")
                oh = jg
                P.wait("tensor", "dv", ot_war[oh])
                pvm = 0
                for kap in range(nk):
                    _, _, _, moff, lch = slot_geom(jg, kap)
                    pvm = P.op("tensor", lambda e, kap=kap, moff=moff, lch=lch, g=g, oh=oh: e.matmul(
                        P23[:, oh * 512 + moff: (oh + 1) * 512],
                        A["vsb"][:, lch * 512 + g * 128: lch * 512 + (g + 1) * 128],
                        A["pt"][:, kap * 512 + moff: (kap + 1) * 512],
                        start=(kap == 0), stop=(kap == nk - 1)), inc="pe")
                pt_war = pvm
                pv_last[jg] = pvm
                P.wait("scalar", "pe", lmm)
                P.wait("scalar", "dv", ot_dv)
                P.op("scalar", lambda e, lh=lh: e.activation(
                    A["reca"][:], P01[:, lh * 512:(lh + 1) * 512], AF.Ln, bias=0.0, scale=1.0))
                rc = P.op("scalar", lambda e: e.activation(
                    A["recb"][:], A["reca"][:], AF.Exp, scale=-1.0), inc="ac")
                slot_ac[("P01", lh)] = rc
                P.wait("vector", "pe", pvm)
                P.wait("vector", "ac", rc)
                ot_dv = P.op("vector", lambda e, h=h, jg=jg, oh=oh: e.tensor_tensor(
                    A["ot"][:, h * 1024 + jg * 512: h * 1024 + (jg + 1) * 512],
                    P23[:, oh * 512:(oh + 1) * 512], A["recb"][:], ALU.mult), inc="dv")
                ot_war[oh] = ot_dv
                for _ in range(3):
                    b_half()
                if jg == 0 and h + 1 < NH:
                    emit_qproj_rope(h + 1)
                    if h + 2 < NH:
                        P.wait("gpsimd", "pe", qproj_done[h])
                        waq_ld[h + 2] = P.dma("gpsimd", A["waq%d" % ((h + 2) % 2)][:],
                                              waq[h + 2], "gq")

        attn_ot_dv = ot_dv
        attn_pe_done = pv_last[1]
        attn_ac_done = P.counts["ac"]
        if "qt0" in dump_aps:
            P.wait("sync", "dv", qrope_done[0])
            P.dma("sync", dump_aps["qt0"], A["qt0"][:], "sy")
        if "pt" in dump_aps:
            P.wait("sync", "pe", pt_war)
            P.dma("sync", dump_aps["pt"], A["pt"][:], "sy")
        if "ot" in dump_aps:
            P.wait("sync", "dv", attn_ot_dv)
            P.dma("sync", dump_aps["ot"], A["ot"][:], "sy")

        while b_st["k"] < 128:
            b_half()

        P.wait("gpsimd", "dv", qrope_done[NH - 1])
        P.wait("gpsimd", "dv", attn_ot_dv)
        P.wait("gpsimd", "pe", attn_pe_done)
        wao_mm = {}
        wao_ld = [0] * 4
        rt_ev = {}
        rt_store = [0] * NMC
        for mg in range(4):
            buf = A["wao%d" % (mg % 2)]
            if mg >= 2:
                P.wait("gpsimd", "pe", wao_mm[(mg - 2, NH - 1)])
            wao_ld[mg] = P.dma("gpsimd", buf[:], wao[mg], "gq")
            for h in range(NH):
                P.wait("tensor", "gq", wao_ld[mg])
                if h == 0:
                    P.wait("tensor", "ac", attn_ac_done)
                    P.wait("tensor", "dv", attn_ot_dv)
                    P.wait("tensor", "dv", qe_done[NH - 1])
                    if p67_war["eng"] is not None:
                        P.wait("tensor", p67_war["eng"], p67_war["val"])
                    if mg > 0:
                        for mc4 in range(4):
                            P.wait("tensor", "dv", rt_ev[(mg - 1, mc4)])
                    P.wait("tensor", "dv", attn_ot_dv)
                for mc4 in range(4):
                    for th in range(2):
                        wao_mm[(mg, h)] = P.op("tensor", lambda e, mc4=mc4, th=th, h=h, buf=buf: e.matmul(
                            PS[mc4][:, th * 512:(th + 1) * 512],
                            buf[:, h * 512 + mc4 * 128: h * 512 + (mc4 + 1) * 128],
                            A["ot"][:, h * 1024 + th * 512: h * 1024 + (th + 1) * 512],
                            start=(h == 0), stop=(h == NH - 1)), inc="pe")
            for mc4 in range(4):
                mc = mg * 4 + mc4
                rb = A["rts%d" % (mc4 % 2)]
                P.wait("vector", "pe", wao_mm[(mg, NH - 1)])
                if mc4 >= 2:
                    P.wait("vector", "sy", rt_store[mc - 2])
                rt_ev[(mg, mc4)] = P.op("vector", lambda e, mc4=mc4, rb=rb: e.tensor_copy(
                    rb[:], PS[mc4][:]), inc="dv")
                P.wait("sync", "dv", rt_ev[(mg, mc4)])
                rt_store[mc] = P.dma("sync", scrR[mc], rb[:], "sy")
            p67_war.update(eng="dv", val=rt_ev[(mg, 3)])
        rt_stores_done = rt_store[NMC - 1]
        p67_war.update(eng="dv", val=rt_ev[(3, 3)])

        load_act_table(tids["silu_and_others"])
        P.wait("gpsimd", "pe", attn_pe_done)
        P.wait("gpsimd", "pe", max(b_st["mm"].values()))
        P.wait("vector", "sy", rt_stores_done)
        f1_ld = [0] * NC_H
        f1_mm_g = {}
        f1_mm_l = {}
        f1_sil = {}
        f1_hid = {}
        f1_h2 = {}
        for c in range(NC_H):
            w13b = A["w13b%d" % (c % 3)]
            w1b = w13b[:, 0:2048]
            w3b = w13b[:, 2048:4096]
            if c >= 3:
                P.wait("gpsimd", "pe", f1_mm_l[c - 3])
            f1_ld[c] = P.dma("gpsimd", w13b[:], w13s[c], "gq")
            P.wait("tensor", "gq", f1_ld[c])
            if c == 0:
                P.wait("tensor", "ac", attn_ac_done)
                P.wait("tensor", "dv", rt_ev[(3, 0)])
                P.wait("tensor", "dv", rt_ev[(3, 1)])
            for th in range(2):
                if (c, th) != (0, 0):
                    pass
                prev_sil = f1_sil.get((c - 1, th), 0)
                P.wait("tensor", "ac", prev_sil)
                for mc in range(NMC):
                    f1_mm_g[(c, th)] = P.op("tensor", lambda e, th=th, mc=mc, w1b=w1b: e.matmul(
                        P01[:, th * 512:(th + 1) * 512],
                        w1b[:, mc * 128:(mc + 1) * 128],
                        A["xnq"][:, mc * 1024 + th * 512: mc * 1024 + (th + 1) * 512],
                        start=(mc == 0), stop=(mc == NMC - 1)), inc="pe")
            for th in range(2):
                prev_hid = f1_hid.get((c - 1, th), 0)
                P.wait("tensor", "dv", prev_hid)
                for mc in range(NMC):
                    f1_mm_l[c] = P.op("tensor", lambda e, th=th, mc=mc, w3b=w3b: e.matmul(
                        P23[:, th * 512:(th + 1) * 512],
                        w3b[:, mc * 128:(mc + 1) * 128],
                        A["xnq"][:, mc * 1024 + th * 512: mc * 1024 + (th + 1) * 512],
                        start=(mc == 0), stop=(mc == NMC - 1)), inc="pe")
                f1_mm_l.setdefault((c, th), f1_mm_l[c])
                f1_mm_l[(c, th)] = f1_mm_l[c]
            for th in range(2):
                P.wait("scalar", "pe", f1_mm_g[(c, th)])
                f1_sil[(c, th)] = P.op("scalar", lambda e, th=th: e.activation(
                    A["gsil"][:, th * 512:(th + 1) * 512], P01[:, th * 512:(th + 1) * 512],
                    AF.Silu, bias=0.0, scale=1.0), inc="ac")
                P.wait("vector", "ac", f1_sil[(c, th)])
                P.wait("vector", "pe", f1_mm_l[(c, th)])
                f1_hid[(c, th)] = P.op("vector", lambda e, c=c, th=th: e.tensor_tensor(
                    A["hidden"][:, c * 1024 + th * 512: c * 1024 + (th + 1) * 512],
                    A["gsil"][:, th * 512:(th + 1) * 512],
                    P23[:, th * 512:(th + 1) * 512], ALU.mult), inc="dv")
            P.wait("tensor", "dv", f1_hid[(c, 1)])
            if c >= 1:
                eng, ev = f1_h2[c - 1]
                P.wait("tensor", eng, ev)
            h2mm = 0
            for th in range(2):
                h2mm = P.op("tensor", lambda e, c=c, th=th: e.matmul(
                    P45[:, th * 512:(th + 1) * 512], A["h128"][:],
                    A["hidden"][:, c * 1024 + th * 512: c * 1024 + (th + 1) * 512],
                    start=True, stop=True), inc="pe")
            if c % 2 == 0:
                P.wait("scalar", "pe", h2mm)
                ev = P.op("scalar", lambda e, c=c: e.activation(
                    A["hidden"][:, c * 1024:(c + 1) * 1024], P45[:], AF.Copy,
                    bias=0.0, scale=1.0), inc="ac")
                f1_h2[c] = ("ac", ev)
            else:
                P.wait("vector", "pe", h2mm)
                ev = P.op("vector", lambda e, c=c: e.tensor_copy(
                    A["hidden"][:, c * 1024:(c + 1) * 1024], P45[:]), inc="dv")
                f1_h2[c] = ("dv", ev)
        hid_done = f1_hid[(NC_H - 1, 1)]
        f1_ac_done = P.counts["ac"]
        f1_dv_done = P.counts["dv"]
        f1_pe_done = P.counts["pe"]
        if "hid" in dump_aps:
            P.wait("sync", "dv", hid_done)
            P.dma("sync", dump_aps["hid"], A["hidden"][:], "sy")

        assert b_st["k"] >= 128, "B-step incomplete: %d" % b_st["k"]
        P.wait("gpsimd", "pe", f1_pe_done)
        P.wait("gpsimd", "dv", hid_done)
        P.wait("gpsimd", "sy", b_st["st"])
        P.wait("sync", "pe", f1_pe_done)
        P.wait("sync", "sy", b_st["st"])
        f2_ld = {}
        f2_mm = {}
        f2_sil = {}
        cmb_dv = {}
        out_st = [0] * NMC
        xc_ld = {}
        rt_ld = {}
        for mq in range(4):
            for c in range(NC_H):
                wqb = A["w2q%d" % (c % 8)]
                prev = f2_mm.get((mq, c - 8), f2_mm.get((mq - 1, NC_H - 8 + (c % 8)), 0))
                P.wait("sync", "pe", prev)
                f2_ld[(mq, c)] = P.dma("sync", wqb[:], scrB[mq, c], "sy")
                P.wait("tensor", "sy", f2_ld[(mq, c)])
                if c == 0:
                    P.wait("tensor", "dv", f1_dv_done)
                    P.wait("tensor", "ac", f1_ac_done)
                    if mq == 0:
                        if p67_war["eng"] is not None:
                            P.wait("tensor", p67_war["eng"], p67_war["val"])
                for mc4 in range(4):
                    if c == 0 and mq > 0:
                        P.wait("tensor", "ac", f2_sil[(mq - 1, mc4)])
                    for th in range(2):
                        f2_mm[(mq, c)] = P.op("tensor", lambda e, mc4=mc4, th=th, c=c, wqb=wqb: e.matmul(
                            PS[mc4][:, th * 512:(th + 1) * 512],
                            wqb[:, mc4 * 128:(mc4 + 1) * 128],
                            A["hidden"][:, c * 1024 + th * 512: c * 1024 + (th + 1) * 512],
                            start=(c == 0), stop=(c == NC_H - 1)), inc="pe")
            for mc4 in range(4):
                mc = mq * 4 + mc4
                P.wait("sync", "dv", cmb_dv.get(mc - 2, 0))
                xc_ld[mc] = P.dma("sync", A["xc%d" % (mc % 2)][:], xq[mc], "sy")
                P.wait("gpsimd", "sy", rt_stores_done)
                P.wait("gpsimd", "dv", cmb_dv.get(mc - 2, 0))
                rt_ld[mc] = P.dma("gpsimd", A["rt%d" % (mc % 2)][:], scrR[mc], "gq")
                P.wait("scalar", "pe", f2_mm[(mq, NC_H - 1)])
                P.wait("scalar", "dv", cmb_dv.get(mc - 1, 0))
                f2_sil[(mq, mc4)] = P.op("scalar", lambda e, mc4=mc4: e.activation(
                    A["gf"][:], PS[mc4][:], AF.Silu, bias=0.0, scale=1.0), inc="ac")
                P.wait("vector", "ac", f2_sil[(mq, mc4)])
                P.wait("vector", "gq", rt_ld[mc])
                P.wait("vector", "sy", xc_ld[mc])
                P.wait("vector", "sy", b_st["st"])
                if mc >= 2:
                    P.wait("vector", "sy", out_st[mc - 2])
                P.op("vector", lambda e, mc=mc: e.tensor_mul(
                    A["gf"][:], A["gf"][:], A["rt%d" % (mc % 2)][:]))
                cmb_dv[mc] = P.op("vector", lambda e, mc=mc: e.tensor_add(
                    A["ou%d" % (mc % 2)][:], A["gf"][:], A["xc%d" % (mc % 2)][:]), inc="dv")
                P.wait("sync", "dv", cmb_dv[mc])
                out_st[mc] = P.dma("sync", out_t[mc], A["ou%d" % (mc % 2)][:], "sy")
        if "rt" in dump_aps:
            P.wait("sync", "sy", rt_stores_done)
            for mc in range(NMC):
                P.dma("sync", dump_aps["rt"][mc], scrR[mc], "sy")

        P.wait("sync", "sy", P.counts["sy"])
        P.run(block)
    return nc



def _rope_tables():
    pos = np.arange(T, dtype=np.float32)[:, None]
    freqs = np.power(np.float32(10000.0),
                     -(np.arange(64, dtype=np.float32) / np.float32(64.0)))[None, :].astype(np.float32)
    ang = (pos * freqs).astype(np.float32)
    cos = np.cos(ang.astype(np.float64)).astype(np.float32)
    sin = np.sin(ang.astype(np.float64)).astype(np.float32)
    ck = np.empty((128, T), np.float32)
    sk = np.empty((128, T), np.float32)
    ck[0:64] = cos.T
    ck[64:128] = cos.T
    sk[0:64] = -sin.T
    sk[64:128] = sin.T
    return ck, sk


def _local_perm(par):
    idx = []
    for s in range(NSLAB):
        us = [par, 2 + par, 1 - par, 3 - par]
        for u in us:
            c0 = (s * 4 + u) * 128
            idx.extend(range(c0, c0 + 128))
    return np.array(idx, np.int64)


def prepare_in_maps(L0_attn_x, L0_attn_w_aq, L0_attn_w_ak, L0_attn_w_av, L0_attn_w_ao,
                    L0_ffn_W1, L0_ffn_W2, L0_ffn_W3):
    x = np.ascontiguousarray(np.asarray(L0_attn_x, dtype=np.float32))
    w_aq = np.asarray(L0_attn_w_aq, np.float32)
    w_ak = np.asarray(L0_attn_w_ak, np.float32)
    w_av = np.asarray(L0_attn_w_av, np.float32)
    w_ao = np.asarray(L0_attn_w_ao, np.float32)
    W1 = np.asarray(L0_ffn_W1, np.float32)
    W2 = np.asarray(L0_ffn_W2, np.float32)
    W3 = np.asarray(L0_ffn_W3, np.float32)

    waq_h = np.ascontiguousarray(
        w_aq.reshape(NH, 16, 128, 128).transpose(0, 2, 1, 3).reshape(NH, 128, 2048))
    wak_h = np.ascontiguousarray(
        w_ak.reshape(G, 16, 128, 128).transpose(2, 0, 1, 3).reshape(128, 8192))
    wav_h = np.ascontiguousarray(
        w_av.reshape(G, 16, 128, 128).transpose(2, 1, 0, 3).reshape(128, 8192))
    wao_h = np.ascontiguousarray(
        w_ao.reshape(NH, 128, 4, 512).transpose(2, 1, 0, 3).reshape(4, 128, 8192))
    w1_h = W1.reshape(NC_H, 128, NMC, 128).transpose(0, 3, 2, 1).reshape(NC_H, 128, 2048)
    w3_h = W3.reshape(NC_H, 128, NMC, 128).transpose(0, 3, 2, 1).reshape(NC_H, 128, 2048)
    w13_h = np.ascontiguousarray(
        np.stack([w1_h, w3_h], axis=2).reshape(NC_H, 128, 4096))
    w2t_h = np.ascontiguousarray(W2.T)

    ck, sk = _rope_tables()

    def sylvester(n):
        m = np.array([[1]], np.int64)
        while m.shape[0] < n:
            m = np.block([[m, m], [m, -m]])
        return m

    h128_h = sylvester(128).astype(ml_dtypes.bfloat16)
    h64 = sylvester(64)
    h64d_h = np.zeros((128, 128), np.int64)
    h64d_h[0:64, 0:64] = h64
    h64d_h[64:128, 64:128] = h64
    h64d_h = h64d_h.astype(ml_dtypes.bfloat16)
    tri_h = np.triu(np.ones((128, 128), np.float32)).astype(ml_dtypes.bfloat16)

    in_maps = []
    perms = {}
    for core in range(8):
        b, par = core // 2, core % 2
        perm = _local_perm(par)
        perms[core] = perm
        xl = x[b][perm]
        xs_h = np.ascontiguousarray(
            xl.reshape(NSLAB, 512, NMC, 128).transpose(0, 3, 2, 1))
        qsel_l = perm.reshape(NSLAB, 4, 128)[:, 0:2].reshape(-1)
        xq_h = np.ascontiguousarray(
            x[b][qsel_l].reshape(1024, NMC, 128).transpose(1, 2, 0))
        ckl = np.ascontiguousarray(ck[:, perm])
        skl = np.ascontiguousarray(sk[:, perm])
        qsel = perm.reshape(NSLAB, 4, 128)[:, 0:2].reshape(-1)
        cql = np.ascontiguousarray(ck[:, qsel])
        sql = np.ascontiguousarray(sk[:, qsel])
        sig1_h = np.full((128, 1), 1.0 if par == 1 else 0.0, np.float32)
        in_maps.append({
            "xs": xs_h, "xq": xq_h, "waq": waq_h, "wak": wak_h, "wav": wav_h, "wao": wao_h,
            "w13s": w13_h, "w2t": w2t_h,
            "cosk": ckl, "sink": skl, "cosq": cql, "sinq": sql,
            "tri": tri_h, "sig1": sig1_h, "h128": h128_h, "h64d": h64d_h,
        })
    return in_maps, perms


def kernel(L0_attn_x, L0_attn_w_aq, L0_attn_w_ak, L0_attn_w_av, L0_attn_w_ao,
           L0_ffn_W1, L0_ffn_W2, L0_ffn_W3):
    in_maps, perms = prepare_in_maps(
        L0_attn_x, L0_attn_w_aq, L0_attn_w_ak, L0_attn_w_av, L0_attn_w_ao,
        L0_ffn_W1, L0_ffn_W2, L0_ffn_W3)
    nc = build_nc()
    res = run_bass_kernel_spmd(nc, in_maps, core_ids=list(range(8)))

    out = np.empty((B, T, M), np.float32)
    for core in range(8):
        b = core // 2
        oc = res.results[core]["out"]
        ov = oc.reshape(NMC, 128, 8, 128).transpose(2, 3, 0, 1).reshape(1024, M)
        qsel = perms[core].reshape(NSLAB, 4, 128)[:, 0:2].reshape(-1)
        out[b][qsel] = ov
    return out


if __name__ == "__main__":
    pass


# version 11
# speedup vs baseline: 1.1336x; 1.0139x over previous
import math

import numpy as np
import ml_dtypes

import concourse.bass as bass
import concourse.mybir as mybir
from concourse.bass_utils import run_bass_kernel_spmd

F32 = mybir.dt.float32
BF16 = mybir.dt.bfloat16
AF = mybir.ActivationFunctionType
ALU = mybir.AluOpType

B, T, M, D, G, NH = 4, 2048, 2048, 128, 4, 16
HF = 8192
NMC = 16
NC_H = 64
NSLAB = 4
KB = 1024

QKSCALE = float(128.0 ** -0.5)
S8192 = float(1.0 / math.sqrt(8192.0))
EXP_SHIFT = -20.0
EPS = 1e-5

ENGS = ("tensor", "scalar", "vector", "sync", "gpsimd")

_TABLE_IDS = None


def _table_ids(nc):
    global _TABLE_IDS
    if _TABLE_IDS is None:
        from concourse.hw_specs import get_activation_tables
        names = list(get_activation_tables(nc.m.arch).keys())
        _TABLE_IDS = {n: i for i, n in enumerate(names)}
    return _TABLE_IDS


class Plan:
    def __init__(self, nc):
        self.nc = nc
        self.items = {e: [] for e in ENGS}
        self.sem_handles = {}
        self.counts = {}
        self.watermark = {}

    def add_sem(self, name, handle):
        self.sem_handles[name] = handle
        self.counts[name] = 0

    def wait(self, eng, sem, val=None):
        if val is None:
            val = self.counts[sem]
        assert val <= self.counts[sem], (eng, sem, val, self.counts[sem])
        if val <= 0:
            return
        key = (eng, sem)
        if self.watermark.get(key, -1) >= val:
            return
        self.watermark[key] = val
        h = self.sem_handles[sem]
        self.items[eng].append(lambda e, h=h, v=val: e.wait_ge(h, v))

    def op(self, eng, fn, inc=None, inc_n=1):
        if inc is not None:
            self.counts[inc] += inc_n
            h = self.sem_handles[inc]

            def f(e, fn=fn, h=h, n=inc_n):
                fn(e).then_inc(h, n)
        else:
            def f(e, fn=fn):
                fn(e)
        self.items[eng].append(f)
        return self.counts[inc] if inc is not None else None

    def dma(self, eng, out, in_, sem):
        return self.op(eng, lambda e, o=out, i=in_: e.dma_start(out=o, in_=i),
                       inc=sem, inc_n=16)

    def run(self, block):
        block.tensor(lambda e: [f(e) for f in self.items["tensor"]])
        block.scalar(lambda e: [f(e) for f in self.items["scalar"]])
        block.vector(lambda e: [f(e) for f in self.items["vector"]])
        block.sync(lambda e: [f(e) for f in self.items["sync"]])
        block.gpsimd(lambda e: [f(e) for f in self.items["gpsimd"]])


def build_nc(dumps=()):
    nc = bass.Bass("TRN2", target_bir_lowering=False, detect_race_conditions=False)

    xs = nc.dram_tensor("xs", [NSLAB, 128, NMC, 512], F32, kind="ExternalInput").ap()
    xq = nc.dram_tensor("xq", [NMC, 128, 1024], F32, kind="ExternalInput").ap()
    waq = nc.dram_tensor("waq", [NH, 128, 2048], F32, kind="ExternalInput").ap()
    wak = nc.dram_tensor("wak", [128, 8192], F32, kind="ExternalInput").ap()
    wav = nc.dram_tensor("wav", [128, 8192], F32, kind="ExternalInput").ap()
    wao = nc.dram_tensor("wao", [4, 128, 8192], F32, kind="ExternalInput").ap()
    w13s = nc.dram_tensor("w13s", [NC_H, 128, 4096], F32, kind="ExternalInput").ap()
    w2t = nc.dram_tensor("w2t", [HF, 2048], F32, kind="ExternalInput").ap()
    cosk = nc.dram_tensor("cosk", [128, 2048], F32, kind="ExternalInput").ap()
    sink = nc.dram_tensor("sink", [128, 2048], F32, kind="ExternalInput").ap()
    cosq = nc.dram_tensor("cosq", [128, 1024], F32, kind="ExternalInput").ap()
    sinq = nc.dram_tensor("sinq", [128, 1024], F32, kind="ExternalInput").ap()
    tri_i = nc.dram_tensor("tri", [128, 128], BF16, kind="ExternalInput").ap()
    sig1_i = nc.dram_tensor("sig1", [128, 1], F32, kind="ExternalInput").ap()
    h128_i = nc.dram_tensor("h128", [128, 128], BF16, kind="ExternalInput").ap()
    h64d_i = nc.dram_tensor("h64d", [128, 128], BF16, kind="ExternalInput").ap()

    out_t = nc.dram_tensor("out", [NMC, 128, 1024], F32, kind="ExternalOutput").ap()
    dump_aps = {}
    for nm in dumps:
        shapes = {"xn": [128, 32768], "kt": [128, 8192], "vsb": [128, 8192],
                  "ot": [128, 16384], "hid": [128, 65536], "pt": [128, 8192],
                  "qt0": [128, 1024], "rt": [NMC, 128, 1024], "xnq": [128, 16384]}
        dt = BF16
        dump_aps[nm] = nc.dram_tensor("dump_" + nm, shapes[nm], dt, kind="ExternalOutput").ap()

    scrB = nc.dram_tensor("scrB", [4, 64, 128, 512], BF16, kind="Internal").ap()
    scrR = nc.dram_tensor("scrR", [NMC, 128, 1024], BF16, kind="Internal").ap()

    A = {}

    SB_BASE = 16512

    def sb(name, shape, dtype, off_kb):
        A[name] = nc.alloc_sbuf_tensor_at(name, shape, dtype,
                                          offset=SB_BASE + int(off_kb * KB))
        return A[name]

    sb("ones_sq", [128, 128], BF16, 0.0)
    sb("bias20", [128, 1], F32, 0.25)
    sb("epsb", [128, 1], F32, 0.28125)
    sb("sig1", [128, 1], F32, 0.3125)
    sb("tri", [128, 128], BF16, 0.5)
    sb("h128", [128, 128], BF16, 0.75)
    sb("h64d", [128, 128], BF16, 1.0)
    sb("cosq", [128, 1024], F32, 1.25)
    sb("sinq", [128, 1024], F32, 5.25)
    sb("xnq", [128, NMC * 1024], BF16, 9.25)
    XN = 41.5
    sb("xn", [128, NMC * 2048], BF16, XN)
    sb("hidden", [128, NC_H * 1024], BF16, XN)
    sb("qraw", [128, 1024], F32, XN)
    sb("qshift", [128, 1024], F32, XN + 4)
    sb("qtmp", [128, 1024], F32, XN + 8)
    sb("waq0", [128, 2048], BF16, XN + 12)
    sb("waq1", [128, 2048], BF16, XN + 16)
    sb("reca", [128, 512], F32, XN + 20)
    sb("recb", [128, 512], F32, XN + 22)
    sb("wao0", [128, 8192], BF16, XN)
    sb("wao1", [128, 8192], BF16, XN + 16)
    sb("rts0", [128, 1024], BF16, XN + 32)
    sb("rts1", [128, 1024], BF16, XN + 34)
    sb("xslab", [128, NMC * 512], F32, 105.5)
    sb("xb", [128, NMC * 512], BF16, 9.25)
    sb("xsq", [128, NMC * 512], BF16, 25.25)
    sb("ln_mean", [128, 512], F32, 137.5)
    sb("ln_t1", [128, 512], F32, 139.5)
    sb("ln_t2", [128, 512], F32, 141.5)
    sb("kraw0", [128, 1024], F32, 105.5)
    sb("kraw1", [128, 1024], F32, 109.5)
    sb("kshift0", [128, 1024], F32, 113.5)
    sb("kshift1", [128, 1024], F32, 149.5)
    sb("cosk", [128, 2048], F32, 117.5)
    sb("sink", [128, 2048], F32, 125.5)
    sb("wakb", [128, 8192], BF16, 133.5)
    sb("wavb", [128, 8192], BF16, 149.5)
    sb("pt", [128, 16 * 512], BF16, 105.5)
    sb("ot", [128, NH * 1024], BF16, 121.5)
    sb("qt0", [128, 1024], BF16, 153.5)
    sb("qt1", [128, 1024], BF16, 155.5)
    sb("kt", [128, G * 2048], BF16, 157.5)
    sb("vsb", [128, 16 * 512], BF16, 173.5)
    sb("bin0", [128, 2048], BF16, 189.5)
    sb("bin1", [128, 2048], BF16, 193.5)
    sb("bout0", [128, 2048], BF16, 197.5)
    sb("bout1", [128, 2048], BF16, 201.5)
    sb("w13b0", [128, 4096], BF16, 169.5)
    sb("w13b1", [128, 4096], BF16, 177.5)
    sb("w13b2", [128, 4096], BF16, 189.5)
    sb("gsil", [128, 1024], F32, 185.5)
    for _i in range(8):
        sb("w2q%d" % _i, [128, 512], BF16, 169.5 + _i)
    sb("gf", [128, 1024], F32, 177.5)
    sb("xc0", [128, 1024], F32, 181.5)
    sb("xc1", [128, 1024], F32, 185.5)
    sb("rt0", [128, 1024], F32, 189.5)
    sb("rt1", [128, 1024], F32, 193.5)
    sb("ou0", [128, 1024], F32, 197.5)
    sb("ou1", [128, 1024], F32, 201.5)

    P01 = nc.alloc_psum_tensor("P01", [128, 1024], F32)
    P23 = nc.alloc_psum_tensor("P23", [128, 1024], F32)
    P45 = nc.alloc_psum_tensor("P45", [128, 1024], F32)
    P67 = nc.alloc_psum_tensor("P67", [128, 1024], F32)
    PS = [P01, P23, P45, P67]

    with (
        nc.semaphore("s_pe") as s_pe,
        nc.semaphore("s_ac") as s_ac,
        nc.semaphore("s_dv") as s_dv,
        nc.semaphore("s_sy") as s_sy,
        nc.semaphore("s_gq") as s_gq,
        nc.semaphore("s_sh") as s_sh,
        nc.semaphore("s_sh2") as s_sh2,
        nc.semaphore("s_bl") as s_bl,
        nc.semaphore("s_bl2") as s_bl2,
        nc.semaphore("s_os") as s_os,
        nc.semaphore("s_bs") as s_bs,
        nc.semaphore("s_rs") as s_rs,
        nc.Block() as block,
    ):
        P = Plan(nc)
        for nm, h in (("pe", s_pe), ("ac", s_ac), ("dv", s_dv), ("sy", s_sy),
                      ("gq", s_gq), ("sh", s_sh), ("sh2", s_sh2),
                      ("bl", s_bl), ("bl2", s_bl2),
                      ("os", s_os), ("bs", s_bs), ("rs", s_rs)):
            P.add_sem(nm, h)

        tids = _table_ids(nc)

        def load_act_table(tid):
            P.op("scalar", lambda e, tid=tid: e.add_instruction(mybir.InstLoadActFuncSet(
                name=nc.get_next_instruction_name(), act_func_set_id=tid, ins=[], outs=[])))

        P.op("vector", lambda e: e.memset(A["ones_sq"][:], 1.0))
        P.op("vector", lambda e: e.memset(A["bias20"][:], EXP_SHIFT))
        consts_dv = P.op("vector", lambda e: e.memset(A["epsb"][:], EPS), inc="dv")
        P.dma("sync", A["tri"][:], tri_i, "sy")
        P.dma("sync", A["sig1"][:], sig1_i, "sy")
        P.dma("sync", A["h128"][:], h128_i, "sy")
        P.dma("sync", A["h64d"][:], h64d_i, "sy")
        P.dma("sync", A["cosq"][:], cosq, "sy")
        P.dma("sync", A["sinq"][:], sinq, "sy")
        consts_sy = P.counts["sy"]
        load_act_table(tids["natural_log_exp_and_others"])

        p67_war = {"eng": None, "val": 0}
        b_st = {"k": 0, "ld": {}, "mm": {}, "stores": {}, "st": 0}

        def b_load(pp):
            if pp > 63 or pp in b_st["ld"]:
                return
            if pp >= 2:
                P.wait("gpsimd", "pe", b_st["mm"].get(2 * (pp - 2) + 1, 0))
            bi = A["bin%d" % (pp % 2)]
            l1 = P.dma("gpsimd", bi[0:64, :],
                  w2t.rearrange("(c p) m -> c p m", p=128)[:, 2 * pp, :], "bl")
            l2 = P.dma("gpsimd", bi[64:128, :],
                  w2t.rearrange("(c p) m -> c p m", p=128)[:, 2 * pp + 1, :], "bl2")
            b_st["ld"][pp] = (l1, l2)

        def b_half():
            k = b_st["k"]
            if k >= 128:
                return
            pp, u = k >> 1, k & 1
            bi = A["bin%d" % (pp % 2)]
            bo = A["bout%d" % (pp % 2)]
            b_load(pp)
            b_load(pp + 1)
            P.wait("tensor", "sy", consts_sy)
            P.wait("tensor", "bl", b_st["ld"][pp][0])
            P.wait("tensor", "bl2", b_st["ld"][pp][1])
            if p67_war["eng"] is not None:
                P.wait("tensor", p67_war["eng"], p67_war["val"])
            for j in range(2):
                mm = P.op("tensor", lambda e, u=u, j=j, bi=bi: e.matmul(
                    P67[:, j * 512:(j + 1) * 512], A["h64d"][:],
                    bi[:, u * 1024 + j * 512: u * 1024 + (j + 1) * 512],
                    start=True, stop=True), inc="pe")
            b_st["mm"][k] = mm
            buf_war = b_st["stores"].get(k - 4, 0)
            if k % 2 == 0:
                P.wait("scalar", "pe", mm)
                P.wait("scalar", "bs", buf_war)
                ev = P.op("scalar", lambda e, bo=bo, u=u: e.activation(
                    bo[:, u * 1024:(u + 1) * 1024], P67[:], AF.Copy,
                    bias=0.0, scale=S8192), inc="ac")
                p67_war.update(eng="ac", val=ev)
                ev_eng = "ac"
            else:
                P.wait("vector", "pe", mm)
                P.wait("vector", "bs", buf_war)
                ev = P.op("vector", lambda e, bo=bo, u=u: e.tensor_scalar_mul(
                    bo[:, u * 1024:(u + 1) * 1024], P67[:], S8192), inc="dv")
                p67_war.update(eng="dv", val=ev)
                ev_eng = "dv"
            P.wait("sync", ev_eng, ev)
            for j in range(2):
                dst = scrB[2 * u:2 * u + 2, :, 2 * pp + j, :].rearrange("q c m -> c q m")
                b_st["st"] = P.dma("sync", dst,
                                   bo[j * 64:(j + 1) * 64, u * 1024:(u + 1) * 1024], "bs")
            b_st["stores"][k] = b_st["st"]
            b_st["k"] = k + 1

        xslab_v = A["xslab"][:].rearrange("p (mc t) -> p mc t", mc=NMC)
        xld = [0] * NSLAB
        apply_done = [0] * NSLAB
        stats_done = [0] * NSLAB
        rows_done = [0] * NSLAB

        xbld = [0] * NSLAB
        v_ev = [0] * 16
        ld_wav = 0
        for s in range(NSLAB):
            if s > 0:
                P.wait("sync", "dv", apply_done[s - 1])
                P.wait("gpsimd", "pe", stats_done[s - 1])
            P.dma("sync", xslab_v[:, 0:8, :], xs[s][:, 0:8, :], "sy")
            xld[s] = P.dma("sync", xslab_v[:, 8:16, :], xs[s][:, 8:16, :], "sy")
            if s == 0:
                ld_wav = P.dma("gpsimd", A["wavb"][:], wav, "gq")
            P.wait("scalar", "sy", xld[s] - 16)
            if s > 0:
                P.wait("scalar", "pe", stats_done[s - 1])
            P.op("scalar", lambda e: e.activation(
                A["xb"][:, 0:4096], A["xslab"][:, 0:4096], AF.Copy, bias=0.0, scale=1.0), inc="ac")
            P.op("scalar", lambda e: e.activation(
                A["xsq"][:, 0:4096], A["xslab"][:, 0:4096], AF.Square, bias=0.0, scale=1.0), inc="ac")
            P.wait("scalar", "sy", xld[s])
            P.op("scalar", lambda e: e.activation(
                A["xb"][:, 4096:8192], A["xslab"][:, 4096:8192], AF.Copy, bias=0.0, scale=1.0), inc="ac")
            sq_ac = P.op("scalar", lambda e: e.activation(
                A["xsq"][:, 4096:8192], A["xslab"][:, 4096:8192], AF.Square, bias=0.0, scale=1.0), inc="ac")
            P.wait("tensor", "ac", sq_ac)
            P.wait("tensor", "dv", consts_dv)
            if s > 0:
                P.wait("tensor", "dv", rows_done[s - 1])
            for mc in range(NMC):
                P.op("tensor", lambda e, mc=mc: e.matmul(
                    P01[:, 0:512], A["ones_sq"][:], A["xb"][:, mc * 512:(mc + 1) * 512],
                    start=(mc == 0), stop=(mc == NMC - 1)), inc="pe")
            for mc in range(NMC):
                stats_done[s] = P.op("tensor", lambda e, mc=mc: e.matmul(
                    P01[:, 512:1024], A["ones_sq"][:], A["xsq"][:, mc * 512:(mc + 1) * 512],
                    start=(mc == 0), stop=(mc == NMC - 1)), inc="pe")
            P.wait("vector", "pe", stats_done[s])
            P.op("vector", lambda e: e.tensor_scalar_mul(A["ln_mean"][:], P01[:, 0:512], 1.0 / 2048.0))
            P.op("vector", lambda e: e.tensor_scalar_mul(A["ln_t1"][:], P01[:, 512:1024], 1.0 / 2048.0))
            P.op("vector", lambda e: e.tensor_mul(A["ln_t2"][:], A["ln_mean"][:], A["ln_mean"][:]))
            rows_done[s] = P.op("vector", lambda e: e.tensor_sub(
                A["ln_t1"][:], A["ln_t1"][:], A["ln_t2"][:]), inc="dv")
            P.wait("scalar", "dv", rows_done[s])
            P.op("scalar", lambda e: e.activation(
                A["ln_t2"][:], A["ln_t1"][:], AF.Ln, bias=A["epsb"][:], scale=1.0))
            ac_rstd = P.op("scalar", lambda e: e.activation(
                A["ln_t1"][:], A["ln_t2"][:], AF.Exp, scale=-0.5), inc="ac")
            P.wait("vector", "ac", ac_rstd)
            P.op("vector", lambda e: e.tensor_mul(A["ln_t2"][:], A["ln_mean"][:], A["ln_t1"][:]))
            for mc in range(NMC):
                P.op("vector", lambda e, mc=mc: e.tensor_mul(
                    A["xslab"][:, mc * 512:(mc + 1) * 512],
                    A["xslab"][:, mc * 512:(mc + 1) * 512], A["ln_t1"][:]))
                apply_done[s] = P.op("vector", lambda e, mc=mc, s=s: e.tensor_sub(
                    A["xn"][:, mc * 2048 + s * 512: mc * 2048 + (s + 1) * 512],
                    A["xslab"][:, mc * 512:(mc + 1) * 512], A["ln_t2"][:]), inc="dv")
            P.wait("tensor", "gq", ld_wav)
            P.wait("tensor", "dv", apply_done[s])
            for tc in range(4 * s, 4 * s + 4):
                half = tc % 2
                if tc >= 2:
                    P.wait("tensor", "dv", v_ev[tc - 2])
                for mc in range(NMC):
                    vmm = P.op("tensor", lambda e, tc=tc, mc=mc, half=half: e.matmul(
                        P23[:, half * 512:(half + 1) * 512],
                        A["xn"][:, mc * 2048 + tc * 128: mc * 2048 + (tc + 1) * 128],
                        A["wavb"][:, mc * 512:(mc + 1) * 512],
                        start=(mc == 0), stop=(mc == NMC - 1)), inc="pe")
                P.wait("vector", "pe", vmm)
                v_ev[tc] = P.op("vector", lambda e, tc=tc, half=half: e.tensor_copy(
                    A["vsb"][:, tc * 512:(tc + 1) * 512],
                    P23[:, half * 512:(half + 1) * 512]), inc="dv")
            for _ in range(4):
                b_half()
        xn_done = apply_done[NSLAB - 1]
        v_done_dv = v_ev[15]

        P.wait("vector", "pe", stats_done[NSLAB - 1])
        xnq_done = 0
        for mc in range(NMC):
            src = A["xn"][:, mc * 2048:(mc + 1) * 2048].rearrange(
                "p (s t) -> p s t", s=NSLAB)[:, :, 0:256]
            xnq_done = P.op("vector", lambda e, mc=mc, src=src: e.tensor_copy(
                A["xnq"][:, mc * 1024:(mc + 1) * 1024], src), inc="dv")
        if "xn" in dump_aps:
            P.wait("sync", "dv", xn_done)
            P.dma("sync", dump_aps["xn"], A["xn"][:], "sy")
        if "xnq" in dump_aps:
            P.wait("sync", "dv", xnq_done)
            P.dma("sync", dump_aps["xnq"], A["xnq"][:], "sy")

        P.wait("gpsimd", "pe", stats_done[NSLAB - 1])
        P.wait("gpsimd", "dv", xn_done)
        ld_wak = P.dma("gpsimd", A["wakb"][:], wak, "gq")
        P.wait("sync", "dv", xn_done)
        P.dma("sync", A["cosk"][:], cosk, "sy")
        ld_sk = P.dma("sync", A["sink"][:], sink, "sy")

        P.wait("tensor", "gq", ld_wak)
        kv_done_dv = 0
        kt_psum_war = [v_ev[14], v_ev[15]]
        rope_done = {}
        for i in range(8):
            g, th = i >> 1, i & 1
            kr = A["kraw%d" % (i % 2)]
            ks = A["kshift%d" % (i % 2)]
            ev = 0
            for tq in range(2):
                q = th * 2 + tq
                P.wait("tensor", "dv", kt_psum_war[tq])
                for mc in range(NMC):
                    kmm = P.op("tensor", lambda e, g=g, q=q, mc=mc, tq=tq: e.matmul(
                        P23[:, tq * 512:(tq + 1) * 512],
                        A["wakb"][:, (g * 16 + mc) * 128:(g * 16 + mc + 1) * 128],
                        A["xn"][:, mc * 2048 + q * 512: mc * 2048 + (q + 1) * 512],
                        start=(mc == 0), stop=(mc == NMC - 1)), inc="pe")
                P.wait("vector", "pe", kmm)
                if i >= 2:
                    P.wait("vector", "dv", rope_done[i - 2])
                ev = P.op("vector", lambda e, kr=kr, tq=tq: e.tensor_copy(
                    kr[:, tq * 512:(tq + 1) * 512],
                    P23[:, tq * 512:(tq + 1) * 512]), inc="dv")
                kt_psum_war[tq] = ev
            P.wait("sync", "dv", ev)
            ks1 = P.dma("sync", ks[0:64, :], kr[64:128, :], "sh")
            ks2 = P.dma("sync", ks[64:128, :], kr[0:64, :], "sh2")
            P.wait("vector", "sh", ks1)
            P.wait("vector", "sh2", ks2)
            P.wait("vector", "sy", ld_sk)
            P.op("vector", lambda e, kr=kr, th=th: e.tensor_mul(
                kr[:], kr[:], A["cosk"][:, th * 1024:(th + 1) * 1024]))
            P.op("vector", lambda e, ks=ks, th=th: e.tensor_mul(
                ks[:], ks[:], A["sink"][:, th * 1024:(th + 1) * 1024]))
            kv_done_dv = P.op("vector", lambda e, g=g, th=th, kr=kr, ks=ks: e.tensor_add(
                A["kt"][:, g * 2048 + th * 1024: g * 2048 + (th + 1) * 1024],
                kr[:], ks[:]), inc="dv")
            rope_done[i] = kv_done_dv
            b_half()
            b_half()
        if "kt" in dump_aps:
            P.wait("sync", "dv", kv_done_dv)
            P.dma("sync", dump_aps["kt"], A["kt"][:], "sy")
        if "vsb" in dump_aps:
            P.wait("sync", "dv", v_done_dv)
            P.dma("sync", dump_aps["vsb"], A["vsb"][:], "sy")

        def slot_geom(jg, kap):
            Pp, sg = kap >> 1, kap & 1
            rp = Pp - 4 * jg
            moff = max(0, rp) * 128
            lch = (Pp >> 1) * 4 + sg * 2 + (Pp & 1)
            return Pp, sg, rp, moff, lch

        waq_ld = [0] * (NH + 1)
        qproj_done = [0] * NH
        qe_done = [0] * NH
        qrope_done = [0] * NH
        slot_ac = {}
        exp_all = 0
        pt_war = 0
        ot_war = [kv_done_dv, kv_done_dv]
        ot_dv = 0
        pv_last = [0, 0]
        scores_prev = [0, 0]

        def emit_qproj_rope(h):
            qb = A["qt%d" % (h % 2)]
            wb = A["waq%d" % (h % 2)]
            P.wait("tensor", "gq", waq_ld[h])
            P.wait("tensor", "dv", xnq_done)
            P.wait("tensor", "ac", slot_ac.get(("P45", 0), 0))
            P.wait("tensor", "ac", slot_ac.get(("P45", 1), 0))
            if h >= 1:
                P.wait("tensor", "dv", qe_done[h - 1])
            for th in range(2):
                for mc in range(NMC):
                    qproj_done[h] = P.op("tensor", lambda e, th=th, mc=mc, wb=wb: e.matmul(
                        P45[:, th * 512:(th + 1) * 512],
                        wb[:, mc * 128:(mc + 1) * 128],
                        A["xnq"][:, mc * 1024 + th * 512: mc * 1024 + (th + 1) * 512],
                        start=(mc == 0), stop=(mc == NMC - 1)), inc="pe")
            P.wait("vector", "pe", qproj_done[h])
            if h >= 2:
                P.wait("vector", "pe", scores_prev[h % 2])
            if h >= 1:
                P.wait("vector", "sh", P.counts["sh"])
                P.wait("vector", "sh2", P.counts["sh2"])
            qe_done[h] = P.op("vector", lambda e: e.tensor_copy(A["qraw"][:], P45[:]), inc="dv")
            P.wait("sync", "dv", qe_done[h])
            qs1 = P.dma("sync", A["qshift"][0:64, :], A["qraw"][64:128, :], "sh")
            qs2 = P.dma("sync", A["qshift"][64:128, :], A["qraw"][0:64, :], "sh2")
            P.wait("vector", "sh", qs1)
            P.wait("vector", "sh2", qs2)
            P.wait("vector", "sy", consts_sy)
            P.op("vector", lambda e: e.tensor_mul(A["qraw"][:], A["qraw"][:], A["cosq"][:]))
            P.op("vector", lambda e: e.tensor_mul(A["qshift"][:], A["qshift"][:], A["sinq"][:]))
            qrope_done[h] = P.op("vector", lambda e, qb=qb: e.tensor_add(
                qb[:], A["qraw"][:], A["qshift"][:]), inc="dv")

        kv_pe_done = P.counts["pe"]
        P.wait("gpsimd", "pe", kv_pe_done)
        P.wait("gpsimd", "dv", kv_done_dv)
        P.wait("gpsimd", "sy", P.counts["sy"])
        waq_ld[0] = P.dma("gpsimd", A["waq0"][:], waq[0], "gq")
        P.wait("tensor", "dv", v_done_dv)
        P.wait("tensor", "dv", rows_done[NSLAB - 1])
        emit_qproj_rope(0)
        waq_ld[1] = P.dma("gpsimd", A["waq1"][:], waq[1], "gq")

        for h in range(NH):
            g = h // 4
            qb = A["qt%d" % (h % 2)]
            for jg in range(2):
                nk = 8 * (jg + 1)
                slots = [(P01, "P01", 0), (P01, "P01", 1),
                         (P45, "P45", 0), (P45, "P45", 1)] if jg == 0 else \
                        [(P01, "P01", 0), (P01, "P01", 1),
                         (P23, "P23", 0), (P23, "P23", 1)]
                P.wait("tensor", "dv", qrope_done[h])
                P.wait("tensor", "dv", kv_done_dv)
                mask_dv = 0
                smm_last = 0
                for kap in range(0, nk, 2):
                    Pp, _, rp, moff, _ = slot_geom(jg, kap)
                    base = 2 * (Pp & 1)
                    ps, psname, _ = slots[base]
                    smm = 0
                    for sg in range(2):
                        _, _, _, _, lch = slot_geom(jg, kap + sg)
                        hf = slots[base + sg][2]
                        P.wait("tensor", "ac", slot_ac.get((psname, hf), 0))
                        if psname == "P23":
                            P.wait("tensor", "dv", ot_war[hf])
                        smm = P.op("tensor", lambda e, g=g, lch=lch, ps=ps, hf=hf, moff=moff, jg=jg, qb=qb: e.matmul(
                            ps[:, hf * 512 + moff: (hf + 1) * 512],
                            A["kt"][:, g * 2048 + lch * 128: g * 2048 + (lch + 1) * 128],
                            qb[:, jg * 512 + moff: (jg + 1) * 512],
                            start=True, stop=True), inc="pe")
                    smm_last = smm
                    P.wait("scalar", "pe", smm)
                    P.wait("scalar", "pe", pt_war)
                    src = ps[:].rearrange("p (two w) -> p two w", two=2)[:, :, moff:512]
                    dst = A["pt"][:, kap * 512:(kap + 2) * 512].rearrange(
                        "p (two w) -> p two w", two=2)[:, :, moff:512]
                    eac = P.op("scalar", lambda e, src=src, dst=dst: e.activation(
                        dst, src, AF.Exp, bias=A["bias20"][:], scale=QKSCALE), inc="ac")
                    slot_ac[(psname, 0)] = slot_ac[(psname, 1)] = eac
                    exp_all = eac
                    if 0 <= rp <= 3:
                        P.wait("vector", "ac", eac)
                        P.wait("vector", "sy", consts_sy)
                        for sg in range(2):
                            kp = kap + sg
                            if sg == 0:
                                mask_dv = P.op("vector", lambda e, kp=kp, rp=rp: e.tensor_mul(
                                    A["pt"][:, kp * 512 + rp * 128: kp * 512 + (rp + 1) * 128],
                                    A["pt"][:, kp * 512 + rp * 128: kp * 512 + (rp + 1) * 128],
                                    A["tri"][:]), inc="dv")
                            else:
                                mask_dv = P.op("vector", lambda e, kp=kp, rp=rp: e.tensor_scalar_mul(
                                    A["pt"][:, kp * 512 + rp * 128: kp * 512 + (rp + 1) * 128],
                                    A["pt"][:, kp * 512 + rp * 128: kp * 512 + (rp + 1) * 128],
                                    A["sig1"][:]), inc="dv")

                scores_prev[h % 2] = smm_last
                P.wait("tensor", "ac", exp_all)
                if mask_dv:
                    P.wait("tensor", "dv", mask_dv)
                lh = jg
                P.wait("tensor", "ac", slot_ac.get(("P01", lh), 0))
                lmm = 0
                for kap in range(nk):
                    _, _, _, moff, _ = slot_geom(jg, kap)
                    lmm = P.op("tensor", lambda e, kap=kap, moff=moff, lh=lh: e.matmul(
                        P01[:, lh * 512 + moff: (lh + 1) * 512], A["ones_sq"][:],
                        A["pt"][:, kap * 512 + moff: (kap + 1) * 512],
                        start=(kap == 0), stop=(kap == nk - 1)), inc="pe")
                oh = jg
                P.wait("tensor", "dv", ot_war[oh])
                pvm = 0
                for kap in range(nk):
                    _, _, _, moff, lch = slot_geom(jg, kap)
                    pvm = P.op("tensor", lambda e, kap=kap, moff=moff, lch=lch, g=g, oh=oh: e.matmul(
                        P23[:, oh * 512 + moff: (oh + 1) * 512],
                        A["vsb"][:, lch * 512 + g * 128: lch * 512 + (g + 1) * 128],
                        A["pt"][:, kap * 512 + moff: (kap + 1) * 512],
                        start=(kap == 0), stop=(kap == nk - 1)), inc="pe")
                pt_war = pvm
                pv_last[jg] = pvm
                P.wait("scalar", "pe", lmm)
                P.wait("scalar", "dv", ot_dv)
                P.op("scalar", lambda e, lh=lh: e.activation(
                    A["reca"][:], P01[:, lh * 512:(lh + 1) * 512], AF.Ln, bias=0.0, scale=1.0))
                rc = P.op("scalar", lambda e: e.activation(
                    A["recb"][:], A["reca"][:], AF.Exp, scale=-1.0), inc="ac")
                slot_ac[("P01", lh)] = rc
                P.wait("vector", "pe", pvm)
                P.wait("vector", "ac", rc)
                ot_dv = P.op("vector", lambda e, h=h, jg=jg, oh=oh: e.tensor_tensor(
                    A["ot"][:, h * 1024 + jg * 512: h * 1024 + (jg + 1) * 512],
                    P23[:, oh * 512:(oh + 1) * 512], A["recb"][:], ALU.mult), inc="dv")
                ot_war[oh] = ot_dv
                for _ in range(3):
                    b_half()
                if jg == 0 and h + 1 < NH:
                    emit_qproj_rope(h + 1)
                    if h + 2 < NH:
                        P.wait("gpsimd", "pe", qproj_done[h])
                        waq_ld[h + 2] = P.dma("gpsimd", A["waq%d" % ((h + 2) % 2)][:],
                                              waq[h + 2], "gq")

        attn_ot_dv = ot_dv
        attn_pe_done = pv_last[1]
        attn_ac_done = P.counts["ac"]
        if "qt0" in dump_aps:
            P.wait("sync", "dv", qrope_done[0])
            P.dma("sync", dump_aps["qt0"], A["qt0"][:], "sy")
        if "pt" in dump_aps:
            P.wait("sync", "pe", pt_war)
            P.dma("sync", dump_aps["pt"], A["pt"][:], "sy")
        if "ot" in dump_aps:
            P.wait("sync", "dv", attn_ot_dv)
            P.dma("sync", dump_aps["ot"], A["ot"][:], "sy")

        while b_st["k"] < 128:
            b_half()

        P.wait("gpsimd", "dv", qrope_done[NH - 1])
        P.wait("gpsimd", "dv", attn_ot_dv)
        P.wait("gpsimd", "pe", attn_pe_done)
        wao_mm = {}
        wao_ld = [0] * 4
        rt_ev = {}
        rt_store = [0] * NMC
        for mg in range(4):
            buf = A["wao%d" % (mg % 2)]
            if mg >= 2:
                P.wait("gpsimd", "pe", wao_mm[(mg - 2, NH - 1)])
            wao_ld[mg] = P.dma("gpsimd", buf[:], wao[mg], "gq")
            for h in range(NH):
                P.wait("tensor", "gq", wao_ld[mg])
                if h == 0:
                    P.wait("tensor", "ac", attn_ac_done)
                    P.wait("tensor", "dv", attn_ot_dv)
                    P.wait("tensor", "dv", qe_done[NH - 1])
                    if p67_war["eng"] is not None:
                        P.wait("tensor", p67_war["eng"], p67_war["val"])
                    if mg > 0:
                        for mc4 in range(4):
                            P.wait("tensor", "dv", rt_ev[(mg - 1, mc4)])
                    P.wait("tensor", "dv", attn_ot_dv)
                for mc4 in range(4):
                    for th in range(2):
                        wao_mm[(mg, h)] = P.op("tensor", lambda e, mc4=mc4, th=th, h=h, buf=buf: e.matmul(
                            PS[mc4][:, th * 512:(th + 1) * 512],
                            buf[:, h * 512 + mc4 * 128: h * 512 + (mc4 + 1) * 128],
                            A["ot"][:, h * 1024 + th * 512: h * 1024 + (th + 1) * 512],
                            start=(h == 0), stop=(h == NH - 1)), inc="pe")
            for mc4 in range(4):
                mc = mg * 4 + mc4
                rb = A["rts%d" % (mc4 % 2)]
                P.wait("vector", "pe", wao_mm[(mg, NH - 1)])
                if mc4 >= 2:
                    P.wait("vector", "rs", rt_store[mc - 2])
                elif mg > 0:
                    P.wait("vector", "rs", rt_store[4 * (mg - 1) + mc4 + 2])
                rt_ev[(mg, mc4)] = P.op("vector", lambda e, mc4=mc4, rb=rb: e.tensor_copy(
                    rb[:], PS[mc4][:]), inc="dv")
                P.wait("sync", "dv", rt_ev[(mg, mc4)])
                rt_store[mc] = P.dma("sync", scrR[mc], rb[:], "rs")
            p67_war.update(eng="dv", val=rt_ev[(mg, 3)])
        rt_stores_done = rt_store[NMC - 1]
        p67_war.update(eng="dv", val=rt_ev[(3, 3)])

        load_act_table(tids["silu_and_others"])
        P.wait("gpsimd", "pe", attn_pe_done)
        P.wait("gpsimd", "pe", max(b_st["mm"].values()))
        P.wait("vector", "rs", rt_stores_done)
        f1_ld = [0] * NC_H
        f1_mm_g = {}
        f1_mm_l = {}
        f1_sil = {}
        f1_hid = {}
        f1_h2 = {}
        for c in range(NC_H):
            w13b = A["w13b%d" % (c % 3)]
            w1b = w13b[:, 0:2048]
            w3b = w13b[:, 2048:4096]
            if c >= 3:
                P.wait("gpsimd", "pe", f1_mm_l[c - 3])
            f1_ld[c] = P.dma("gpsimd", w13b[:], w13s[c], "gq")
            P.wait("tensor", "gq", f1_ld[c])
            if c == 0:
                P.wait("tensor", "ac", attn_ac_done)
                P.wait("tensor", "dv", rt_ev[(3, 0)])
                P.wait("tensor", "dv", rt_ev[(3, 1)])
            for th in range(2):
                if (c, th) != (0, 0):
                    pass
                prev_sil = f1_sil.get((c - 1, th), 0)
                P.wait("tensor", "ac", prev_sil)
                for mc in range(NMC):
                    f1_mm_g[(c, th)] = P.op("tensor", lambda e, th=th, mc=mc, w1b=w1b: e.matmul(
                        P01[:, th * 512:(th + 1) * 512],
                        w1b[:, mc * 128:(mc + 1) * 128],
                        A["xnq"][:, mc * 1024 + th * 512: mc * 1024 + (th + 1) * 512],
                        start=(mc == 0), stop=(mc == NMC - 1)), inc="pe")
            for th in range(2):
                prev_hid = f1_hid.get((c - 1, th), 0)
                P.wait("tensor", "dv", prev_hid)
                for mc in range(NMC):
                    f1_mm_l[c] = P.op("tensor", lambda e, th=th, mc=mc, w3b=w3b: e.matmul(
                        P23[:, th * 512:(th + 1) * 512],
                        w3b[:, mc * 128:(mc + 1) * 128],
                        A["xnq"][:, mc * 1024 + th * 512: mc * 1024 + (th + 1) * 512],
                        start=(mc == 0), stop=(mc == NMC - 1)), inc="pe")
                f1_mm_l.setdefault((c, th), f1_mm_l[c])
                f1_mm_l[(c, th)] = f1_mm_l[c]
            for th in range(2):
                P.wait("scalar", "pe", f1_mm_g[(c, th)])
                f1_sil[(c, th)] = P.op("scalar", lambda e, th=th: e.activation(
                    A["gsil"][:, th * 512:(th + 1) * 512], P01[:, th * 512:(th + 1) * 512],
                    AF.Silu, bias=0.0, scale=1.0), inc="ac")
                P.wait("vector", "ac", f1_sil[(c, th)])
                P.wait("vector", "pe", f1_mm_l[(c, th)])
                f1_hid[(c, th)] = P.op("vector", lambda e, c=c, th=th: e.tensor_tensor(
                    A["hidden"][:, c * 1024 + th * 512: c * 1024 + (th + 1) * 512],
                    A["gsil"][:, th * 512:(th + 1) * 512],
                    P23[:, th * 512:(th + 1) * 512], ALU.mult), inc="dv")
            P.wait("tensor", "dv", f1_hid[(c, 1)])
            if c >= 1:
                eng, ev = f1_h2[c - 1]
                P.wait("tensor", eng, ev)
            h2mm = 0
            for th in range(2):
                h2mm = P.op("tensor", lambda e, c=c, th=th: e.matmul(
                    P45[:, th * 512:(th + 1) * 512], A["h128"][:],
                    A["hidden"][:, c * 1024 + th * 512: c * 1024 + (th + 1) * 512],
                    start=True, stop=True), inc="pe")
            if c % 2 == 0:
                P.wait("scalar", "pe", h2mm)
                ev = P.op("scalar", lambda e, c=c: e.activation(
                    A["hidden"][:, c * 1024:(c + 1) * 1024], P45[:], AF.Copy,
                    bias=0.0, scale=1.0), inc="ac")
                f1_h2[c] = ("ac", ev)
            else:
                P.wait("vector", "pe", h2mm)
                ev = P.op("vector", lambda e, c=c: e.tensor_copy(
                    A["hidden"][:, c * 1024:(c + 1) * 1024], P45[:]), inc="dv")
                f1_h2[c] = ("dv", ev)
        hid_done = f1_hid[(NC_H - 1, 1)]
        f1_ac_done = P.counts["ac"]
        f1_dv_done = P.counts["dv"]
        f1_pe_done = P.counts["pe"]
        if "hid" in dump_aps:
            P.wait("sync", "dv", hid_done)
            P.dma("sync", dump_aps["hid"], A["hidden"][:], "sy")

        assert b_st["k"] >= 128, "B-step incomplete: %d" % b_st["k"]
        P.wait("gpsimd", "pe", f1_pe_done)
        P.wait("gpsimd", "dv", hid_done)
        P.wait("gpsimd", "bs", b_st["st"])
        P.wait("sync", "pe", f1_pe_done)
        P.wait("sync", "bs", b_st["st"])
        f2_ld = {}
        f2_mm = {}
        f2_sil = {}
        cmb_dv = {}
        out_st = [0] * NMC
        xc_ld = {}
        rt_ld = {}
        for mq in range(4):
            for c in range(NC_H):
                wqb = A["w2q%d" % (c % 8)]
                prev = f2_mm.get((mq, c - 8), f2_mm.get((mq - 1, NC_H - 8 + (c % 8)), 0))
                P.wait("sync", "pe", prev)
                f2_ld[(mq, c)] = P.dma("sync", wqb[:], scrB[mq, c], "sy")
                P.wait("tensor", "sy", f2_ld[(mq, c)])
                if c == 0:
                    P.wait("tensor", "dv", f1_dv_done)
                    P.wait("tensor", "ac", f1_ac_done)
                    if mq == 0:
                        if p67_war["eng"] is not None:
                            P.wait("tensor", p67_war["eng"], p67_war["val"])
                for mc4 in range(4):
                    if c == 0 and mq > 0:
                        P.wait("tensor", "ac", f2_sil[(mq - 1, mc4)])
                    for th in range(2):
                        f2_mm[(mq, c)] = P.op("tensor", lambda e, mc4=mc4, th=th, c=c, wqb=wqb: e.matmul(
                            PS[mc4][:, th * 512:(th + 1) * 512],
                            wqb[:, mc4 * 128:(mc4 + 1) * 128],
                            A["hidden"][:, c * 1024 + th * 512: c * 1024 + (th + 1) * 512],
                            start=(c == 0), stop=(c == NC_H - 1)), inc="pe")
            for mc4 in range(4):
                mc = mq * 4 + mc4
                P.wait("sync", "dv", cmb_dv.get(mc - 2, 0))
                xc_ld[mc] = P.dma("sync", A["xc%d" % (mc % 2)][:], xq[mc], "sy")
                P.wait("gpsimd", "rs", rt_stores_done)
                P.wait("gpsimd", "dv", cmb_dv.get(mc - 2, 0))
                rt_ld[mc] = P.dma("gpsimd", A["rt%d" % (mc % 2)][:], scrR[mc], "gq")
                P.wait("scalar", "pe", f2_mm[(mq, NC_H - 1)])
                P.wait("scalar", "dv", cmb_dv.get(mc - 1, 0))
                f2_sil[(mq, mc4)] = P.op("scalar", lambda e, mc4=mc4: e.activation(
                    A["gf"][:], PS[mc4][:], AF.Silu, bias=0.0, scale=1.0), inc="ac")
                P.wait("vector", "ac", f2_sil[(mq, mc4)])
                P.wait("vector", "gq", rt_ld[mc])
                P.wait("vector", "sy", xc_ld[mc])
                P.wait("vector", "bs", b_st["st"])
                if mc >= 2:
                    P.wait("vector", "os", out_st[mc - 2])
                P.op("vector", lambda e, mc=mc: e.tensor_mul(
                    A["gf"][:], A["gf"][:], A["rt%d" % (mc % 2)][:]))
                cmb_dv[mc] = P.op("vector", lambda e, mc=mc: e.tensor_add(
                    A["ou%d" % (mc % 2)][:], A["gf"][:], A["xc%d" % (mc % 2)][:]), inc="dv")
                P.wait("sync", "dv", cmb_dv[mc])
                out_st[mc] = P.dma("sync", out_t[mc], A["ou%d" % (mc % 2)][:], "os")
        if "rt" in dump_aps:
            P.wait("sync", "rs", rt_stores_done)
            for mc in range(NMC):
                P.dma("sync", dump_aps["rt"][mc], scrR[mc], "sy")

        P.wait("sync", "sy", P.counts["sy"])
        P.wait("sync", "os", P.counts["os"])
        P.run(block)
    return nc



def _rope_tables():
    pos = np.arange(T, dtype=np.float32)[:, None]
    freqs = np.power(np.float32(10000.0),
                     -(np.arange(64, dtype=np.float32) / np.float32(64.0)))[None, :].astype(np.float32)
    ang = (pos * freqs).astype(np.float32)
    cos = np.cos(ang.astype(np.float64)).astype(np.float32)
    sin = np.sin(ang.astype(np.float64)).astype(np.float32)
    ck = np.empty((128, T), np.float32)
    sk = np.empty((128, T), np.float32)
    ck[0:64] = cos.T
    ck[64:128] = cos.T
    sk[0:64] = -sin.T
    sk[64:128] = sin.T
    return ck, sk


def _local_perm(par):
    idx = []
    for s in range(NSLAB):
        us = [par, 2 + par, 1 - par, 3 - par]
        for u in us:
            c0 = (s * 4 + u) * 128
            idx.extend(range(c0, c0 + 128))
    return np.array(idx, np.int64)


def prepare_in_maps(L0_attn_x, L0_attn_w_aq, L0_attn_w_ak, L0_attn_w_av, L0_attn_w_ao,
                    L0_ffn_W1, L0_ffn_W2, L0_ffn_W3):
    x = np.ascontiguousarray(np.asarray(L0_attn_x, dtype=np.float32))
    w_aq = np.asarray(L0_attn_w_aq, np.float32)
    w_ak = np.asarray(L0_attn_w_ak, np.float32)
    w_av = np.asarray(L0_attn_w_av, np.float32)
    w_ao = np.asarray(L0_attn_w_ao, np.float32)
    W1 = np.asarray(L0_ffn_W1, np.float32)
    W2 = np.asarray(L0_ffn_W2, np.float32)
    W3 = np.asarray(L0_ffn_W3, np.float32)

    waq_h = np.ascontiguousarray(
        w_aq.reshape(NH, 16, 128, 128).transpose(0, 2, 1, 3).reshape(NH, 128, 2048))
    wak_h = np.ascontiguousarray(
        w_ak.reshape(G, 16, 128, 128).transpose(2, 0, 1, 3).reshape(128, 8192))
    wav_h = np.ascontiguousarray(
        w_av.reshape(G, 16, 128, 128).transpose(2, 1, 0, 3).reshape(128, 8192))
    wao_h = np.ascontiguousarray(
        w_ao.reshape(NH, 128, 4, 512).transpose(2, 1, 0, 3).reshape(4, 128, 8192))
    w1_h = W1.reshape(NC_H, 128, NMC, 128).transpose(0, 3, 2, 1).reshape(NC_H, 128, 2048)
    w3_h = W3.reshape(NC_H, 128, NMC, 128).transpose(0, 3, 2, 1).reshape(NC_H, 128, 2048)
    w13_h = np.ascontiguousarray(
        np.stack([w1_h, w3_h], axis=2).reshape(NC_H, 128, 4096))
    w2t_h = np.ascontiguousarray(W2.T)

    ck, sk = _rope_tables()

    def sylvester(n):
        m = np.array([[1]], np.int64)
        while m.shape[0] < n:
            m = np.block([[m, m], [m, -m]])
        return m

    h128_h = sylvester(128).astype(ml_dtypes.bfloat16)
    h64 = sylvester(64)
    h64d_h = np.zeros((128, 128), np.int64)
    h64d_h[0:64, 0:64] = h64
    h64d_h[64:128, 64:128] = h64
    h64d_h = h64d_h.astype(ml_dtypes.bfloat16)
    tri_h = np.triu(np.ones((128, 128), np.float32)).astype(ml_dtypes.bfloat16)

    in_maps = []
    perms = {}
    for core in range(8):
        b, par = core // 2, core % 2
        perm = _local_perm(par)
        perms[core] = perm
        xl = x[b][perm]
        xs_h = np.ascontiguousarray(
            xl.reshape(NSLAB, 512, NMC, 128).transpose(0, 3, 2, 1))
        qsel_l = perm.reshape(NSLAB, 4, 128)[:, 0:2].reshape(-1)
        xq_h = np.ascontiguousarray(
            x[b][qsel_l].reshape(1024, NMC, 128).transpose(1, 2, 0))
        ckl = np.ascontiguousarray(ck[:, perm])
        skl = np.ascontiguousarray(sk[:, perm])
        qsel = perm.reshape(NSLAB, 4, 128)[:, 0:2].reshape(-1)
        cql = np.ascontiguousarray(ck[:, qsel])
        sql = np.ascontiguousarray(sk[:, qsel])
        sig1_h = np.full((128, 1), 1.0 if par == 1 else 0.0, np.float32)
        in_maps.append({
            "xs": xs_h, "xq": xq_h, "waq": waq_h, "wak": wak_h, "wav": wav_h, "wao": wao_h,
            "w13s": w13_h, "w2t": w2t_h,
            "cosk": ckl, "sink": skl, "cosq": cql, "sinq": sql,
            "tri": tri_h, "sig1": sig1_h, "h128": h128_h, "h64d": h64d_h,
        })
    return in_maps, perms


def kernel(L0_attn_x, L0_attn_w_aq, L0_attn_w_ak, L0_attn_w_av, L0_attn_w_ao,
           L0_ffn_W1, L0_ffn_W2, L0_ffn_W3):
    in_maps, perms = prepare_in_maps(
        L0_attn_x, L0_attn_w_aq, L0_attn_w_ak, L0_attn_w_av, L0_attn_w_ao,
        L0_ffn_W1, L0_ffn_W2, L0_ffn_W3)
    nc = build_nc()
    res = run_bass_kernel_spmd(nc, in_maps, core_ids=list(range(8)))

    out = np.empty((B, T, M), np.float32)
    for core in range(8):
        b = core // 2
        oc = res.results[core]["out"]
        ov = oc.reshape(NMC, 128, 8, 128).transpose(2, 3, 0, 1).reshape(1024, M)
        qsel = perms[core].reshape(NSLAB, 4, 128)[:, 0:2].reshape(-1)
        out[b][qsel] = ov
    return out


if __name__ == "__main__":
    pass


# version 13 (final)
# speedup vs baseline: 1.1355x; 1.0017x over previous
import math

import numpy as np
import ml_dtypes

import concourse.bass as bass
import concourse.mybir as mybir
from concourse.bass_utils import run_bass_kernel_spmd

F32 = mybir.dt.float32
BF16 = mybir.dt.bfloat16
AF = mybir.ActivationFunctionType
ALU = mybir.AluOpType

B, T, M, D, G, NH = 4, 2048, 2048, 128, 4, 16
HF = 8192
NMC = 16
NC_H = 64
NSLAB = 4
KB = 1024

QKSCALE = float(128.0 ** -0.5)
S8192 = float(1.0 / math.sqrt(8192.0))
EXP_SHIFT = -20.0
EPS = 1e-5

ENGS = ("tensor", "scalar", "vector", "sync", "gpsimd")

_TABLE_IDS = None


def _table_ids(nc):
    global _TABLE_IDS
    if _TABLE_IDS is None:
        from concourse.hw_specs import get_activation_tables
        names = list(get_activation_tables(nc.m.arch).keys())
        _TABLE_IDS = {n: i for i, n in enumerate(names)}
    return _TABLE_IDS


class Plan:
    def __init__(self, nc):
        self.nc = nc
        self.items = {e: [] for e in ENGS}
        self.sem_handles = {}
        self.counts = {}
        self.watermark = {}

    def add_sem(self, name, handle):
        self.sem_handles[name] = handle
        self.counts[name] = 0

    def wait(self, eng, sem, val=None):
        if val is None:
            val = self.counts[sem]
        assert val <= self.counts[sem], (eng, sem, val, self.counts[sem])
        if val <= 0:
            return
        key = (eng, sem)
        if self.watermark.get(key, -1) >= val:
            return
        self.watermark[key] = val
        h = self.sem_handles[sem]
        self.items[eng].append(lambda e, h=h, v=val: e.wait_ge(h, v))

    def op(self, eng, fn, inc=None, inc_n=1):
        if inc is not None:
            self.counts[inc] += inc_n
            h = self.sem_handles[inc]

            def f(e, fn=fn, h=h, n=inc_n):
                fn(e).then_inc(h, n)
        else:
            def f(e, fn=fn):
                fn(e)
        self.items[eng].append(f)
        return self.counts[inc] if inc is not None else None

    def dma(self, eng, out, in_, sem):
        return self.op(eng, lambda e, o=out, i=in_: e.dma_start(out=o, in_=i),
                       inc=sem, inc_n=16)

    def run(self, block):
        block.tensor(lambda e: [f(e) for f in self.items["tensor"]])
        block.scalar(lambda e: [f(e) for f in self.items["scalar"]])
        block.vector(lambda e: [f(e) for f in self.items["vector"]])
        block.sync(lambda e: [f(e) for f in self.items["sync"]])
        block.gpsimd(lambda e: [f(e) for f in self.items["gpsimd"]])


RACE_DETECT = False


def build_nc(dumps=()):
    nc = bass.Bass("TRN2", target_bir_lowering=False,
                   detect_race_conditions=RACE_DETECT)

    xs = nc.dram_tensor("xs", [NSLAB, 128, NMC, 512], F32, kind="ExternalInput").ap()
    xq = nc.dram_tensor("xq", [NMC, 128, 1024], F32, kind="ExternalInput").ap()
    waq = nc.dram_tensor("waq", [NH, 128, 2048], F32, kind="ExternalInput").ap()
    wak = nc.dram_tensor("wak", [128, 8192], F32, kind="ExternalInput").ap()
    wav = nc.dram_tensor("wav", [128, 8192], F32, kind="ExternalInput").ap()
    wao = nc.dram_tensor("wao", [4, 128, 8192], F32, kind="ExternalInput").ap()
    w13s = nc.dram_tensor("w13s", [NC_H, 128, 4096], F32, kind="ExternalInput").ap()
    w2t = nc.dram_tensor("w2t", [HF, 2048], F32, kind="ExternalInput").ap()
    cosk = nc.dram_tensor("cosk", [128, 2048], F32, kind="ExternalInput").ap()
    sink = nc.dram_tensor("sink", [128, 2048], F32, kind="ExternalInput").ap()
    cosq = nc.dram_tensor("cosq", [128, 1024], F32, kind="ExternalInput").ap()
    sinq = nc.dram_tensor("sinq", [128, 1024], F32, kind="ExternalInput").ap()
    tri_i = nc.dram_tensor("tri", [128, 128], BF16, kind="ExternalInput").ap()
    sig1_i = nc.dram_tensor("sig1", [128, 1], F32, kind="ExternalInput").ap()
    h128_i = nc.dram_tensor("h128", [128, 128], BF16, kind="ExternalInput").ap()
    h64d_i = nc.dram_tensor("h64d", [128, 128], BF16, kind="ExternalInput").ap()

    out_t = nc.dram_tensor("out", [NMC, 128, 1024], F32, kind="ExternalOutput").ap()
    dump_aps = {}
    for nm in dumps:
        shapes = {"xn": [128, 32768], "kt": [128, 8192], "vsb": [128, 8192],
                  "ot": [128, 16384], "hid": [128, 65536], "pt": [128, 8192],
                  "qt0": [128, 1024], "rt": [NMC, 128, 1024], "xnq": [128, 16384]}
        dt = BF16
        dump_aps[nm] = nc.dram_tensor("dump_" + nm, shapes[nm], dt, kind="ExternalOutput").ap()

    scrB = nc.dram_tensor("scrB", [4, 64, 128, 512], BF16, kind="Internal").ap()
    scrR = nc.dram_tensor("scrR", [NMC, 128, 1024], BF16, kind="Internal").ap()

    A = {}

    SB_BASE = 16512

    def sb(name, shape, dtype, off_kb):
        A[name] = nc.alloc_sbuf_tensor_at(name, shape, dtype,
                                          offset=SB_BASE + int(off_kb * KB))
        return A[name]

    sb("ones_sq", [128, 128], BF16, 0.0)
    sb("bias20", [128, 1], F32, 0.25)
    sb("epsb", [128, 1], F32, 0.28125)
    sb("sig1", [128, 1], F32, 0.3125)
    sb("tri", [128, 128], BF16, 0.5)
    sb("h128", [128, 128], BF16, 0.75)
    sb("h64d", [128, 128], BF16, 1.0)
    sb("cosq", [128, 1024], F32, 1.25)
    sb("sinq", [128, 1024], F32, 5.25)
    sb("xnq", [128, NMC * 1024], BF16, 9.25)
    XN = 41.5
    sb("xn", [128, NMC * 2048], BF16, XN)
    sb("hidden", [128, NC_H * 1024], BF16, XN)
    sb("qraw", [128, 1024], F32, XN)
    sb("qshift", [128, 1024], F32, XN + 4)
    sb("qtmp", [128, 1024], F32, XN + 8)
    sb("waq0", [128, 2048], BF16, XN + 12)
    sb("waq1", [128, 2048], BF16, XN + 16)
    sb("reca", [128, 512], F32, XN + 20)
    sb("recb", [128, 512], F32, XN + 22)
    sb("wao0", [128, 8192], BF16, XN)
    sb("wao1", [128, 8192], BF16, XN + 16)
    sb("rts0", [128, 1024], BF16, XN + 32)
    sb("rts1", [128, 1024], BF16, XN + 34)
    sb("xslab", [128, NMC * 512], F32, 105.5)
    sb("xb", [128, NMC * 512], BF16, 9.25)
    sb("xsq", [128, NMC * 512], BF16, 25.25)
    sb("ln_mean", [128, 512], F32, 137.5)
    sb("ln_t1", [128, 512], F32, 139.5)
    sb("ln_t2", [128, 512], F32, 141.5)
    sb("kraw0", [128, 1024], F32, 105.5)
    sb("kraw1", [128, 1024], F32, 109.5)
    sb("kshift0", [128, 1024], F32, 113.5)
    sb("kshift1", [128, 1024], F32, 149.5)
    sb("cosk", [128, 2048], F32, 117.5)
    sb("sink", [128, 2048], F32, 125.5)
    sb("wakb", [128, 8192], BF16, 133.5)
    sb("wavb", [128, 8192], BF16, 149.5)
    sb("pt", [128, 16 * 512], BF16, 105.5)
    sb("ot", [128, NH * 1024], BF16, 121.5)
    sb("qt0", [128, 1024], BF16, 153.5)
    sb("qt1", [128, 1024], BF16, 155.5)
    sb("kt", [128, G * 2048], BF16, 157.5)
    sb("vsb", [128, 16 * 512], BF16, 173.5)
    sb("bin0", [128, 2048], BF16, 189.5)
    sb("bin1", [128, 2048], BF16, 193.5)
    sb("bout0", [128, 2048], BF16, 197.5)
    sb("bout1", [128, 2048], BF16, 201.5)
    sb("w13b0", [128, 4096], BF16, 169.5)
    sb("w13b1", [128, 4096], BF16, 177.5)
    sb("w13b2", [128, 4096], BF16, 189.5)
    sb("gsil", [128, 1024], F32, 185.5)
    for _i in range(8):
        sb("w2q%d" % _i, [128, 512], BF16, 169.5 + _i)
    sb("gf", [128, 1024], F32, 177.5)
    sb("xc0", [128, 1024], F32, 181.5)
    sb("xc1", [128, 1024], F32, 185.5)
    sb("rt0", [128, 1024], F32, 189.5)
    sb("rt1", [128, 1024], F32, 193.5)
    sb("ou0", [128, 1024], F32, 197.5)
    sb("ou1", [128, 1024], F32, 201.5)

    P01 = nc.alloc_psum_tensor("P01", [128, 1024], F32)
    P23 = nc.alloc_psum_tensor("P23", [128, 1024], F32)
    P45 = nc.alloc_psum_tensor("P45", [128, 1024], F32)
    P67 = nc.alloc_psum_tensor("P67", [128, 1024], F32)
    PS = [P01, P23, P45, P67]

    sem_names = ["pe", "ac", "dv", "sy", "sk", "xl1", "xl2", "sh", "sh2",
                 "qh", "qh2", "wv", "wk", "q0", "q1", "wo0", "wo1",
                 "fw0", "fw1", "fw2", "rl0", "rl1",
                 "bl0", "bl0b", "bl1", "bl1b",
                 "bs00", "bs01", "bs10", "bs11",
                 "rs0", "rs1", "os0", "os1", "xq0", "xq1"]
    sem_names += ["fl%d" % i for i in range(8)]
    sems = {nm: nc.alloc_semaphore(name="s_" + nm) for nm in sem_names}
    with nc.Block() as block:
        P = Plan(nc)
        for nm in sem_names:
            P.add_sem(nm, sems[nm])

        tids = _table_ids(nc)

        def load_act_table(tid):
            P.op("scalar", lambda e, tid=tid: e.add_instruction(mybir.InstLoadActFuncSet(
                name=nc.get_next_instruction_name(), act_func_set_id=tid, ins=[], outs=[])))

        P.op("vector", lambda e: e.memset(A["ones_sq"][:], 1.0))
        P.op("vector", lambda e: e.memset(A["bias20"][:], EXP_SHIFT))
        consts_dv = P.op("vector", lambda e: e.memset(A["epsb"][:], EPS), inc="dv")
        P.dma("sync", A["tri"][:], tri_i, "sy")
        P.dma("sync", A["sig1"][:], sig1_i, "sy")
        P.dma("sync", A["h128"][:], h128_i, "sy")
        P.dma("sync", A["h64d"][:], h64d_i, "sy")
        P.dma("sync", A["cosq"][:], cosq, "sy")
        P.dma("sync", A["sinq"][:], sinq, "sy")
        consts_sy = P.counts["sy"]
        load_act_table(tids["natural_log_exp_and_others"])

        p67_war = {"eng": None, "val": 0}
        b_st = {"k": 0, "ld": {}, "mm": {}, "stores": {}, "st": 0}

        def b_load(pp):
            if pp > 63 or pp in b_st["ld"]:
                return
            if pp >= 2:
                P.wait("gpsimd", "pe", b_st["mm"].get(2 * (pp - 2) + 1, 0))
            bi = A["bin%d" % (pp % 2)]
            l1 = P.dma("gpsimd", bi[0:64, :],
                  w2t.rearrange("(c p) m -> c p m", p=128)[:, 2 * pp, :], "bl%d" % (pp % 2))
            l2 = P.dma("gpsimd", bi[64:128, :],
                  w2t.rearrange("(c p) m -> c p m", p=128)[:, 2 * pp + 1, :], "bl%db" % (pp % 2))
            b_st["ld"][pp] = (l1, l2)

        def b_half():
            k = b_st["k"]
            if k >= 128:
                return
            pp, u = k >> 1, k & 1
            bi = A["bin%d" % (pp % 2)]
            bo = A["bout%d" % (pp % 2)]
            b_load(pp)
            b_load(pp + 1)
            P.wait("tensor", "sy", consts_sy)
            P.wait("tensor", "bl%d" % (pp % 2), b_st["ld"][pp][0])
            P.wait("tensor", "bl%db" % (pp % 2), b_st["ld"][pp][1])
            if p67_war["eng"] is not None:
                P.wait("tensor", p67_war["eng"], p67_war["val"])
            for j in range(2):
                mm = P.op("tensor", lambda e, u=u, j=j, bi=bi: e.matmul(
                    P67[:, j * 512:(j + 1) * 512], A["h64d"][:],
                    bi[:, u * 1024 + j * 512: u * 1024 + (j + 1) * 512],
                    start=True, stop=True), inc="pe")
            b_st["mm"][k] = mm
            bsem = "bs%d%d" % ((k >> 1) & 1, k & 1)
            buf_war = b_st["stores"].get(k - 4, 0)
            if k % 2 == 0:
                P.wait("scalar", "pe", mm)
                P.wait("scalar", bsem, buf_war)
                ev = P.op("scalar", lambda e, bo=bo, u=u: e.activation(
                    bo[:, u * 1024:(u + 1) * 1024], P67[:], AF.Copy,
                    bias=0.0, scale=S8192), inc="ac")
                p67_war.update(eng="ac", val=ev)
                ev_eng = "ac"
            else:
                P.wait("vector", "pe", mm)
                P.wait("vector", bsem, buf_war)
                ev = P.op("vector", lambda e, bo=bo, u=u: e.tensor_scalar_mul(
                    bo[:, u * 1024:(u + 1) * 1024], P67[:], S8192), inc="dv")
                p67_war.update(eng="dv", val=ev)
                ev_eng = "dv"
            P.wait("sync", ev_eng, ev)
            for j in range(2):
                dst = scrB[2 * u:2 * u + 2, :, 2 * pp + j, :].rearrange("q c m -> c q m")
                b_st["st"] = P.dma("sync", dst,
                                   bo[j * 64:(j + 1) * 64, u * 1024:(u + 1) * 1024], bsem)
            b_st["stores"][k] = b_st["st"]
            b_st["k"] = k + 1

        xslab_v = A["xslab"][:].rearrange("p (mc t) -> p mc t", mc=NMC)
        xld = [0] * NSLAB
        apply_done = [0] * NSLAB
        stats_done = [0] * NSLAB
        rows_done = [0] * NSLAB

        xbld = [0] * NSLAB
        v_ev = [0] * 16
        ld_wav = 0
        for s in range(NSLAB):
            if s > 0:
                P.wait("sync", "dv", apply_done[s - 1])
                P.wait("gpsimd", "pe", stats_done[s - 1])
            xld1 = P.dma("sync", xslab_v[:, 0:8, :], xs[s][:, 0:8, :], "xl1")
            xld[s] = P.dma("sync", xslab_v[:, 8:16, :], xs[s][:, 8:16, :], "xl2")
            if s == 0:
                ld_wav = P.dma("gpsimd", A["wavb"][:], wav, "wv")
            P.wait("scalar", "xl1", xld1)
            if s > 0:
                P.wait("scalar", "pe", stats_done[s - 1])
            P.op("scalar", lambda e: e.activation(
                A["xb"][:, 0:4096], A["xslab"][:, 0:4096], AF.Copy, bias=0.0, scale=1.0), inc="ac")
            P.op("scalar", lambda e: e.activation(
                A["xsq"][:, 0:4096], A["xslab"][:, 0:4096], AF.Square, bias=0.0, scale=1.0), inc="ac")
            P.wait("scalar", "xl2", xld[s])
            P.op("scalar", lambda e: e.activation(
                A["xb"][:, 4096:8192], A["xslab"][:, 4096:8192], AF.Copy, bias=0.0, scale=1.0), inc="ac")
            sq_ac = P.op("scalar", lambda e: e.activation(
                A["xsq"][:, 4096:8192], A["xslab"][:, 4096:8192], AF.Square, bias=0.0, scale=1.0), inc="ac")
            P.wait("tensor", "ac", sq_ac)
            P.wait("tensor", "dv", consts_dv)
            if s > 0:
                P.wait("tensor", "dv", rows_done[s - 1])
            for mc in range(NMC):
                P.op("tensor", lambda e, mc=mc: e.matmul(
                    P01[:, 0:512], A["ones_sq"][:], A["xb"][:, mc * 512:(mc + 1) * 512],
                    start=(mc == 0), stop=(mc == NMC - 1)), inc="pe")
            for mc in range(NMC):
                stats_done[s] = P.op("tensor", lambda e, mc=mc: e.matmul(
                    P01[:, 512:1024], A["ones_sq"][:], A["xsq"][:, mc * 512:(mc + 1) * 512],
                    start=(mc == 0), stop=(mc == NMC - 1)), inc="pe")
            P.wait("vector", "pe", stats_done[s])
            P.op("vector", lambda e: e.tensor_scalar_mul(A["ln_mean"][:], P01[:, 0:512], 1.0 / 2048.0))
            P.op("vector", lambda e: e.tensor_scalar_mul(A["ln_t1"][:], P01[:, 512:1024], 1.0 / 2048.0))
            P.op("vector", lambda e: e.tensor_mul(A["ln_t2"][:], A["ln_mean"][:], A["ln_mean"][:]))
            rows_done[s] = P.op("vector", lambda e: e.tensor_sub(
                A["ln_t1"][:], A["ln_t1"][:], A["ln_t2"][:]), inc="dv")
            P.wait("scalar", "dv", rows_done[s])
            P.op("scalar", lambda e: e.activation(
                A["ln_t2"][:], A["ln_t1"][:], AF.Ln, bias=A["epsb"][:], scale=1.0))
            ac_rstd = P.op("scalar", lambda e: e.activation(
                A["ln_t1"][:], A["ln_t2"][:], AF.Exp, scale=-0.5), inc="ac")
            P.wait("vector", "ac", ac_rstd)
            P.op("vector", lambda e: e.tensor_mul(A["ln_t2"][:], A["ln_mean"][:], A["ln_t1"][:]))
            for mc in range(NMC):
                P.op("vector", lambda e, mc=mc: e.tensor_mul(
                    A["xslab"][:, mc * 512:(mc + 1) * 512],
                    A["xslab"][:, mc * 512:(mc + 1) * 512], A["ln_t1"][:]))
                apply_done[s] = P.op("vector", lambda e, mc=mc, s=s: e.tensor_sub(
                    A["xn"][:, mc * 2048 + s * 512: mc * 2048 + (s + 1) * 512],
                    A["xslab"][:, mc * 512:(mc + 1) * 512], A["ln_t2"][:]), inc="dv")
            P.wait("tensor", "wv", ld_wav)
            P.wait("tensor", "dv", apply_done[s])
            for tc in range(4 * s, 4 * s + 4):
                half = tc % 2
                if tc >= 1:
                    P.wait("tensor", "dv", v_ev[tc - 1])
                for mc in range(NMC):
                    vmm = P.op("tensor", lambda e, tc=tc, mc=mc, half=half: e.matmul(
                        P23[:, half * 512:(half + 1) * 512],
                        A["xn"][:, mc * 2048 + tc * 128: mc * 2048 + (tc + 1) * 128],
                        A["wavb"][:, mc * 512:(mc + 1) * 512],
                        start=(mc == 0), stop=(mc == NMC - 1)), inc="pe")
                P.wait("vector", "pe", vmm)
                v_ev[tc] = P.op("vector", lambda e, tc=tc, half=half: e.tensor_copy(
                    A["vsb"][:, tc * 512:(tc + 1) * 512],
                    P23[:, half * 512:(half + 1) * 512]), inc="dv")
            for _ in range(4):
                b_half()
        xn_done = apply_done[NSLAB - 1]
        v_done_dv = v_ev[15]

        P.wait("vector", "pe", stats_done[NSLAB - 1])
        xnq_done = 0
        for mc in range(NMC):
            src = A["xn"][:, mc * 2048:(mc + 1) * 2048].rearrange(
                "p (s t) -> p s t", s=NSLAB)[:, :, 0:256]
            xnq_done = P.op("vector", lambda e, mc=mc, src=src: e.tensor_copy(
                A["xnq"][:, mc * 1024:(mc + 1) * 1024], src), inc="dv")
        if "xn" in dump_aps:
            P.wait("sync", "dv", xn_done)
            P.dma("sync", dump_aps["xn"], A["xn"][:], "sy")
        if "xnq" in dump_aps:
            P.wait("sync", "dv", xnq_done)
            P.dma("sync", dump_aps["xnq"], A["xnq"][:], "sy")

        P.wait("gpsimd", "pe", stats_done[NSLAB - 1])
        P.wait("gpsimd", "dv", xn_done)
        ld_wak = P.dma("gpsimd", A["wakb"][:], wak, "wk")
        P.wait("sync", "dv", xn_done)
        P.dma("sync", A["cosk"][:], cosk, "sk")
        ld_sk = P.dma("sync", A["sink"][:], sink, "sk")

        P.wait("tensor", "wk", ld_wak)
        kv_done_dv = 0
        kt_psum_war = [v_ev[14], v_ev[15]]
        rope_done = {}
        for i in range(8):
            g, th = i >> 1, i & 1
            kr = A["kraw%d" % (i % 2)]
            ks = A["kshift%d" % (i % 2)]
            ev = 0
            kmm = 0
            for tq in range(2):
                q = th * 2 + tq
                P.wait("tensor", "dv", kt_psum_war[tq])
                for mc in range(NMC):
                    kmm = P.op("tensor", lambda e, g=g, q=q, mc=mc, tq=tq: e.matmul(
                        P23[:, tq * 512:(tq + 1) * 512],
                        A["wakb"][:, (g * 16 + mc) * 128:(g * 16 + mc + 1) * 128],
                        A["xn"][:, mc * 2048 + q * 512: mc * 2048 + (q + 1) * 512],
                        start=(mc == 0), stop=(mc == NMC - 1)), inc="pe")
            for tq in range(2):
                P.wait("vector", "pe", kmm)
                if i >= 2:
                    P.wait("vector", "dv", rope_done[i - 2])
                ev = P.op("vector", lambda e, kr=kr, tq=tq: e.tensor_copy(
                    kr[:, tq * 512:(tq + 1) * 512],
                    P23[:, tq * 512:(tq + 1) * 512]), inc="dv")
                kt_psum_war[tq] = ev
            P.wait("sync", "dv", ev)
            ks1 = P.dma("sync", ks[0:64, :], kr[64:128, :], "sh")
            ks2 = P.dma("sync", ks[64:128, :], kr[0:64, :], "sh2")
            P.wait("vector", "sh", ks1)
            P.wait("vector", "sh2", ks2)
            P.wait("vector", "sk", ld_sk)
            P.op("vector", lambda e, kr=kr, th=th: e.tensor_mul(
                kr[:], kr[:], A["cosk"][:, th * 1024:(th + 1) * 1024]))
            P.op("vector", lambda e, ks=ks, th=th: e.tensor_mul(
                ks[:], ks[:], A["sink"][:, th * 1024:(th + 1) * 1024]))
            kv_done_dv = P.op("vector", lambda e, g=g, th=th, kr=kr, ks=ks: e.tensor_add(
                A["kt"][:, g * 2048 + th * 1024: g * 2048 + (th + 1) * 1024],
                kr[:], ks[:]), inc="dv")
            rope_done[i] = kv_done_dv
            b_half()
            b_half()
        if "kt" in dump_aps:
            P.wait("sync", "dv", kv_done_dv)
            P.dma("sync", dump_aps["kt"], A["kt"][:], "sy")
        if "vsb" in dump_aps:
            P.wait("sync", "dv", v_done_dv)
            P.dma("sync", dump_aps["vsb"], A["vsb"][:], "sy")

        def slot_geom(jg, kap):
            Pp, sg = kap >> 1, kap & 1
            rp = Pp - 4 * jg
            moff = max(0, rp) * 128
            lch = (Pp >> 1) * 4 + sg * 2 + (Pp & 1)
            return Pp, sg, rp, moff, lch

        waq_ld = [0] * (NH + 1)
        qproj_done = [0] * NH
        qe_done = [0] * NH
        qrope_done = [0] * NH
        slot_ac = {}
        exp_all = 0
        pt_war = 0
        ot_war = [kv_done_dv, kv_done_dv]
        ot_dv = 0
        pv_last = [0, 0]
        scores_prev = [0, 0]

        def emit_qproj_rope(h):
            qb = A["qt%d" % (h % 2)]
            wb = A["waq%d" % (h % 2)]
            P.wait("tensor", "q%d" % (h % 2), waq_ld[h])
            P.wait("tensor", "dv", xnq_done)
            P.wait("tensor", "ac", slot_ac.get(("P45", 0), 0))
            P.wait("tensor", "ac", slot_ac.get(("P45", 1), 0))
            if h >= 1:
                P.wait("tensor", "dv", qe_done[h - 1])
            for th in range(2):
                for mc in range(NMC):
                    qproj_done[h] = P.op("tensor", lambda e, th=th, mc=mc, wb=wb: e.matmul(
                        P45[:, th * 512:(th + 1) * 512],
                        wb[:, mc * 128:(mc + 1) * 128],
                        A["xnq"][:, mc * 1024 + th * 512: mc * 1024 + (th + 1) * 512],
                        start=(mc == 0), stop=(mc == NMC - 1)), inc="pe")
            P.wait("vector", "pe", qproj_done[h])
            if h >= 2:
                P.wait("vector", "pe", scores_prev[h % 2])
            if h >= 1:
                P.wait("vector", "qh", P.counts["qh"])
                P.wait("vector", "qh2", P.counts["qh2"])
            qe_done[h] = P.op("vector", lambda e: e.tensor_copy(A["qraw"][:], P45[:]), inc="dv")
            P.wait("sync", "dv", qe_done[h])
            qs1 = P.dma("sync", A["qshift"][0:64, :], A["qraw"][64:128, :], "qh")
            qs2 = P.dma("sync", A["qshift"][64:128, :], A["qraw"][0:64, :], "qh2")
            P.wait("vector", "qh", qs1)
            P.wait("vector", "qh2", qs2)
            P.wait("vector", "sy", consts_sy)
            P.op("vector", lambda e: e.tensor_mul(A["qraw"][:], A["qraw"][:], A["cosq"][:]))
            P.op("vector", lambda e: e.tensor_mul(A["qshift"][:], A["qshift"][:], A["sinq"][:]))
            qrope_done[h] = P.op("vector", lambda e, qb=qb: e.tensor_add(
                qb[:], A["qraw"][:], A["qshift"][:]), inc="dv")

        kv_pe_done = P.counts["pe"]
        P.wait("gpsimd", "pe", kv_pe_done)
        P.wait("gpsimd", "dv", kv_done_dv)
        P.wait("gpsimd", "sy", P.counts["sy"])
        waq_ld[0] = P.dma("gpsimd", A["waq0"][:], waq[0], "q0")
        P.wait("tensor", "dv", v_done_dv)
        P.wait("tensor", "dv", rows_done[NSLAB - 1])
        emit_qproj_rope(0)
        waq_ld[1] = P.dma("gpsimd", A["waq1"][:], waq[1], "q1")

        for h in range(NH):
            g = h // 4
            qb = A["qt%d" % (h % 2)]
            for jg in range(2):
                nk = 8 * (jg + 1)
                slots = [(P01, "P01", 0), (P01, "P01", 1),
                         (P45, "P45", 0), (P45, "P45", 1)] if jg == 0 else \
                        [(P01, "P01", 0), (P01, "P01", 1),
                         (P23, "P23", 0), (P23, "P23", 1)]
                P.wait("tensor", "dv", qrope_done[h])
                P.wait("tensor", "dv", kv_done_dv)
                mask_dv = 0
                smm_last = 0
                for kap in range(0, nk, 2):
                    Pp, _, rp, moff, _ = slot_geom(jg, kap)
                    base = 2 * (Pp & 1)
                    ps, psname, _ = slots[base]
                    smm = 0
                    for sg in range(2):
                        _, _, _, _, lch = slot_geom(jg, kap + sg)
                        hf = slots[base + sg][2]
                        P.wait("tensor", "ac", slot_ac.get((psname, hf), 0))
                        if psname == "P23":
                            P.wait("tensor", "dv", ot_war[hf])
                        smm = P.op("tensor", lambda e, g=g, lch=lch, ps=ps, hf=hf, moff=moff, jg=jg, qb=qb: e.matmul(
                            ps[:, hf * 512 + moff: (hf + 1) * 512],
                            A["kt"][:, g * 2048 + lch * 128: g * 2048 + (lch + 1) * 128],
                            qb[:, jg * 512 + moff: (jg + 1) * 512],
                            start=True, stop=True), inc="pe")
                    smm_last = smm
                    P.wait("scalar", "pe", smm)
                    P.wait("scalar", "pe", pt_war)
                    src = ps[:].rearrange("p (two w) -> p two w", two=2)[:, :, moff:512]
                    dst = A["pt"][:, kap * 512:(kap + 2) * 512].rearrange(
                        "p (two w) -> p two w", two=2)[:, :, moff:512]
                    eac = P.op("scalar", lambda e, src=src, dst=dst: e.activation(
                        dst, src, AF.Exp, bias=A["bias20"][:], scale=QKSCALE), inc="ac")
                    slot_ac[(psname, 0)] = slot_ac[(psname, 1)] = eac
                    exp_all = eac
                    if 0 <= rp <= 3:
                        P.wait("vector", "ac", eac)
                        P.wait("vector", "sy", consts_sy)
                        for sg in range(2):
                            kp = kap + sg
                            if sg == 0:
                                mask_dv = P.op("vector", lambda e, kp=kp, rp=rp: e.tensor_mul(
                                    A["pt"][:, kp * 512 + rp * 128: kp * 512 + (rp + 1) * 128],
                                    A["pt"][:, kp * 512 + rp * 128: kp * 512 + (rp + 1) * 128],
                                    A["tri"][:]), inc="dv")
                            else:
                                mask_dv = P.op("vector", lambda e, kp=kp, rp=rp: e.tensor_scalar_mul(
                                    A["pt"][:, kp * 512 + rp * 128: kp * 512 + (rp + 1) * 128],
                                    A["pt"][:, kp * 512 + rp * 128: kp * 512 + (rp + 1) * 128],
                                    A["sig1"][:]), inc="dv")

                scores_prev[h % 2] = smm_last
                P.wait("tensor", "ac", exp_all)
                if mask_dv:
                    P.wait("tensor", "dv", mask_dv)
                lh = jg
                P.wait("tensor", "ac", slot_ac.get(("P01", 0), 0))
                P.wait("tensor", "ac", slot_ac.get(("P01", 1), 0))
                lmm = 0
                for kap in range(nk):
                    _, _, _, moff, _ = slot_geom(jg, kap)
                    lmm = P.op("tensor", lambda e, kap=kap, moff=moff, lh=lh, nk=nk: e.matmul(
                        P01[:, lh * 512 + moff: (lh + 1) * 512], A["ones_sq"][:],
                        A["pt"][:, kap * 512 + moff: (kap + 1) * 512],
                        start=(kap == 0), stop=(kap == nk - 1)), inc="pe")
                oh = jg
                P.wait("tensor", "dv", ot_war[oh])
                pvm = 0
                for kap in range(nk):
                    _, _, _, moff, lch = slot_geom(jg, kap)
                    pvm = P.op("tensor", lambda e, kap=kap, moff=moff, lch=lch, g=g, oh=oh, nk=nk: e.matmul(
                        P23[:, oh * 512 + moff: (oh + 1) * 512],
                        A["vsb"][:, lch * 512 + g * 128: lch * 512 + (g + 1) * 128],
                        A["pt"][:, kap * 512 + moff: (kap + 1) * 512],
                        start=(kap == 0), stop=(kap == nk - 1)), inc="pe")
                pt_war = pvm
                pv_last[jg] = pvm
                P.wait("scalar", "pe", lmm)
                P.wait("scalar", "dv", ot_dv)
                P.op("scalar", lambda e, lh=lh: e.activation(
                    A["reca"][:], P01[:, lh * 512:(lh + 1) * 512], AF.Ln, bias=0.0, scale=1.0))
                rc = P.op("scalar", lambda e: e.activation(
                    A["recb"][:], A["reca"][:], AF.Exp, scale=-1.0), inc="ac")
                slot_ac[("P01", lh)] = rc
                P.wait("vector", "pe", pvm)
                P.wait("vector", "ac", rc)
                ot_dv = P.op("vector", lambda e, h=h, jg=jg, oh=oh: e.tensor_tensor(
                    A["ot"][:, h * 1024 + jg * 512: h * 1024 + (jg + 1) * 512],
                    P23[:, oh * 512:(oh + 1) * 512], A["recb"][:], ALU.mult), inc="dv")
                ot_war[oh] = ot_dv
                for _ in range(3):
                    b_half()
                if jg == 0 and h + 1 < NH:
                    emit_qproj_rope(h + 1)
                    if h + 2 < NH:
                        P.wait("gpsimd", "pe", qproj_done[h])
                        waq_ld[h + 2] = P.dma("gpsimd", A["waq%d" % ((h + 2) % 2)][:],
                                              waq[h + 2], "q%d" % (h % 2))

        attn_ot_dv = ot_dv
        attn_pe_done = pv_last[1]
        attn_ac_done = P.counts["ac"]
        if "qt0" in dump_aps:
            P.wait("sync", "dv", qrope_done[0])
            P.dma("sync", dump_aps["qt0"], A["qt0"][:], "sy")
        if "pt" in dump_aps:
            P.wait("sync", "pe", pt_war)
            P.dma("sync", dump_aps["pt"], A["pt"][:], "sy")
        if "ot" in dump_aps:
            P.wait("sync", "dv", attn_ot_dv)
            P.dma("sync", dump_aps["ot"], A["ot"][:], "sy")

        while b_st["k"] < 128:
            b_half()

        P.wait("gpsimd", "dv", qrope_done[NH - 1])
        P.wait("gpsimd", "dv", attn_ot_dv)
        P.wait("gpsimd", "pe", attn_pe_done)
        wao_mm = {}
        wao_ld = [0] * 4
        rt_ev = {}
        rt_store = [0] * NMC
        for mg in range(4):
            buf = A["wao%d" % (mg % 2)]
            if mg >= 2:
                P.wait("gpsimd", "pe", wao_mm[(mg - 2, NH - 1)])
            wao_ld[mg] = P.dma("gpsimd", buf[:], wao[mg], "wo%d" % (mg % 2))
            for h in range(NH):
                P.wait("tensor", "wo%d" % (mg % 2), wao_ld[mg])
                if h == 0:
                    P.wait("tensor", "ac", attn_ac_done)
                    P.wait("tensor", "dv", attn_ot_dv)
                    P.wait("tensor", "dv", qe_done[NH - 1])
                    if p67_war["eng"] is not None:
                        P.wait("tensor", p67_war["eng"], p67_war["val"])
                    if mg > 0:
                        for mc4 in range(4):
                            P.wait("tensor", "dv", rt_ev[(mg - 1, mc4)])
                    P.wait("tensor", "dv", attn_ot_dv)
                for mc4 in range(4):
                    for th in range(2):
                        wao_mm[(mg, h)] = P.op("tensor", lambda e, mc4=mc4, th=th, h=h, buf=buf: e.matmul(
                            PS[mc4][:, th * 512:(th + 1) * 512],
                            buf[:, h * 512 + mc4 * 128: h * 512 + (mc4 + 1) * 128],
                            A["ot"][:, h * 1024 + th * 512: h * 1024 + (th + 1) * 512],
                            start=(h == 0), stop=(h == NH - 1)), inc="pe")
            for mc4 in range(4):
                mc = mg * 4 + mc4
                rb = A["rts%d" % (mc4 % 2)]
                P.wait("vector", "pe", wao_mm[(mg, NH - 1)])
                if mc4 >= 2:
                    P.wait("vector", "rs%d" % (mc4 % 2), rt_store[mc - 2])
                elif mg > 0:
                    P.wait("vector", "rs%d" % (mc4 % 2), rt_store[4 * (mg - 1) + mc4 + 2])
                rt_ev[(mg, mc4)] = P.op("vector", lambda e, mc4=mc4, rb=rb: e.tensor_copy(
                    rb[:], PS[mc4][:]), inc="dv")
                P.wait("sync", "dv", rt_ev[(mg, mc4)])
                rt_store[mc] = P.dma("sync", scrR[mc], rb[:], "rs%d" % (mc4 % 2))
            p67_war.update(eng="dv", val=rt_ev[(mg, 3)])
        rt_stores_done = (P.counts["rs0"], P.counts["rs1"])
        p67_war.update(eng="dv", val=rt_ev[(3, 3)])

        load_act_table(tids["silu_and_others"])
        P.wait("gpsimd", "pe", attn_pe_done)
        P.wait("gpsimd", "pe", max(b_st["mm"].values()))
        P.wait("vector", "rs0", rt_stores_done[0])
        P.wait("vector", "rs1", rt_stores_done[1])
        f1_ld = [0] * NC_H
        f1_mm_g = {}
        f1_mm_l = {}
        f1_sil = {}
        f1_hid = {}
        f1_h2 = {}
        for c in range(NC_H):
            w13b = A["w13b%d" % (c % 3)]
            w1b = w13b[:, 0:2048]
            w3b = w13b[:, 2048:4096]
            if c >= 3:
                P.wait("gpsimd", "pe", f1_mm_l[c - 3])
            f1_ld[c] = P.dma("gpsimd", w13b[:], w13s[c], "fw%d" % (c % 3))
            P.wait("tensor", "fw%d" % (c % 3), f1_ld[c])
            if c == 0:
                P.wait("tensor", "ac", attn_ac_done)
                P.wait("tensor", "dv", rt_ev[(3, 0)])
                P.wait("tensor", "dv", rt_ev[(3, 1)])
            for th in range(2):
                if (c, th) != (0, 0):
                    pass
                prev_sil = f1_sil.get((c - 1, th), 0)
                P.wait("tensor", "ac", prev_sil)
                for mc in range(NMC):
                    f1_mm_g[(c, th)] = P.op("tensor", lambda e, th=th, mc=mc, w1b=w1b: e.matmul(
                        P01[:, th * 512:(th + 1) * 512],
                        w1b[:, mc * 128:(mc + 1) * 128],
                        A["xnq"][:, mc * 1024 + th * 512: mc * 1024 + (th + 1) * 512],
                        start=(mc == 0), stop=(mc == NMC - 1)), inc="pe")
            for th in range(2):
                prev_hid = f1_hid.get((c - 1, th), 0)
                P.wait("tensor", "dv", prev_hid)
                for mc in range(NMC):
                    f1_mm_l[c] = P.op("tensor", lambda e, th=th, mc=mc, w3b=w3b: e.matmul(
                        P23[:, th * 512:(th + 1) * 512],
                        w3b[:, mc * 128:(mc + 1) * 128],
                        A["xnq"][:, mc * 1024 + th * 512: mc * 1024 + (th + 1) * 512],
                        start=(mc == 0), stop=(mc == NMC - 1)), inc="pe")
                f1_mm_l.setdefault((c, th), f1_mm_l[c])
                f1_mm_l[(c, th)] = f1_mm_l[c]
            for th in range(2):
                P.wait("scalar", "pe", f1_mm_g[(c, th)])
                f1_sil[(c, th)] = P.op("scalar", lambda e, th=th: e.activation(
                    A["gsil"][:, th * 512:(th + 1) * 512], P01[:, th * 512:(th + 1) * 512],
                    AF.Silu, bias=0.0, scale=1.0), inc="ac")
                P.wait("vector", "ac", f1_sil[(c, th)])
                P.wait("vector", "pe", f1_mm_l[(c, th)])
                f1_hid[(c, th)] = P.op("vector", lambda e, c=c, th=th: e.tensor_tensor(
                    A["hidden"][:, c * 1024 + th * 512: c * 1024 + (th + 1) * 512],
                    A["gsil"][:, th * 512:(th + 1) * 512],
                    P23[:, th * 512:(th + 1) * 512], ALU.mult), inc="dv")
            P.wait("tensor", "dv", f1_hid[(c, 1)])
            if c >= 1:
                eng, ev = f1_h2[c - 1]
                P.wait("tensor", eng, ev)
            h2mm = 0
            for th in range(2):
                h2mm = P.op("tensor", lambda e, c=c, th=th: e.matmul(
                    P45[:, th * 512:(th + 1) * 512], A["h128"][:],
                    A["hidden"][:, c * 1024 + th * 512: c * 1024 + (th + 1) * 512],
                    start=True, stop=True), inc="pe")
            if c % 2 == 0:
                P.wait("scalar", "pe", h2mm)
                ev = P.op("scalar", lambda e, c=c: e.activation(
                    A["hidden"][:, c * 1024:(c + 1) * 1024], P45[:], AF.Copy,
                    bias=0.0, scale=1.0), inc="ac")
                f1_h2[c] = ("ac", ev)
            else:
                P.wait("vector", "pe", h2mm)
                ev = P.op("vector", lambda e, c=c: e.tensor_copy(
                    A["hidden"][:, c * 1024:(c + 1) * 1024], P45[:]), inc="dv")
                f1_h2[c] = ("dv", ev)
        hid_done = f1_hid[(NC_H - 1, 1)]
        f1_ac_done = P.counts["ac"]
        f1_dv_done = P.counts["dv"]
        f1_pe_done = P.counts["pe"]
        if "hid" in dump_aps:
            P.wait("sync", "dv", hid_done)
            P.dma("sync", dump_aps["hid"], A["hidden"][:], "sy")

        assert b_st["k"] >= 128, "B-step incomplete: %d" % b_st["k"]
        P.wait("gpsimd", "pe", f1_pe_done)
        P.wait("gpsimd", "dv", hid_done)
        P.wait("sync", "pe", f1_pe_done)
        for _bs in ("bs00", "bs01", "bs10", "bs11"):
            P.wait("gpsimd", _bs, P.counts[_bs])
            P.wait("sync", _bs, P.counts[_bs])
        f2_ld = {}
        f2_mm = {}
        f2_sil = {}
        cmb_dv = {}
        out_st = [0] * NMC
        xc_ld = {}
        rt_ld = {}
        for mq in range(4):
            for c in range(NC_H):
                wqb = A["w2q%d" % (c % 8)]
                prev = f2_mm.get((mq, c - 8), f2_mm.get((mq - 1, NC_H - 8 + (c % 8)), 0))
                P.wait("sync", "pe", prev)
                f2_ld[(mq, c)] = P.dma("sync", wqb[:], scrB[mq, c], "fl%d" % (c % 8))
                P.wait("tensor", "fl%d" % (c % 8), f2_ld[(mq, c)])
                if c == 0:
                    P.wait("tensor", "dv", f1_dv_done)
                    P.wait("tensor", "ac", f1_ac_done)
                    if mq == 0:
                        if p67_war["eng"] is not None:
                            P.wait("tensor", p67_war["eng"], p67_war["val"])
                for mc4 in range(4):
                    if c == 0 and mq > 0:
                        P.wait("tensor", "ac", f2_sil[(mq - 1, mc4)])
                    for th in range(2):
                        f2_mm[(mq, c)] = P.op("tensor", lambda e, mc4=mc4, th=th, c=c, wqb=wqb: e.matmul(
                            PS[mc4][:, th * 512:(th + 1) * 512],
                            wqb[:, mc4 * 128:(mc4 + 1) * 128],
                            A["hidden"][:, c * 1024 + th * 512: c * 1024 + (th + 1) * 512],
                            start=(c == 0), stop=(c == NC_H - 1)), inc="pe")
            for mc4 in range(4):
                mc = mq * 4 + mc4
                P.wait("sync", "dv", cmb_dv.get(mc - 2, 0))
                xc_ld[mc] = P.dma("sync", A["xc%d" % (mc % 2)][:], xq[mc], "xq%d" % (mc % 2))
                P.wait("gpsimd", "rs0", rt_stores_done[0])
                P.wait("gpsimd", "rs1", rt_stores_done[1])
                P.wait("gpsimd", "dv", cmb_dv.get(mc - 2, 0))
                rt_ld[mc] = P.dma("gpsimd", A["rt%d" % (mc % 2)][:], scrR[mc], "rl%d" % (mc % 2))
                P.wait("scalar", "pe", f2_mm[(mq, NC_H - 1)])
                P.wait("scalar", "dv", cmb_dv.get(mc - 1, 0))
                f2_sil[(mq, mc4)] = P.op("scalar", lambda e, mc4=mc4: e.activation(
                    A["gf"][:], PS[mc4][:], AF.Silu, bias=0.0, scale=1.0), inc="ac")
                P.wait("vector", "ac", f2_sil[(mq, mc4)])
                P.wait("vector", "rl%d" % (mc % 2), rt_ld[mc])
                P.wait("vector", "xq%d" % (mc % 2), xc_ld[mc])
                for _bs in ("bs00", "bs01", "bs10", "bs11"):
                    P.wait("vector", _bs, P.counts[_bs])
                if mc >= 2:
                    P.wait("vector", "os%d" % (mc % 2), out_st[mc - 2])
                P.op("vector", lambda e, mc=mc: e.tensor_mul(
                    A["gf"][:], A["gf"][:], A["rt%d" % (mc % 2)][:]))
                cmb_dv[mc] = P.op("vector", lambda e, mc=mc: e.tensor_add(
                    A["ou%d" % (mc % 2)][:], A["gf"][:], A["xc%d" % (mc % 2)][:]), inc="dv")
                P.wait("sync", "dv", cmb_dv[mc])
                out_st[mc] = P.dma("sync", out_t[mc], A["ou%d" % (mc % 2)][:], "os%d" % (mc % 2))
        if "rt" in dump_aps:
            P.wait("sync", "rs0", rt_stores_done[0])
            P.wait("sync", "rs1", rt_stores_done[1])
            for mc in range(NMC):
                P.dma("sync", dump_aps["rt"][mc], scrR[mc], "sy")

        P.wait("sync", "sy", P.counts["sy"])
        P.wait("sync", "os0", P.counts["os0"])
        P.wait("sync", "os1", P.counts["os1"])
        P.run(block)
    return nc



def _rope_tables():
    pos = np.arange(T, dtype=np.float32)[:, None]
    freqs = np.power(np.float32(10000.0),
                     -(np.arange(64, dtype=np.float32) / np.float32(64.0)))[None, :].astype(np.float32)
    ang = (pos * freqs).astype(np.float32)
    cos = np.cos(ang.astype(np.float64)).astype(np.float32)
    sin = np.sin(ang.astype(np.float64)).astype(np.float32)
    ck = np.empty((128, T), np.float32)
    sk = np.empty((128, T), np.float32)
    ck[0:64] = cos.T
    ck[64:128] = cos.T
    sk[0:64] = -sin.T
    sk[64:128] = sin.T
    return ck, sk


def _local_perm(par):
    idx = []
    for s in range(NSLAB):
        us = [par, 2 + par, 1 - par, 3 - par]
        for u in us:
            c0 = (s * 4 + u) * 128
            idx.extend(range(c0, c0 + 128))
    return np.array(idx, np.int64)


def prepare_in_maps(L0_attn_x, L0_attn_w_aq, L0_attn_w_ak, L0_attn_w_av, L0_attn_w_ao,
                    L0_ffn_W1, L0_ffn_W2, L0_ffn_W3):
    x = np.ascontiguousarray(np.asarray(L0_attn_x, dtype=np.float32))
    w_aq = np.asarray(L0_attn_w_aq, np.float32)
    w_ak = np.asarray(L0_attn_w_ak, np.float32)
    w_av = np.asarray(L0_attn_w_av, np.float32)
    w_ao = np.asarray(L0_attn_w_ao, np.float32)
    W1 = np.asarray(L0_ffn_W1, np.float32)
    W2 = np.asarray(L0_ffn_W2, np.float32)
    W3 = np.asarray(L0_ffn_W3, np.float32)

    waq_h = np.ascontiguousarray(
        w_aq.reshape(NH, 16, 128, 128).transpose(0, 2, 1, 3).reshape(NH, 128, 2048))
    wak_h = np.ascontiguousarray(
        w_ak.reshape(G, 16, 128, 128).transpose(2, 0, 1, 3).reshape(128, 8192))
    wav_h = np.ascontiguousarray(
        w_av.reshape(G, 16, 128, 128).transpose(2, 1, 0, 3).reshape(128, 8192))
    wao_h = np.ascontiguousarray(
        w_ao.reshape(NH, 128, 4, 512).transpose(2, 1, 0, 3).reshape(4, 128, 8192))
    w1_h = W1.reshape(NC_H, 128, NMC, 128).transpose(0, 3, 2, 1).reshape(NC_H, 128, 2048)
    w3_h = W3.reshape(NC_H, 128, NMC, 128).transpose(0, 3, 2, 1).reshape(NC_H, 128, 2048)
    w13_h = np.ascontiguousarray(
        np.stack([w1_h, w3_h], axis=2).reshape(NC_H, 128, 4096))
    w2t_h = np.ascontiguousarray(W2.T)

    ck, sk = _rope_tables()

    def sylvester(n):
        m = np.array([[1]], np.int64)
        while m.shape[0] < n:
            m = np.block([[m, m], [m, -m]])
        return m

    h128_h = sylvester(128).astype(ml_dtypes.bfloat16)
    h64 = sylvester(64)
    h64d_h = np.zeros((128, 128), np.int64)
    h64d_h[0:64, 0:64] = h64
    h64d_h[64:128, 64:128] = h64
    h64d_h = h64d_h.astype(ml_dtypes.bfloat16)
    tri_h = np.triu(np.ones((128, 128), np.float32)).astype(ml_dtypes.bfloat16)

    in_maps = []
    perms = {}
    for core in range(8):
        b, par = core // 2, core % 2
        perm = _local_perm(par)
        perms[core] = perm
        xl = x[b][perm]
        xs_h = np.ascontiguousarray(
            xl.reshape(NSLAB, 512, NMC, 128).transpose(0, 3, 2, 1))
        qsel_l = perm.reshape(NSLAB, 4, 128)[:, 0:2].reshape(-1)
        xq_h = np.ascontiguousarray(
            x[b][qsel_l].reshape(1024, NMC, 128).transpose(1, 2, 0))
        ckl = np.ascontiguousarray(ck[:, perm])
        skl = np.ascontiguousarray(sk[:, perm])
        qsel = perm.reshape(NSLAB, 4, 128)[:, 0:2].reshape(-1)
        cql = np.ascontiguousarray(ck[:, qsel])
        sql = np.ascontiguousarray(sk[:, qsel])
        sig1_h = np.full((128, 1), 1.0 if par == 1 else 0.0, np.float32)
        in_maps.append({
            "xs": xs_h, "xq": xq_h, "waq": waq_h, "wak": wak_h, "wav": wav_h, "wao": wao_h,
            "w13s": w13_h, "w2t": w2t_h,
            "cosk": ckl, "sink": skl, "cosq": cql, "sinq": sql,
            "tri": tri_h, "sig1": sig1_h, "h128": h128_h, "h64d": h64d_h,
        })
    return in_maps, perms


def kernel(L0_attn_x, L0_attn_w_aq, L0_attn_w_ak, L0_attn_w_av, L0_attn_w_ao,
           L0_ffn_W1, L0_ffn_W2, L0_ffn_W3):
    in_maps, perms = prepare_in_maps(
        L0_attn_x, L0_attn_w_aq, L0_attn_w_ak, L0_attn_w_av, L0_attn_w_ao,
        L0_ffn_W1, L0_ffn_W2, L0_ffn_W3)
    nc = build_nc()
    res = run_bass_kernel_spmd(nc, in_maps, core_ids=list(range(8)))

    out = np.empty((B, T, M), np.float32)
    for core in range(8):
        b = core // 2
        oc = res.results[core]["out"]
        ov = oc.reshape(NMC, 128, 8, 128).transpose(2, 3, 0, 1).reshape(1024, M)
        qsel = perms[core].reshape(NSLAB, 4, 128)[:, 0:2].reshape(-1)
        out[b][qsel] = ov
    return out


if __name__ == "__main__":
    pass
